# version 10
# baseline (speedup 1.0000x reference)
import numpy as np
import ml_dtypes
import concourse.bass as bass
import concourse.mybir as mybir
from concourse.bass_utils import run_bass_kernel_spmd
from contextlib import ExitStack

F32 = mybir.dt.float32
BF16 = mybir.dt.bfloat16
AF = mybir.ActivationFunctionType
ALU = mybir.AluOpType
AX = mybir.AxisListType.X

NCORES = 8
BLOC = 2
L = 2048
T = BLOC * L
D = 1024
NB = T // 128
BF = ml_dtypes.bfloat16

ENGS = ("t", "v", "s", "g", "y")


class Op:
    __slots__ = ("eng", "fn", "reads", "writes", "dma", "idx", "sem_val",
                 "milestone", "deps", "tag", "sem_id", "cc")

    def __init__(self, eng, fn, reads, writes, dma, idx, tag, cc=False):
        self.eng = eng
        self.fn = fn
        self.reads = tuple(reads)
        self.writes = tuple(writes)
        self.dma = dma
        self.idx = idx
        self.milestone = False
        self.deps = []
        self.sem_val = 0
        self.tag = tag
        self.cc = cc


class Builder:

    def __init__(self):
        self.ops = []
        self.last_writer = {}
        self.readers = {}
        self.stage_keys = ()

    def barrier(self, tag):
        lastc = {}
        dmas = {}
        for o in self.ops:
            if o.dma:
                dmas.setdefault(o.eng, []).append(o.idx)
            else:
                lastc[o.eng] = o.idx
        deps = set()
        for e2, idx in lastc.items():
            if e2 != "t":
                deps.add(idx)
        for q, lst in dmas.items():
            deps.update(lst[-self.NDMACLS:])
        bar = self.op("t", lambda e: e.nop(), reads=(), writes=(f"__bar_{tag}",),
                      tag=f"bar_{tag}")
        for d in sorted(deps):
            if d not in bar.deps:
                bar.deps.append(d)
        self.stage_keys = (f"__bar_{tag}",)
        return bar

    def op(self, eng, fn, reads=(), writes=(), dma=False, tag="", cc=False):
        reads = tuple(reads) + self.stage_keys
        o = Op(eng, fn, reads, writes, dma, len(self.ops), tag, cc)
        deps = set()
        for bb in o.reads:
            w = self.last_writer.get(bb)
            if w is not None:
                deps.add(w)
        for bb in o.writes:
            w = self.last_writer.get(bb)
            if w is not None:
                deps.add(w)
            for r in self.readers.get(bb, ()):
                deps.add(r)
        for d in sorted(deps):
            dop = self.ops[d]
            if dop.eng == o.eng and not dop.dma:
                continue
            o.deps.append(d)
        for bb in o.writes:
            self.last_writer[bb] = o.idx
            self.readers[bb] = []
        for bb in o.reads:
            self.readers.setdefault(bb, []).append(o.idx)
        self.ops.append(o)
        return o

    def t(self, fn, reads=(), writes=(), tag=""):
        return self.op("t", fn, reads, writes, False, tag)

    def v(self, fn, reads=(), writes=(), tag=""):
        return self.op("v", fn, reads, writes, False, tag)

    def s(self, fn, reads=(), writes=(), tag=""):
        return self.op("s", fn, reads, writes, False, tag)

    def g(self, fn, reads=(), writes=(), tag=""):
        return self.op("g", fn, reads, writes, False, tag)

    def gdma(self, fn, reads=(), writes=(), tag=""):
        return self.op("g", fn, reads, writes, True, tag)

    def ccg(self, fn, reads=(), writes=(), tag=""):
        return self.op("g", fn, reads, writes, False, tag, cc=True)

    def ydma(self, fn, reads=(), writes=(), tag=""):
        return self.op("y", fn, reads, writes, True, tag)

    NDMACLS = 8

    def build(self, nc, final_eng="g"):
        for o in self.ops:
            if o.dma:
                o.milestone = True
            for d in o.deps:
                self.ops[d].milestone = True
        last_by_sem = {}
        for o in self.ops:
            last_by_sem[(o.eng, o.dma)] = o.idx
        for idx in last_by_sem.values():
            self.ops[idx].milestone = True
        cum = {}
        dma_ord = {}
        for o in self.ops:
            if o.dma:
                k = dma_ord.get(o.eng, 0)
                dma_ord[o.eng] = k + 1
                sid = (o.eng, "d", k % self.NDMACLS)
                o.sem_id = sid
                cum[sid] = cum.get(sid, 0) + 16
                o.sem_val = cum[sid]
            elif o.cc:
                sid = (o.eng, "cc")
                o.sem_id = sid
                o.milestone = True
                cum[sid] = cum.get(sid, 0) + 1
                o.sem_val = cum[sid]
            else:
                sid = (o.eng, "c")
                o.sem_id = sid
                if o.milestone:
                    cum[sid] = cum.get(sid, 0) + 1
                o.sem_val = cum.get(sid, 0)

        es = ExitStack()
        sems = {sid: es.enter_context(
            nc.semaphore("sem_" + "_".join(map(str, sid)))) for sid in cum}
        eng_ops = {e: [o for o in self.ops if o.eng == e] for e in ENGS}

        class _AutoDrain:
            AUTODRAIN = True
            _PASS = ("wait_ge", "drain", "dma_start", "sem_inc")

            def __init__(self, eng):
                object.__setattr__(self, "_e", eng)
                object.__setattr__(self, "_n", 0)

            def __getattr__(self, name):
                fn = getattr(self._e, name)
                if name.startswith("_") or name in self._PASS or not callable(fn):
                    return fn

                def wrapped(*a, **k):
                    if self.AUTODRAIN and self._n > 0:
                        self._e.drain()
                    object.__setattr__(self, "_n", self._n + 1)
                    return fn(*a, **k)
                return wrapped

        with nc.Block() as blk:
            def emitter(eng_name):
                def inner(eng):
                    if eng_name in ("v", "s", "g"):
                        eng = _AutoDrain(eng)
                    hw = {}

                    def wait(sid, val):
                        if hw.get(sid, -1) < val:
                            eng.wait_ge(sems[sid], val)
                            hw[sid] = val

                    for o in eng_ops[eng_name]:
                        for d in o.deps:
                            dop = self.ops[d]
                            wait(dop.sem_id, dop.sem_val)
                        if o.dma and o.sem_val > 16:
                            wait(o.sem_id, o.sem_val - 16)
                        ins = o.fn(eng)
                        if o.milestone:
                            ins.then_inc(sems[o.sem_id], 16 if o.dma else 1)
                    if eng_name == final_eng:
                        for sid, v in cum.items():
                            if v > 0 and sid != (final_eng, "c"):
                                wait(sid, v)
                return inner

            if eng_ops["t"]:
                blk.tensor(emitter("t"))
            if eng_ops["v"]:
                blk.vector(emitter("v"))
            if eng_ops["s"]:
                blk.scalar(emitter("s"))
            if eng_ops["g"] or final_eng == "g":
                blk.gpsimd(emitter("g"))
            if eng_ops["y"]:
                blk.sync(emitter("y"))
        es.close()
        return nc


_PFX = [""]


def sbuf(es, nc, name, shape, dtype):
    return es.enter_context(nc.sbuf_tensor(_PFX[0] + name, shape, dtype))


def psumt(es, nc, name, shape, dtype=F32):
    return es.enter_context(nc.psum_tensor(_PFX[0] + name, shape, dtype))


def psum8(es, nc, dtype=F32, sfx=""):
    return [psumt(es, nc, f"ps{i}{sfx}", [128, 512], dtype) for i in range(8)]


PS = tuple(f"ps{i}" for i in range(8))



def emit_l1(nc, b, dr):
    _PFX[0] = "a_"
    x0 = dr("x0", [T, D], F32, "ExternalInput")
    Wqkv = {c: dr("W" + c, [D, D], F32, "ExternalInput") for c in "qkv"}
    convw = dr("convw", [128, 96], F32, "ExternalInput")
    Wbgh = dr("Wbgh", [D, 16], F32, "ExternalInput")
    bgh_bc = dr("bgh_bc", [128, 8], F32, "ExternalInput")
    xn0_tok = dr("xn0_tok", [T, D], BF16, "ExternalOutput")
    out_fm = {c: dr(c + "f", [D, T], BF16, "ExternalOutput") for c in "qkv"}
    gates = dr("gates", [T, 20], F32, "ExternalOutput")
    raw = {c: dr(c + "raw", [D, T], BF16, "Internal") for c in "qk"}

    es = ExitStack()

    xt = [sbuf(es, nc, f"xt{i}", [128, D], F32) for i in range(2)]
    stats = [sbuf(es, nc, f"stats{i}", [128, 8], F32) for i in range(2)]
    xnt = [sbuf(es, nc, f"xnt{i}", [128, D], BF16) for i in range(2)]
    sqscr = sbuf(es, nc, "sqscr", [128, D], F32)
    xn_fm = [sbuf(es, nc, f"xnfm{c}", [128, T], BF16) for c in range(8)]
    wsb = [sbuf(es, nc, f"wsb{c}", [128, D], BF16) for c in range(8)]
    cw = sbuf(es, nc, "cw", [128, 96], F32)
    stage = sbuf(es, nc, "stage", [128, 2 * (L + 3)], F32)
    conv_o = sbuf(es, nc, "convo", [128, T], F32)
    silu_o = sbuf(es, nc, "siluo", [128, T], BF16)
    wbgh_sb = sbuf(es, nc, "wbghsb", [128, 128], BF16)
    bgh_sb = sbuf(es, nc, "bghsb", [128, 8], F32)
    glog = [sbuf(es, nc, f"glog{i}", [128, 16], F32) for i in range(2)]
    gtmp = [sbuf(es, nc, f"gtmp{i}", [128, 28], F32) for i in range(2)]
    gout = [sbuf(es, nc, f"gout{i}", [128, 20], F32) for i in range(2)]
    qa = sbuf(es, nc, "qa", [128, T], BF16)
    qb = sbuf(es, nc, "qb", [128, T], BF16)
    sqa = sbuf(es, nc, "sqa", [128, T], BF16)
    sqb = sbuf(es, nc, "sqb", [128, T], BF16)
    onesc = sbuf(es, nc, "onesc", [128, 1], BF16)
    ones_row = sbuf(es, nc, "onesrow", [1, 128], BF16)
    rr = conv_o[0:1, :]
    rstd_bf = silu_o[0:1, :]
    outn = sbuf(es, nc, "outn", [128, T], BF16)
    outnB = sbuf(es, nc, "outnB", [128, T], BF16)

    pj = psum8(es, nc)

    for i in range(NB):
        u = i % 2
        XT, ST, XN = f"xt{u}", f"st{u}", f"xn{u}"
        b.ydma(lambda e, i=i, u=u: e.dma_start(out=xt[u][:], in_=x0[i * 128:(i + 1) * 128, :]),
               reads=(XT + "w",), writes=(XT,), tag=f"ln_ld{i}")
        b.s(lambda e, u=u: e.activation(out=sqscr[:], in_=xt[u][:], func=AF.Square,
                                        accum_out=stats[u][:, 0:1]),
            reads=(XT,), writes=(ST, "sqscr"), tag=f"ln_sq{i}")

        def lnv(e, u=u):
            e.reduce_sum(stats[u][:, 1:2], xt[u][:], axis=AX)
            e.tensor_scalar(stats[u][:, 2:3], stats[u][:, 1:2], -1.0 / D, None, ALU.mult)
            e.tensor_tensor(stats[u][:, 3:4], stats[u][:, 2:3], stats[u][:, 2:3], ALU.mult)
            e.tensor_scalar(stats[u][:, 4:5], stats[u][:, 0:1], 1.0 / D, None, ALU.mult)
            e.tensor_tensor(stats[u][:, 5:6], stats[u][:, 4:5], stats[u][:, 3:4], ALU.subtract)
            return e.tensor_scalar(stats[u][:, 6:7], stats[u][:, 5:6], 1.0, 1e-5, ALU.mult, ALU.add)
        b.v(lnv, reads=(XT, ST), writes=(ST,), tag=f"ln_v{i}")

        def lns(e, u=u):
            e.activation(out=stats[u][:, 7:8], in_=stats[u][:, 6:7], func=AF.Ln)
            return e.activation(out=stats[u][:, 7:8], in_=stats[u][:, 7:8], func=AF.Exp, scale=-0.5)
        b.s(lns, reads=(ST,), writes=(ST,), tag=f"ln_rs{i}")
        b.v(lambda e, u=u: e.tensor_scalar(xnt[u][:], xt[u][:], stats[u][:, 2:3],
                                           stats[u][:, 7:8], ALU.add, ALU.mult),
            reads=(XT, ST), writes=(XN, XT + "w"), tag=f"ln_ap{i}")
        b.gdma(lambda e, i=i, u=u: e.dma_start(out=xn0_tok[i * 128:(i + 1) * 128, :], in_=xnt[u][:]),
               reads=(XN,), writes=(f"xn0_tok{i}",), tag=f"ln_st{i}")

    def setup(e):
        e.memset(onesc[:], 1.0)
        e.memset(ones_row[:], 1.0)
        e.memset(stage[:, 0:3], 0.0)
        return e.memset(stage[:, L + 3:L + 6], 0.0)
    b.g(setup, reads=(), writes=("consts", "stage_pad"), tag="consts")
    b.gdma(lambda e: e.dma_start(out=cw[:], in_=convw[:, :]), writes=("cw",), tag="cwl")
    b.gdma(lambda e: e.dma_start(out=wbgh_sb.rearrange("p (c f) -> p c f", f=16)[:, :, :],
                                 in_=Wbgh.rearrange("(c p) f -> p c f", p=128)[:, :, :]),
           writes=("wbgh",), tag="wbghl")
    b.gdma(lambda e: e.dma_start(out=bgh_sb[:], in_=bgh_bc[:, :]), writes=("bgh",), tag="bghl")

    all_xn = tuple(f"xn0_tok{i}" for i in range(NB))
    for c in range(8):
        b.ydma(lambda e, c=c: e.dma_start(out=xn_fm[c][:], in_=xn0_tok[:, c * 128:(c + 1) * 128],
                                          transpose=True),
               reads=all_xn, writes=(f"xnfm{c}",), tag=f"xf{c}")
    XNFM = tuple(f"xnfm{c}" for c in range(8))

    for i in range(NB):
        u = i % 2
        GP = PS[6 + u]

        def gmm(e, i=i, u=u):
            r = None
            for dc in range(8):
                r = e.matmul(pj[6 + u][:, 0:16], xn_fm[dc][:, i * 128:(i + 1) * 128],
                             wbgh_sb[:, dc * 16:(dc + 1) * 16], start=(dc == 0), stop=(dc == 7))
            return r
        b.t(gmm, reads=XNFM + ("wbgh",), writes=(GP,), tag=f"gmm{i}")

        def gv1(e, u=u):
            e.tensor_copy(glog[u][:], pj[6 + u][:, 0:16])
            e.tensor_tensor(gtmp[u][:, 0:8], glog[u][:, 8:16], bgh_sb[:, 0:8], ALU.add)
            e.tensor_tensor(gtmp[u][:, 8:12], gtmp[u][:, 0:4], gtmp[u][:, 4:8], ALU.max)
            e.tensor_scalar(gtmp[u][:, 8:12], gtmp[u][:, 8:12], 0.0, None, ALU.max)
            e.tensor_tensor(gtmp[u][:, 12:16], gtmp[u][:, 0:4], gtmp[u][:, 8:12], ALU.subtract)
            e.tensor_tensor(gtmp[u][:, 16:20], gtmp[u][:, 4:8], gtmp[u][:, 8:12], ALU.subtract)
            return e.tensor_scalar(gtmp[u][:, 20:24], gtmp[u][:, 8:12], -1.0, None, ALU.mult)
        b.v(gv1, reads=(GP, "bgh"), writes=(f"gt{u}", GP), tag=f"gv1_{i}")

        def gs1(e, u=u):
            e.activation(out=gout[u][:, 0:8], in_=glog[u][:, 0:8], func=AF.Sigmoid)
            return e.activation(out=gtmp[u][:, 12:24], in_=gtmp[u][:, 12:24], func=AF.Exp)
        b.s(gs1, reads=(f"gt{u}",), writes=(f"gt{u}", f"go{u}"), tag=f"gs1_{i}")

        def gv2(e, u=u):
            e.tensor_tensor(gtmp[u][:, 24:28], gtmp[u][:, 12:16], gtmp[u][:, 16:20], ALU.add)
            e.tensor_tensor(gtmp[u][:, 24:28], gtmp[u][:, 24:28], gtmp[u][:, 20:24], ALU.add)
            e.reciprocal(gtmp[u][:, 24:28], gtmp[u][:, 24:28])
            e.tensor_tensor(gout[u][:, 8:12], gtmp[u][:, 12:16], gtmp[u][:, 24:28], ALU.mult)
            e.tensor_tensor(gout[u][:, 12:16], gtmp[u][:, 16:20], gtmp[u][:, 24:28], ALU.mult)
            return e.tensor_tensor(gout[u][:, 16:20], gtmp[u][:, 20:24], gtmp[u][:, 24:28], ALU.mult)
        b.v(gv2, reads=(f"gt{u}",), writes=(f"go{u}", f"gt{u}"), tag=f"gv2_{i}")
        b.gdma(lambda e, i=i, u=u: e.dma_start(out=gates[i * 128:(i + 1) * 128, :], in_=gout[u][:]),
               reads=(f"go{u}",), writes=(), tag=f"gst{i}")

    for ci, c in enumerate("qkv"):
        W = Wqkv[c]
        dst_raw = raw[c] if c in "qk" else out_fm["v"]
        WK = tuple(f"w{dc}" for dc in range(8))
        for dc in range(8):
            b.gdma(lambda e, dc=dc, W=W: e.dma_start(out=wsb[dc][:], in_=W[dc * 128:(dc + 1) * 128, :]),
                   writes=(f"w{dc}",), tag=f"wl{c}{dc}")
        for fc in range(8):
            def pmm(e, fc=fc):
                r = None
                for dc in range(8):
                    for tt in range(8):
                        r = e.matmul(pj[tt][:], wsb[dc][:, fc * 128:(fc + 1) * 128],
                                     xn_fm[dc][:, tt * 512:(tt + 1) * 512],
                                     start=(dc == 0), stop=(dc == 7))
                return r
            b.t(pmm, reads=WK + XNFM, writes=PS, tag=f"pmm{c}{fc}")

            def pev(e):
                r = None
                for tt in range(8):
                    off = (3 + tt * 512) if tt < 4 else (L + 6 + (tt - 4) * 512)
                    r = e.tensor_copy(stage[:, off:off + 512], pj[tt][:])
                return r
            b.v(pev, reads=PS, writes=("stage",) + PS, tag=f"pev{c}{fc}")

            def mkconv(sq):
                def conv(e, ci=ci, fc=fc, sq=sq):
                    base = ci * 32 + fc * 4
                    so = sq * (L + 3)
                    oo = sq * L
                    e.tensor_scalar(conv_o[:, oo:oo + L], stage[:, so:so + L],
                                    cw[:, base:base + 1], None, ALU.mult)
                    r = None
                    for j in (1, 2, 3):
                        r = e.scalar_tensor_tensor(conv_o[:, oo:oo + L], stage[:, so + j:so + j + L],
                                                   cw[:, base + j:base + j + 1], conv_o[:, oo:oo + L],
                                                   ALU.mult, ALU.add)
                    return r
                return conv
            b.v(mkconv(0), reads=("stage", "cw", "stage_pad"), writes=("convo",), tag=f"conv{c}{fc}a")
            b.v(mkconv(1), reads=("stage", "cw", "stage_pad"), writes=("convo2",), tag=f"conv{c}{fc}b")
            b.s(lambda e: e.activation(out=silu_o[:], in_=conv_o[:], func=AF.Silu),
                reads=("convo", "convo2"), writes=("siluo",), tag=f"silu{c}{fc}")
            b.gdma(lambda e, fc=fc, dst_raw=dst_raw, c=c: e.dma_start(
                out=dst_raw[fc * 128:(fc + 1) * 128, :], in_=silu_o[:]),
                reads=("siluo",), writes=(f"raw{c}{fc}",), tag=f"rst{c}{fc}")

    for c in "qk":
        for h in range(4):
            b.ydma(lambda e, c=c, h=h: e.dma_start(out=qa[:], in_=raw[c][h * 256:h * 256 + 128, :]),
                   reads=(f"raw{c}{2 * h}", "qaw"), writes=("qa",), tag=f"l2ld{c}{h}a")
            b.ydma(lambda e, c=c, h=h: e.dma_start(out=qb[:], in_=raw[c][h * 256 + 128:h * 256 + 256, :]),
                   reads=(f"raw{c}{2 * h + 1}", "qbw"), writes=("qb",), tag=f"l2ld{c}{h}b")
            b.s(lambda e: e.activation(out=sqa[:], in_=qa[:], func=AF.Square),
                reads=("qa",), writes=("sqa",), tag=f"l2sq{c}{h}a")
            b.s(lambda e: e.activation(out=sqb[:], in_=qb[:], func=AF.Square),
                reads=("qb",), writes=("sqb",), tag=f"l2sq{c}{h}b")

            def ssmm(e):
                r = None
                for tt in range(8):
                    e.matmul(pj[tt][0:1, :], onesc[:], sqa[:, tt * 512:(tt + 1) * 512],
                             start=True, stop=False)
                    r = e.matmul(pj[tt][0:1, :], onesc[:], sqb[:, tt * 512:(tt + 1) * 512],
                                 start=False, stop=True)
                return r
            b.t(ssmm, reads=("sqa", "sqb", "consts"), writes=PS, tag=f"l2ss{c}{h}")

            def epsc(e):
                r = None
                for tt in range(8):
                    r = e.tensor_scalar(rr[0:1, tt * 512:(tt + 1) * 512], pj[tt][0:1, :],
                                        1.0, 1e-12, ALU.mult, ALU.add)
                return r
            b.v(epsc, reads=PS + ("convo",), writes=("rr", "convo") + PS, tag=f"l2eps{c}{h}")

            def rstd2(e):
                e.activation(out=rr[0:1, :], in_=rr[0:1, :], func=AF.Ln)
                e.activation(out=rr[0:1, :], in_=rr[0:1, :], func=AF.Exp, scale=-0.5)
                return e.copy(rstd_bf[0:1, :], rr[0:1, :])
            b.s(rstd2, reads=("rr", "siluo"), writes=("rstdbf", "rr", "siluo"), tag=f"l2r{c}{h}")

            def bcmm(e):
                r = None
                for tt in range(8):
                    r = e.matmul(pj[tt][:], ones_row[:], rstd_bf[0:1, tt * 512:(tt + 1) * 512],
                                 start=True, stop=True)
                return r
            b.t(bcmm, reads=("rstdbf", "consts"), writes=PS, tag=f"l2bc{c}{h}")

            def l2ap(e):
                r = None
                for tt in range(8):
                    sl = slice(tt * 512, (tt + 1) * 512)
                    e.tensor_tensor(outn[:, sl], qa[:, sl], pj[tt][:], ALU.mult)
                    r = e.tensor_tensor(outnB[:, sl], qb[:, sl], pj[tt][:], ALU.mult)
                return r
            b.v(l2ap, reads=PS + ("qa", "qb"), writes=("outn", "outnB") + PS, tag=f"l2ap{c}{h}")
            b.gdma(lambda e, c=c, h=h: e.dma_start(out=out_fm[c][h * 256:h * 256 + 128, :], in_=outn[:]),
                   reads=("outn",), writes=("qaw",), tag=f"l2st{c}{h}a")
            b.gdma(lambda e, c=c, h=h: e.dma_start(out=out_fm[c][h * 256 + 128:h * 256 + 256, :], in_=outnB[:]),
                   reads=("outnB",), writes=("qbw",), tag=f"l2st{c}{h}b")

    es.close()




def emit_l2(nc, b, dr, bhs=None):
    _PFX[0] = "b_"
    qf = dr("qf", [D, T], BF16, "ExternalInput")
    kf = dr("kf", [D, T], BF16, "ExternalInput")
    vf = dr("vf", [D, T], BF16, "ExternalInput")
    gates = dr("gates", [T, 20], F32, "ExternalInput")
    maskBD = dr("maskBD", [128, 128], F32, "ExternalInput")
    maskA = dr("maskA", [64, 256], F32, "ExternalInput")
    ident_f = dr("ident_f", [128, 128], F32, "ExternalInput")
    ident_b = dr("ident_b", [128, 128], BF16, "ExternalInput")
    nw_bc = dr("nw_bc", [128, 256], F32, "ExternalInput")
    o_n = dr("o_n", [T, D], BF16, "ExternalOutput")
    ktokd = dr("ktokd", [L, 256], BF16, "Internal")

    es = ExitStack()
    NSC = L // 128

    q_fm = [sbuf(es, nc, f"q_fm{d}", [128, L], BF16) for d in range(2)]
    k_fm = [sbuf(es, nc, f"k_fm{d}", [128, L], BF16) for d in range(2)]
    k_tok = sbuf(es, nc, "k_tok", [128, 16 * 256], BF16)
    v_tok = sbuf(es, nc, "v_tok", [128, 16 * 256], BF16)
    k_ctokP = [sbuf(es, nc, f"kctokP{p}", [64, 16 * 256], BF16) for p in range(2)]
    g128 = sbuf(es, nc, "g128", [128, NB * 20], F32)
    mBD = sbuf(es, nc, "mBD", [128, 128], F32)
    mA = sbuf(es, nc, "mA", [64, 256], F32)
    g64 = sbuf(es, nc, "g64", [64, (T // 64) * 20], F32)
    i_f = sbuf(es, nc, "i_f", [128, 128], F32)
    i_b = sbuf(es, nc, "i_b", [128, 128], BF16)
    nwbc = sbuf(es, nc, "nwbc", [128, 256], F32)
    S = {sn: [sbuf(es, nc, f"S{sn}{d}", [128, 256], F32) for d in range(2)] for sn in "lg"}
    Sb = {sn: [sbuf(es, nc, f"Sb{sn}{d}", [128, 256], BF16) for d in range(2)] for sn in "lg"}
    kb = {sn: sbuf(es, nc, f"kb{sn}", [128, 256], BF16) for sn in "lg"}
    vb = {sn: sbuf(es, nc, f"vb{sn}", [128, 256], BF16) for sn in "lg"}
    Nf = {sn: sbuf(es, nc, f"Nf{sn}", [128, 128], F32) for sn in "lg"}
    Nbs = {sn: sbuf(es, nc, f"Nbs{sn}", [128, 128], BF16) for sn in "lg"}
    Xb = {sn: [sbuf(es, nc, f"Xb{sn}{k}", [128, 128], BF16) for k in range(5)] for sn in "lg"}
    Xtb = {sn: [sbuf(es, nc, f"Xtb{sn}{k}", [128, 128], BF16) for k in range(4)] for sn in "lg"}
    Rb = {sn: [sbuf(es, nc, f"Rb{sn}{k}", [128, 128], BF16) for k in range(5)] for sn in "lg"}
    wmn = {sn: [sbuf(es, nc, f"wmn{sn}{d}", [128, 128], BF16) for d in range(2)] for sn in "lg"}
    KWnP = {sn: sbuf(es, nc, f"KWnP{sn}", [64, 256], BF16) for sn in "lg"}
    aTP = sbuf(es, nc, "aTP", [64, 256], BF16)
    uP = {sn: [sbuf(es, nc, f"uP{sn}{p}", [64, 256], BF16) for p in range(2)] for sn in "lg"}
    o_sb = sbuf(es, nc, "o_sb", [128, 256], F32)
    o_bf = sbuf(es, nc, "o_bf", [128, 256], BF16)
    rstt = sbuf(es, nc, "rstt", [128, 4], F32)

    po = psumt(es, nc, "po", [128, 256], F32)
    pT = psumt(es, nc, "pT", [128, 128], BF16)
    pbk = {"l": [psumt(es, nc, "pA", [128, 256], F32), psumt(es, nc, "pB", [128, 256], F32)],
           "g": [psumt(es, nc, "pC", [128, 256], F32), psumt(es, nc, "pD", [128, 256], F32)]}
    pbk_k = {"l": ("pA", "pB"), "g": ("pC", "pD")}

    for nm, dst, srcap in [("mBD", mBD, maskBD), ("mA", mA, maskA), ("i_f", i_f, ident_f),
                           ("i_b", i_b, ident_b), ("nwbc", nwbc, nw_bc)]:
        b.ydma(lambda e, dst=dst, srcap=srcap: e.dma_start(out=dst[:], in_=srcap[:, :]),
               writes=("consts",), tag=f"ld{nm}")
    b.ydma(lambda e: e.dma_start(out=g128.rearrange("p (tb c) -> p tb c", c=20)[:, :, :],
                                 in_=gates.rearrange("(tb p) c -> p tb c", p=128)[:, :, :]),
           writes=("consts",), tag="ldg128")
    b.ydma(lambda e: e.dma_start(out=g64.rearrange("q (tp c) -> q tp c", c=20)[:, :, :],
                                 in_=gates.rearrange("(tp q) c -> q tp c", q=64)[:, :, :]),
           writes=("consts",), tag="ldg64")

    if bhs is None:
        bhs = [(bi, h) for bi in range(BLOC) for h in range(4)]
    if True:
        for bi, h in bhs:
            bh = f"{bi}{h}"
            cS = bi * L
            r0 = 256 * h
            KFM = ("kfm0", "kfm1")
            QFM = ("qfm0", "qfm1")
            UK = {sn: (f"uP{sn}0", f"uP{sn}1") for sn in "lg"}
            for d in range(2):
                b.ydma(lambda e, d=d, cS=cS, r0=r0: e.dma_start(
                    out=q_fm[d][:], in_=qf[r0 + d * 128:r0 + (d + 1) * 128, cS:cS + L]),
                    writes=(f"qfm{d}",), tag=f"ldq{bh}{d}")
                b.ydma(lambda e, d=d, cS=cS, r0=r0: e.dma_start(
                    out=k_fm[d][:], in_=kf[r0 + d * 128:r0 + (d + 1) * 128, cS:cS + L]),
                    writes=(f"kfm{d}",), tag=f"ldk{bh}{d}")
            for tt in range(16):
                b.ydma(lambda e, tt=tt, cS=cS, r0=r0: e.dma_start(
                    out=k_tok[:, tt * 256:(tt + 1) * 256],
                    in_=kf[r0:r0 + 256, cS + tt * 128:cS + (tt + 1) * 128], transpose=True),
                    writes=("ktok",) if tt == 0 else ("ktok", "ktokX"), tag=f"ldkt{bh}{tt}")
                b.ydma(lambda e, tt=tt, cS=cS, r0=r0: e.dma_start(
                    out=v_tok[:, tt * 256:(tt + 1) * 256],
                    in_=vf[r0:r0 + 256, cS + tt * 128:cS + (tt + 1) * 128], transpose=True),
                    writes=("vtok",) if tt == 0 else ("vtok", "vtokX"), tag=f"ldvt{bh}{tt}")
            b.gdma(lambda e: e.dma_start(
                out=ktokd.rearrange("(tt p) f -> p tt f", p=128)[:, :, :],
                in_=k_tok.rearrange("p (tt f) -> p tt f", f=256)[:, :, :]),
                reads=("ktok", "ktokX"), writes=("ktokd",), tag=f"stkt{bh}")
            for j in range(4):
                b.ydma(lambda e, j=j: e.dma_start(
                    out=k_ctokP[j // 2][32 * (j % 2):32 * (j % 2) + 32, :].rearrange(
                        "p (sc f) -> p sc f", f=256)[:, :, :],
                    in_=ktokd.rearrange("(sc jj p) f -> p jj sc f",
                                        jj=4, p=32)[:, j, :, :]),
                    reads=("ktokd",), writes=("kctok",), tag=f"ldkc{bh}{j}")

            for sn in "lg":
                for d in range(2):
                    b.v(lambda e, sn=sn, d=d: (e.memset(S[sn][d][:], 0.0),
                                               e.memset(Sb[sn][d][:], 0.0))[-1],
                        writes=(f"S{sn}{d}", f"Sb{sn}{d}"), tag=f"zs{bh}{sn}{d}")

            for sc in range(NSC):
                t0 = sc * 128
                tb = bi * 16 + sc
                tc0 = bi * 64 + sc * 4

                def gmm(e, t0=t0):
                    r = None
                    for d in range(2):
                        r = e.matmul(pbk["l"][0][:, 0:128], k_fm[d][:, t0:t0 + 128],
                                     k_fm[d][:, t0:t0 + 128],
                                     start=(d == 0), stop=(d == 1), skip_group_check=True)
                    return r
                b.t(gmm, reads=KFM, writes=("pA",), tag=f"G{bh}{sc}")

                for sn in "lg":
                    bcol = (0 if sn == "l" else 4) + h
                    def nbuild(e, tb=tb, bcol=bcol, sn=sn):
                        e.tensor_tensor(Nf[sn][:], pbk["l"][0][:, 0:128], mBD[:], ALU.mult)
                        return e.tensor_scalar(Nbs[sn][:], Nf[sn][:],
                                               g128[:, tb * 20 + bcol:tb * 20 + bcol + 1],
                                               None, ALU.mult)
                    b.v(nbuild, reads=("pA", "consts"), writes=(f"Nbs{sn}", f"Nf{sn}"),
                        tag=f"nb{bh}{sc}{sn}")

                def amm(e, t0=t0):
                    r = None
                    for p in range(2):
                        for d in range(2):
                            r = e.matmul(pbk["g"][1][0:64, 128 * p:128 * (p + 1)],
                                         k_fm[d][:, t0 + 64 * p:t0 + 64 * (p + 1)],
                                         q_fm[d][:, t0:t0 + 128],
                                         start=(d == 0), stop=(d == 1), skip_group_check=True)
                    return r
                b.t(amm, reads=KFM + QFM, writes=("pD",), tag=f"amm{bh}{sc}")
                b.v(lambda e: e.tensor_tensor(aTP[:], pbk["g"][1][0:64, 0:256],
                                              mA[:], ALU.mult),
                    reads=("pD", "consts"), writes=("aTP", "pD"), tag=f"aev{bh}{sc}")

                for sn in "lg":
                    b.t(lambda e, sn=sn: e.transpose(pT[:], Nbs[sn][:], i_b[:]),
                        reads=(f"Nbs{sn}", "consts"), writes=("pT",), tag=f"ntr{bh}{sc}{sn}")
                    def mset(e, sn=sn):
                        e.tensor_copy(Xtb[sn][0][:], pT[:])
                        return e.tensor_tensor(Rb[sn][0][:], pT[:], i_f[:], ALU.add)
                    b.v(mset, reads=("pT", "consts"), writes=(f"Xtb{sn}0", f"Rb{sn}0", "pT"),
                        tag=f"mset{bh}{sc}{sn}")
                    b.v(lambda e, sn=sn: e.tensor_copy(Xb[sn][0][:], Nbs[sn][:]),
                        reads=(f"Nbs{sn}",), writes=(f"Xb{sn}0",), tag=f"xcp{bh}{sc}{sn}")

                for k in range(1, 5):
                    for sn in "lg":
                        SQ, RS = pbk_k[sn]
                        sqb = pbk[sn][0]
                        rsb = pbk[sn][1]
                        b.t(lambda e, k=k, sn=sn, sqb=sqb: e.matmul(
                            sqb[:, 0:128], Xtb[sn][k - 1][:], Xb[sn][k - 1][:],
                            start=True, stop=True, skip_group_check=True),
                            reads=(f"Xtb{sn}{k-1}", f"Xb{sn}{k-1}"), writes=(SQ,),
                            tag=f"sqX{bh}{sc}{sn}{k}")
                        b.s(lambda e, k=k, sn=sn, sqb=sqb: e.copy(Xb[sn][k][:], sqb[:, 0:128]),
                            reads=(SQ,), writes=(f"Xb{sn}{k}", SQ), tag=f"sqXe{bh}{sc}{sn}{k}")
                        if k < 4:
                            b.t(lambda e, k=k, sn=sn, sqb=sqb: e.matmul(
                                sqb[:, 0:128], Xb[sn][k - 1][:], Xtb[sn][k - 1][:],
                                start=True, stop=True, skip_group_check=True),
                                reads=(f"Xtb{sn}{k-1}", f"Xb{sn}{k-1}"), writes=(SQ,),
                                tag=f"sqT{bh}{sc}{sn}{k}")
                            b.v(lambda e, k=k, sn=sn, sqb=sqb: e.tensor_copy(Xtb[sn][k][:],
                                                                             sqb[:, 0:128]),
                                reads=(SQ,), writes=(f"Xtb{sn}{k}", SQ), tag=f"sqTe{bh}{sc}{sn}{k}")
                        def rstep(e, k=k, sn=sn, rsb=rsb):
                            e.matmul(rsb[:, 0:128], Xb[sn][k][:], Rb[sn][k - 1][:],
                                     start=True, stop=False, skip_group_check=True)
                            return e.matmul(rsb[:, 0:128], i_b[:], Rb[sn][k - 1][:],
                                            start=False, stop=True, skip_group_check=True)
                        b.t(rstep, reads=(f"Xb{sn}{k}", f"Rb{sn}{k-1}", "consts"), writes=(RS,),
                            tag=f"rs{bh}{sc}{sn}{k}")
                        b.v(lambda e, k=k, sn=sn, rsb=rsb: e.tensor_copy(Rb[sn][k][:], rsb[:, 0:128]),
                            reads=(RS,), writes=(f"Rb{sn}{k}", RS), tag=f"rse{bh}{sc}{sn}{k}")

                for sn in "lg":
                    bcol = (0 if sn == "l" else 4) + h
                    def kvb(e, sc=sc, tb=tb, bcol=bcol, sn=sn):
                        e.activation(out=kb[sn][:], in_=k_tok[:, sc * 256:(sc + 1) * 256],
                                     func=AF.Copy,
                                     scale=g128[:, tb * 20 + bcol:tb * 20 + bcol + 1])
                        return e.activation(out=vb[sn][:], in_=v_tok[:, sc * 256:(sc + 1) * 256],
                                            func=AF.Copy,
                                            scale=g128[:, tb * 20 + bcol:tb * 20 + bcol + 1])
                    b.s(kvb, reads=("ktok", "vtok", "consts"), writes=(f"kb{sn}", f"vb{sn}"),
                        tag=f"kvb{bh}{sc}{sn}")
                for sn in "lg":
                    SQ, RS = pbk_k[sn]
                    for d in range(2):
                        b.t(lambda e, d=d, sn=sn: e.matmul(pbk[sn][0][:, 0:128],
                                                           kb[sn][:, d * 128:(d + 1) * 128],
                                                           Rb[sn][4][:], start=True, stop=True,
                                                           skip_group_check=True),
                            reads=(f"kb{sn}", f"Rb{sn}4"), writes=(SQ,), tag=f"wm{bh}{sc}{sn}{d}")
                        b.s(lambda e, d=d, sn=sn: e.mul(wmn[sn][d][:],
                                                        pbk[sn][0][:, 0:128], -1.0),
                            reads=(SQ,), writes=(f"wmn{sn}{d}", SQ), tag=f"wme{bh}{sc}{sn}{d}")
                    def kwmm(e, t0=t0, sn=sn):
                        r = None
                        for d in range(2):
                            r = e.matmul(pbk[sn][1][0:64, 0:128],
                                         k_fm[d][:, t0:t0 + 64],
                                         wmn[sn][d][:], start=(d == 0), stop=(d == 1),
                                         skip_group_check=True)
                        for d in range(2):
                            r = e.matmul(pbk[sn][1][0:32, 128:256],
                                         k_fm[d][:, t0 + 64:t0 + 96],
                                         wmn[sn][d][:], start=(d == 0), stop=(d == 1),
                                         skip_group_check=True)
                        return r
                    b.t(kwmm, reads=KFM + (f"wmn{sn}0", f"wmn{sn}1"), writes=(RS,),
                        tag=f"kw{bh}{sc}{sn}")
                    b.s(lambda e, sn=sn: e.copy(KWnP[sn][:], pbk[sn][1][0:64, 0:256]),
                        reads=(RS,), writes=(f"KWnP{sn}", RS), tag=f"kwe{bh}{sc}{sn}")

                for ci in range(4):
                    for sn in "lg":
                        wcol = (8 if sn == "l" else 12) + h
                        SBK = (f"Sb{sn}0", f"Sb{sn}1")
                        bank = pbk[sn][ci % 2]
                        BK = pbk_k[sn][ci % 2]
                        r32 = slice(32 * (ci % 2), 32 * (ci % 2) + 32)
                        def umm(e, ci=ci, sn=sn, bank=bank, r32=r32):
                            r = e.matmul(bank[r32, 0:256], Rb[sn][4][:, 32 * ci:32 * (ci + 1)],
                                         vb[sn][:], start=True, stop=False, skip_group_check=True)
                            for d in range(2):
                                r = e.matmul(bank[r32, 0:256], wmn[sn][d][:, 32 * ci:32 * (ci + 1)],
                                             Sb[sn][d][:], start=False,
                                             stop=(ci == 0 and d == 1), skip_group_check=True)
                            return r
                        b.t(umm, reads=(f"Rb{sn}4", f"vb{sn}", f"wmn{sn}0", f"wmn{sn}1") + SBK,
                            writes=(BK,), tag=f"umm{bh}{sc}{sn}{ci}")
                        if ci > 0:
                            def corr(e, ci=ci, sn=sn, bank=bank, r32=r32):
                                np0 = min(ci, 2)
                                r = e.matmul(bank[r32, 0:256],
                                             KWnP[sn][0:32 * np0, 32 * ci:32 * (ci + 1)],
                                             uP[sn][0][0:32 * np0, :],
                                             start=False, stop=(ci < 3),
                                             skip_group_check=True)
                                if ci == 3:
                                    r = e.matmul(bank[r32, 0:256],
                                                 KWnP[sn][0:32, 128 + 96:128 + 128],
                                                 uP[sn][1][0:32, :],
                                                 start=False, stop=True,
                                                 skip_group_check=True)
                                return r
                            b.t(corr, reads=(f"KWnP{sn}", f"uP{sn}0", f"uP{sn}1"),
                                writes=(BK,), tag=f"corr{bh}{sc}{sn}{ci}")
                        tp = (tc0 + ci) // 2
                        b.s(lambda e, ci=ci, tp=tp, wcol=wcol, sn=sn, bank=bank, r32=r32:
                            e.activation(
                            out=uP[sn][ci // 2][r32, :], in_=bank[r32, 0:256], func=AF.Copy,
                            scale=g64[r32, tp * 20 + wcol:tp * 20 + wcol + 1]),
                            reads=(BK, "consts"), writes=(f"uP{sn}{ci // 2}", BK),
                            tag=f"uev{bh}{sc}{sn}{ci}")

                for sn in "lg":
                    SBK = (f"Sb{sn}0", f"Sb{sn}1")
                    def oqs(e, t0=t0, sn=sn):
                        r = None
                        for d in range(2):
                            r = e.matmul(po[:], q_fm[d][:, t0:t0 + 128], Sb[sn][d][:],
                                         start=(sn == "l" and d == 0), stop=False,
                                         skip_group_check=True)
                        return r
                    b.t(oqs, reads=QFM + SBK, writes=("po",), tag=f"oqs{bh}{sc}{sn}")
                    def oau(e, sn=sn):
                        r = None
                        for p in range(2):
                            r = e.matmul(po[:], aTP[:, 128 * p:128 * (p + 1)],
                                         uP[sn][p][:],
                                         start=False,
                                         stop=(sn == "g" and p == 1), skip_group_check=True)
                        return r
                    b.t(oau, reads=("aTP",) + UK[sn],
                        writes=("po",), tag=f"oau{bh}{sc}{sn}")
                    for d in range(2):
                        bank = pbk[sn][d]
                        BK = pbk_k[sn][d]
                        def smm(e, d=d, sc=sc, sn=sn, bank=bank):
                            r = None
                            for p in range(2):
                                r = e.matmul(bank[:, 0:256],
                                             k_ctokP[p][:, sc * 256 + d * 128:
                                                         sc * 256 + (d + 1) * 128],
                                             uP[sn][p][:],
                                             start=(p == 0), stop=(p == 1),
                                             skip_group_check=True)
                            return r
                        b.t(smm, reads=("kctok",) + UK[sn], writes=(BK,), tag=f"smm{bh}{sc}{sn}{d}")
                        def supd(e, d=d, sn=sn, bank=bank):
                            e.tensor_tensor(S[sn][d][:], S[sn][d][:], bank[:, 0:256], ALU.add)
                            return e.tensor_copy(Sb[sn][d][:], S[sn][d][:])
                        b.v(supd, reads=(BK, f"S{sn}{d}"),
                            writes=(f"S{sn}{d}", f"Sb{sn}{d}", BK), tag=f"sup{bh}{sc}{sn}{d}")

                def oev(e, sc=sc, tb=tb, h=h):
                    return e.scalar_tensor_tensor(o_sb[:], v_tok[:, sc * 256:(sc + 1) * 256],
                                                  g128[:, tb * 20 + 16 + h:tb * 20 + 16 + h + 1],
                                                  po[:], ALU.mult, ALU.add)
                b.v(oev, reads=("po", "vtok", "consts"), writes=("o_sb", "po"), tag=f"oev{bh}{sc}")
                b.s(lambda e: e.activation(out=o_bf[:], in_=o_sb[:], func=AF.Square,
                                           accum_out=rstt[:, 0:1]),
                    reads=("o_sb",), writes=("rs", "o_bf"), tag=f"osq{bh}{sc}")
                b.v(lambda e: e.tensor_scalar(rstt[:, 1:2], rstt[:, 0:1], 1.0 / 256, 1e-5,
                                              ALU.mult, ALU.add),
                    reads=("rs",), writes=("rs",), tag=f"oeps{bh}{sc}")
                def orst(e):
                    e.activation(out=rstt[:, 2:3], in_=rstt[:, 1:2], func=AF.Ln)
                    return e.activation(out=rstt[:, 2:3], in_=rstt[:, 2:3], func=AF.Exp, scale=-0.5)
                b.s(orst, reads=("rs",), writes=("rs",), tag=f"orst{bh}{sc}")
                def oap(e):
                    e.tensor_scalar(o_sb[:], o_sb[:], rstt[:, 2:3], None, ALU.mult)
                    return e.tensor_tensor(o_bf[:], o_sb[:], nwbc[:], ALU.mult)
                b.v(oap, reads=("o_sb", "rs", "consts"), writes=("o_bf", "o_sb"), tag=f"oap{bh}{sc}")
                b.gdma(lambda e, bi=bi, sc=sc, h=h: e.dma_start(
                    out=o_n[bi * L + sc * 128:bi * L + (sc + 1) * 128, 256 * h:256 * (h + 1)],
                    in_=o_bf[:]),
                    reads=("o_bf",), writes=(), tag=f"ost{bh}{sc}")

    es.close()


def prep_l2_consts():
    i32 = np.arange(32)
    i128 = np.arange(128)
    mBD = np.zeros((128, 128), np.float32)
    for c in range(4):
        blk = (i128[:, None] // 32 == c) & (i128[None, :] // 32 == c)
        mBD = np.where(blk & (i128[:, None] > i128[None, :]), -1.0, mBD)
    i64 = np.arange(64)
    mA = np.concatenate([(64 * p + i64[:, None] <= i128[None, :]) for p in range(2)],
                        axis=1).astype(np.float32)
    return {
        "maskBD": mBD,
        "maskA": mA,
        "ident_f": np.eye(128, dtype=np.float32),
        "ident_b": np.eye(128, dtype=np.float32).astype(BF),
    }




def emit_l3(nc, b, dr):
    _PFX[0] = "c_"
    o_nd = dr("o_n", [T, D], BF16, "ExternalInput")
    xn0_tok = dr("xn0_tok", [T, D], BF16, "ExternalInput")
    x0 = dr("x0", [T, D], F32, "ExternalInput")
    Wo_at = dr("Wo_at", [D, D], F32, "ExternalInput")
    WgT = dr("WgT", [D, 8], F32, "ExternalInput")
    W1T = dr("W1T", [8, D, 4096], F32, "ExternalInput")
    b1l = dr("b1l", [128, 256], F32, "ExternalInput")
    WoTm = dr("WoTm", [2048, D], F32, "ExternalInput")
    bo_bc = dr("bo_bc", [128, D], BF16, "ExternalInput")
    x1d = dr("x1", [T, D], F32, "ExternalOutput")
    xn1_tok = dr("xn1_tok", [T, D], BF16, "ExternalOutput")
    s0d = dr("s0d", [T, D], F32, "Internal")

    es = ExitStack()

    slab = [sbuf(es, nc, f"slab{d}", [128, 4096], BF16) for d in range(8)]
    mslab = [sbuf(es, nc, f"mslab{d}", [128, 1024], BF16) for d in range(16)]
    acc = sbuf(es, nc, "acc", [128, NB * 1024], BF16)
    gsb = [sbuf(es, nc, f"gsb{i}", [128, 512], BF16) for i in range(16)]
    xnts = [[sbuf(es, nc, f"xnts{p}{d}", [128, 512], BF16) for d in range(8)] for p in range(2)]
    wgt = sbuf(es, nc, "wgt", [128, 64], BF16)
    b1sb = sbuf(es, nc, "b1sb", [128, 256], F32)
    bobc = sbuf(es, nc, "bobc", [128, D], BF16)
    comb = sbuf(es, nc, "comb", [128, NB * 8], F32)
    combs = sbuf(es, nc, "combs", [128, NB], F32)
    ctmp = sbuf(es, nc, "ctmp", [128, 24], F32)
    sgt = [sbuf(es, nc, f"sgt{i}", [128, 512], BF16) for i in range(3)]
    _f0 = slab[0].bitcast(F32)
    _f1 = slab[1].bitcast(F32)
    _f2 = slab[2].bitcast(F32)
    xt = [_f0[:, 0:1024], _f0[:, 1024:2048]]
    st0 = [_f1[:, 0:1024], _f1[:, 1024:2048]]
    evt = sbuf(es, nc, "evt", [128, 1024], F32)
    ev = [evt[:, 0:512], evt[:, 512:1024]]
    g0t = _f2[:, 0:1024]
    x1t = _f2[:, 1024:2048]
    stats = sbuf(es, nc, "stats", [128, 8], F32)
    xnt = sbuf(es, nc, "xnt", [128, D], BF16)
    sqscr = g0t

    pj = psum8(es, nc)

    for d in range(8):
        b.ydma(lambda e, d=d: e.dma_start(out=slab[d][:, 0:T], in_=o_nd[:, d * 128:(d + 1) * 128],
                                          transpose=True),
               writes=(f"slab{d}",), tag=f"ldofm{d}")
        b.gdma(lambda e, d=d: e.dma_start(out=mslab[d][:], in_=Wo_at[d * 128:(d + 1) * 128, :]),
               writes=(f"mslab{d}",), tag=f"ldwo{d}")
    for tb in range(NB):
        u = tb % 2
        PK = (PS[2 * u], PS[2 * u + 1])
        def smm(e, tb=tb, u=u):
            r = None
            for d in range(8):
                for oc in range(2):
                    r = e.matmul(pj[2 * u + oc][:], slab[d][:, tb * 128:(tb + 1) * 128],
                                 mslab[d][:, oc * 512:(oc + 1) * 512],
                                 start=(d == 0), stop=(d == 7), skip_group_check=True)
            return r
        b.t(smm, reads=tuple(f"slab{d}" for d in range(8)) + tuple(f"mslab{d}" for d in range(8)),
            writes=PK, tag=f"smm{tb}")
        def sev(e, u=u):
            e.tensor_copy(ev[0][:, :], pj[2 * u][:])
            return e.tensor_copy(ev[1][:, :], pj[2 * u + 1][:])
        b.v(sev, reads=PK, writes=("ev0", "ev1") + PK, tag=f"sev{tb}")
        b.gdma(lambda e, tb=tb: e.dma_start(out=s0d[tb * 128:(tb + 1) * 128, 0:512], in_=ev[0][:, :]),
               reads=("ev0",), writes=("ev0w",), tag=f"sst{tb}a")
        b.gdma(lambda e, tb=tb: e.dma_start(out=s0d[tb * 128:(tb + 1) * 128, 512:1024], in_=ev[1][:, :]),
               reads=("ev1",), writes=("ev1w",), tag=f"sst{tb}b")

    b.gdma(lambda e: e.dma_start(out=wgt.rearrange("p (c f) -> p c f", f=8)[:, :, :],
                                 in_=WgT.rearrange("(c p) f -> p c f", p=128)[:, :, :]),
           writes=("wgt",), tag="ldwgt")
    b.gdma(lambda e: e.dma_start(out=b1sb[:], in_=b1l[:, :]), writes=("b1sb",), tag="ldb1")
    b.gdma(lambda e: e.dma_start(out=bobc[:], in_=bo_bc[:, :]), writes=("bobc",), tag="ldbo")
    b.v(lambda e: e.memset(acc[:], 0.0), writes=tuple(f"acc{t}" for t in range(NB)), tag="zacc")

    for ts in range(8):
        p = ts % 2
        for d in range(8):
            b.ydma(lambda e, ts=ts, d=d, p=p: e.dma_start(
                out=xnts[p][d][:], in_=xn0_tok[ts * 512:(ts + 1) * 512, d * 128:(d + 1) * 128],
                transpose=True),
                writes=(f"xnts{p}{d}",), tag=f"ldxn{ts}{d}")
        for tq in range(4):
            tb = ts * 4 + tq
            def rmm(e, tq=tq, p=p):
                r = None
                for d in range(8):
                    r = e.matmul(pj[4][:, 0:8], xnts[p][d][:, tq * 128:(tq + 1) * 128],
                                 wgt[:, d * 8:(d + 1) * 8], start=(d == 0), stop=(d == 7),
                                 skip_group_check=True)
                return r
            b.t(rmm, reads=tuple(f"xnts{p}{d}" for d in range(8)) + ("wgt",), writes=("ps4",),
                tag=f"rmm{tb}")
            def rsoft(e, tb=tb):
                e.reduce_max(ctmp[:, 0:1], pj[4][:, 0:8], axis=AX)
                e.tensor_scalar(ctmp[:, 8:16], pj[4][:, 0:8], ctmp[:, 0:1], None, ALU.subtract)
                return e.engine_nop()
            b.v(rsoft, reads=("ps4",), writes=("ct", "ps4"), tag=f"rsm{tb}")
            b.s(lambda e: e.activation(out=ctmp[:, 8:16], in_=ctmp[:, 8:16], func=AF.Exp,
                                       accum_out=ctmp[:, 1:2]),
                reads=("ct",), writes=("ct",), tag=f"rex{tb}")
            def rcomb(e, tb=tb):
                e.reciprocal(ctmp[:, 2:3], ctmp[:, 1:2])
                e.tensor_scalar(ctmp[:, 8:16], ctmp[:, 8:16], ctmp[:, 2:3], None, ALU.mult)
                e.reduce_max(ctmp[:, 3:4], ctmp[:, 8:16], axis=AX)
                e.tensor_scalar(ctmp[:, 16:24], ctmp[:, 8:16], ctmp[:, 3:4], None, ALU.is_lt)
                e.tensor_tensor(ctmp[:, 16:24], ctmp[:, 16:24], ctmp[:, 8:16], ALU.mult)
                e.reduce_max(ctmp[:, 4:5], ctmp[:, 16:24], axis=AX)
                e.tensor_tensor(ctmp[:, 5:6], ctmp[:, 3:4], ctmp[:, 4:5], ALU.add)
                e.tensor_scalar(ctmp[:, 5:6], ctmp[:, 5:6], 1.0, 1e-9, ALU.mult, ALU.add)
                e.reciprocal(ctmp[:, 5:6], ctmp[:, 5:6])
                e.tensor_scalar(ctmp[:, 16:24], ctmp[:, 8:16], ctmp[:, 4:5], None, ALU.is_ge)
                e.tensor_tensor(ctmp[:, 16:24], ctmp[:, 16:24], ctmp[:, 8:16], ALU.mult)
                e.tensor_scalar(comb[:, tb * 8:(tb + 1) * 8], ctmp[:, 16:24], ctmp[:, 5:6],
                                None, ALU.mult)
                return e.reduce_sum(combs[:, tb:tb + 1], comb[:, tb * 8:(tb + 1) * 8], axis=AX)
            b.v(rcomb, reads=("ct",), writes=(f"comb{tb}", "ct"), tag=f"rcb{tb}")


    for e_i in range(8):
        for d in range(8):
            b.gdma(lambda e, e_i=e_i, d=d: e.dma_start(out=slab[d][:],
                                                       in_=W1T[e_i, d * 128:(d + 1) * 128, :]),
                   writes=(f"slab{d}",), tag=f"lw1{e_i}{d}")
        for gc in range(16):
            b.gdma(lambda e, e_i=e_i, gc=gc: e.dma_start(out=mslab[gc][:],
                                                         in_=WoTm[gc * 128:(gc + 1) * 128, :]),
                   writes=(f"mslab{gc}",), tag=f"lwo{e_i}{gc}") if e_i == 0 else None
        for ts in range(8):
            p = ts % 2
            for d in range(8):
                b.ydma(lambda e, ts=ts, d=d, p=p: e.dma_start(
                    out=xnts[p][d][:], in_=xn0_tok[ts * 512:(ts + 1) * 512, d * 128:(d + 1) * 128],
                    transpose=True),
                    writes=(f"xnts{p}{d}",), tag=f"cxn{e_i}{ts}{d}")
            XN = tuple(f"xnts{p}{d}" for d in range(8))
            for gc in range(16):
                w = gc % 3
                ba, bb = pj[2 * w], pj[2 * w + 1]
                KA, KB = PS[2 * w], PS[2 * w + 1]

                def hmm(e, gc=gc, ba=ba, bb=bb, p=p):
                    r = None
                    for d in range(8):
                        e.matmul(ba[:], slab[d][:, gc * 128:(gc + 1) * 128], xnts[p][d][:],
                                 start=(d == 0), stop=(d == 7), skip_group_check=True)
                        r = e.matmul(bb[:], slab[d][:, (gc + 16) * 128:(gc + 17) * 128], xnts[p][d][:],
                                     start=(d == 0), stop=(d == 7), skip_group_check=True)
                    return r
                b.t(hmm, reads=tuple(f"slab{d}" for d in range(8)) + XN, writes=(KA, KB),
                    tag=f"hmm{e_i}{ts}{gc}")
                b.s(lambda e, e_i=e_i, gc=gc, bb=bb, w=w: e.activation(
                    out=sgt[w][:], in_=bb[:], func=AF.Sigmoid,
                    bias=b1sb[:, e_i * 32 + 16 + gc:e_i * 32 + 17 + gc]),
                    reads=(KB, "b1sb"), writes=(f"sgt{w}", KB), tag=f"sig{e_i}{ts}{gc}")
                b.v(lambda e, e_i=e_i, gc=gc, ba=ba, w=w: e.scalar_tensor_tensor(
                    gsb[gc][:], ba[:], b1sb[:, e_i * 32 + gc:e_i * 32 + gc + 1], sgt[w][:],
                    ALU.add, ALU.mult),
                    reads=(KA, f"sgt{w}", "b1sb"), writes=(f"gsb{gc}", KA), tag=f"glu{e_i}{ts}{gc}")
            for tq in range(4):
                tb = ts * 4 + tq
                for oc in range(2):
                    def omm(e, tq=tq, oc=oc):
                        r = None
                        for gc in range(16):
                            r = e.matmul(pj[tq * 2 + oc][:], gsb[gc][:, tq * 128:(tq + 1) * 128],
                                         mslab[gc][:, oc * 512:(oc + 1) * 512],
                                         start=(gc == 0), stop=(gc == 15), skip_group_check=True)
                        return r
                    b.t(omm, reads=tuple(f"gsb{gc}" for gc in range(16))
                        + tuple(f"mslab{gc}" for gc in range(16)),
                        writes=(PS[tq * 2 + oc],), tag=f"omm{e_i}{ts}{tq}{oc}")
                    def oev2(e, e_i=e_i, tb=tb, tq=tq, oc=oc):
                        sl = slice(tb * 1024 + oc * 512, tb * 1024 + (oc + 1) * 512)
                        return e.scalar_tensor_tensor(acc[:, sl], pj[tq * 2 + oc][:],
                                                      comb[:, tb * 8 + e_i:tb * 8 + e_i + 1],
                                                      acc[:, sl], ALU.mult, ALU.add)
                    b.v(oev2, reads=(PS[tq * 2 + oc], f"comb{tb}", f"acc{tb}"),
                        writes=(f"acc{tb}", PS[tq * 2 + oc]), tag=f"oev{e_i}{ts}{tq}{oc}")

    b.t(lambda e: e.nop(),
        reads=tuple(f"acc{t}" for t in range(NB)) + tuple(f"slab{d}" for d in range(8))
        + tuple(f"gsb{gc}" for gc in range(16)),
        writes=("phCdone",), tag="phCdone")
    for tb in range(NB):
        u = tb % 2
        def bo_add(e, tb=tb):
            return e.scalar_tensor_tensor(acc[:, tb * 1024:(tb + 1) * 1024], bobc[:],
                                          combs[:, tb:tb + 1], acc[:, tb * 1024:(tb + 1) * 1024],
                                          ALU.mult, ALU.add)
        b.v(bo_add, reads=("bobc", f"comb{tb}", f"acc{tb}"), writes=(f"acc{tb}",), tag=f"boa{tb}")
        b.s(lambda e, tb=tb: e.activation(out=g0t[:, :], in_=acc[:, tb * 1024:(tb + 1) * 1024],
                                          func=AF.Silu),
            reads=(f"acc{tb}", "phCdone"), writes=("g0t",), tag=f"sil{tb}")
        b.ydma(lambda e, tb=tb, u=u: e.dma_start(out=xt[u][:, :], in_=x0[tb * 128:(tb + 1) * 128, :]),
               reads=(f"xtw{u}", "phCdone"), writes=(f"xt{u}",), tag=f"ldx0{tb}")
        b.ydma(lambda e, tb=tb, u=u: e.dma_start(out=st0[u][:, :], in_=s0d[tb * 128:(tb + 1) * 128, :]),
               reads=(f"s0w{u}", "ev0w", "ev1w", "phCdone"), writes=(f"st0{u}",), tag=f"lds0{tb}")
        def comb2(e, u=u):
            e.tensor_tensor(g0t[:, :], g0t[:, :], st0[u][:, :], ALU.mult)
            return e.tensor_tensor(x1t[:, :], g0t[:, :], xt[u][:, :], ALU.add)
        b.v(comb2, reads=("g0t", f"st0{u}", f"xt{u}"), writes=("x1t", "g0t", f"s0w{u}"),
            tag=f"cmb{tb}")
        b.gdma(lambda e, tb=tb: e.dma_start(out=x1d[tb * 128:(tb + 1) * 128, :], in_=x1t[:, :]),
               reads=("x1t",), writes=("x1st",), tag=f"x1st{tb}")
        b.s(lambda e: e.activation(out=sqscr[:, :], in_=x1t[:, :], func=AF.Square,
                                   accum_out=stats[:, 0:1]),
            reads=("x1t",), writes=("lnst", "g0t"), tag=f"l1sq{tb}")
        def lnv(e):
            e.reduce_sum(stats[:, 1:2], x1t[:, :], axis=AX)
            e.tensor_scalar(stats[:, 2:3], stats[:, 1:2], -1.0 / D, None, ALU.mult)
            e.tensor_tensor(stats[:, 3:4], stats[:, 2:3], stats[:, 2:3], ALU.mult)
            e.tensor_scalar(stats[:, 4:5], stats[:, 0:1], 1.0 / D, None, ALU.mult)
            e.tensor_tensor(stats[:, 5:6], stats[:, 4:5], stats[:, 3:4], ALU.subtract)
            return e.tensor_scalar(stats[:, 6:7], stats[:, 5:6], 1.0, 1e-5, ALU.mult, ALU.add)
        b.v(lnv, reads=("x1t", "lnst"), writes=("lnst",), tag=f"l1v{tb}")
        def lns(e):
            e.activation(out=stats[:, 7:8], in_=stats[:, 6:7], func=AF.Ln)
            return e.activation(out=stats[:, 7:8], in_=stats[:, 7:8], func=AF.Exp, scale=-0.5)
        b.s(lns, reads=("lnst",), writes=("lnst",), tag=f"l1r{tb}")
        b.v(lambda e: e.tensor_scalar(xnt[:], x1t[:, :], stats[:, 2:3], stats[:, 7:8],
                                      ALU.add, ALU.mult),
            reads=("x1t", "lnst"), writes=("xn1o", f"xtw{u}"), tag=f"l1a{tb}")
        b.gdma(lambda e, tb=tb: e.dma_start(out=xn1_tok[tb * 128:(tb + 1) * 128, :], in_=xnt[:]),
               reads=("xn1o",), writes=("xn1st",), tag=f"xnst{tb}")

    es.close()


def prep_l3(inputs, core):
    x = inputs["L0_attn_hidden_states"]
    x0 = np.ascontiguousarray(x[2 * core:2 * core + 2].reshape(T, D))
    return {
        "x0": x0,
        "Wo_at": inputs["L0_attn_Wo"],
        "WgT": np.ascontiguousarray(inputs["L0_ffn_Wg"].T),
        "W1T": np.ascontiguousarray(inputs["L0_ffn_W1"].transpose(0, 2, 1)),
        "b1l": np.ascontiguousarray(inputs["L0_ffn_b1"].reshape(8, 32, 128).transpose(2, 0, 1).reshape(128, 256)),
        "WoTm": np.ascontiguousarray(inputs["L0_ffn_Wo"].T),
        "bo_bc": np.ascontiguousarray(np.broadcast_to(inputs["L0_ffn_bo"][None, :], (128, D))).astype(BF),
    }




def emit_l4(nc, b, dr):
    _PFX[0] = "d_"
    xn1_tok = dr("xn1_tok", [T, D], BF16, "ExternalInput")
    wdr = {n: dr(n, shp, F32, "ExternalInput") for n, shp in [
        ("g_wq", [D, D]), ("g_kvd", [D, 256]), ("g_kup", [256, 256]), ("g_vup", [256, 256]),
        ("g_wo", [D, D]), ("l_wq", [D, D]), ("l_wk", [D, 256]), ("l_wv", [D, 256]),
        ("l_wo", [D, D]), ("Wr", [D, 8]), ("br_bc", [128, 8]),
        ("n1w_bc", [128, D]), ("n2w_bc", [128, D]), ("ropeg", [L, 1024]), ("ropel", [L, 1024])]}
    masksC = dr("masksC", [128, 4 * 512], BF16, "ExternalInput")
    masksL = dr("masksL", [128, 6 * 512], BF16, "ExternalInput")
    su_c = dr("su_c", [128, 128], BF16, "ExternalInput")
    su_f = dr("su_f", [128, 128], F32, "ExternalInput")
    s1d = dr("s1", [T, D], F32, "ExternalOutput")
    rwd = dr("rw", [T, 8], F32, "ExternalOutput")
    rankd = dr("rankloc", [T, 8], F32, "ExternalOutput")
    cntd = dr("cnt", [1, 8], F32, "ExternalOutput")
    t1d = dr("t1d", [T, D], BF16, "Internal")
    t2d = dr("t2d", [T, D], BF16, "Internal")
    xmidd = dr("xmidd", [T, D], F32, "Internal")
    qtd = dr("qtd", [T, D], BF16, "Internal")
    ofmd = dr("ofmd", [D, T], BF16, "Internal")
    ktd = dr("ktd", [T, 256], BF16, "Internal")
    cntscr = dr("cntscr", [32, 8], F32, "Internal")
    offsscr = dr("offsscr", [32, 8], F32, "Internal")

    es = ExitStack()

    tfm = [sbuf(es, nc, f"tfm{d}", [128, T], BF16) for d in range(8)]
    lat = [sbuf(es, nc, f"lat{lc}", [128, T], BF16) for lc in range(2)]
    vsb = sbuf(es, nc, "vsb", [128, NB * 260], BF16)
    wsl = [sbuf(es, nc, f"wsl{d}", [128, 1024], BF16) for d in range(8)]
    wk2 = [sbuf(es, nc, f"wk2{d}", [128, 256], BF16) for d in range(8)]
    wrr = sbuf(es, nc, "wrr", [128, 64], BF16)
    qh = sbuf(es, nc, "qh", [128, L], BF16)
    kh = sbuf(es, nc, "kh", [128, L], BF16)
    qcur = sbuf(es, nc, "qcur", [64, L], BF16)
    kcur = sbuf(es, nc, "kcur", [64, L], BF16)
    ptb = [sbuf(es, nc, f"ptb{i}", [128, 1024], BF16) for i in range(2)]
    msk = sbuf(es, nc, "msk", [128, 10 * 512], BF16)
    subf = sbuf(es, nc, "subf", [128, 128], BF16)
    subF32 = sbuf(es, nc, "subF32", [128, 128], F32)
    onesF = sbuf(es, nc, "onesF", [128, 1], F32)
    onesrF = sbuf(es, nc, "onesrF", [1, 128], F32)
    rope_t = [sbuf(es, nc, f"ropet{i}", [128, 1024], BF16) for i in range(2)]
    ostg = [sbuf(es, nc, f"ostg{i}", [64, 512], BF16) for i in range(2)]
    nw1 = sbuf(es, nc, "nw1", [128, D], BF16)
    nw2 = sbuf(es, nc, "nw2", [128, D], BF16)
    brb = sbuf(es, nc, "brb", [128, 8], F32)
    xt = [sbuf(es, nc, f"xt{i}", [128, D], F32) for i in range(2)]
    qr = [sbuf(es, nc, f"qr{i}", [128, D], F32) for i in range(2)]
    qs = [sbuf(es, nc, f"qs{i}", [128, D], F32) for i in range(2)]
    qn = [sbuf(es, nc, f"qn{i}", [128, D], BF16) for i in range(2)]
    st = [sbuf(es, nc, f"st{i}", [128, 20], F32) for i in range(2)]
    xm2 = [sbuf(es, nc, f"xm2{i}", [128, D], F32) for i in range(2)]
    onesc = sbuf(es, nc, "onesc", [128, 1], BF16)
    onesr = sbuf(es, nc, "onesr", [1, 128], BF16)
    rr = [sbuf(es, nc, f"rr{i}", [1, 512], F32) for i in range(2)]
    rrb = [sbuf(es, nc, f"rrb{i}", [1, 512], BF16) for i in range(2)]
    of32 = [sbuf(es, nc, f"of32{i}", [64, 512], F32) for i in range(2)]
    selF = sbuf(es, nc, "selF", [128, 8], F32)
    selb = sbuf(es, nc, "selb", [128, 8], BF16)
    rwt = sbuf(es, nc, "rwt", [128, 8], F32)
    rk2 = sbuf(es, nc, "rk2", [128, 8], F32)
    cnts = sbuf(es, nc, "cnts", [1, 256], F32)
    c32 = sbuf(es, nc, "c32", [32, 8], F32)
    offs1 = sbuf(es, nc, "offs1", [1, 256], F32)

    esp = ExitStack()
    pj = psum8(esp, nc, sfx="a")
    TFM = tuple(f"tfm{d}" for d in range(8))
    WSL = tuple(f"wsl{d}" for d in range(8))

    for nm, dst in [("n1w_bc", nw1), ("n2w_bc", nw2), ("br_bc", brb)]:
        b.gdma(lambda e, nm=nm, dst=dst: e.dma_start(out=dst[:], in_=wdr[nm][:, :]),
               writes=("consts",), tag=f"c_{nm}")
    b.gdma(lambda e: e.dma_start(out=msk[:, 0:2048], in_=masksC[:, :]), writes=("consts",), tag="cmc")
    b.gdma(lambda e: e.dma_start(out=msk[:, 2048:5120], in_=masksL[:, :]), writes=("consts",), tag="cml")
    b.gdma(lambda e: e.dma_start(out=subf[:], in_=su_c[:, :]), writes=("consts",), tag="csu")
    b.gdma(lambda e: e.dma_start(out=subF32[:], in_=su_f[:, :]), writes=("consts",), tag="csuf")
    b.gdma(lambda e: e.dma_start(
        out=wrr.rearrange("p (c f) -> p c f", f=8)[:, :, :],
        in_=wdr["Wr"].rearrange("(c p) f -> p c f", p=128)[:, :, :]), writes=("wrr",), tag="cwr")
    b.g(lambda e: (e.memset(onesc[:], 1.0), e.memset(onesr[:], 1.0),
                   e.memset(onesF[:], 1.0), e.memset(onesrF[:], 1.0))[-1],
        writes=("consts",), tag="cones")

    def rms_pass(srcd, srck, nw, dstd, dstk, pfx):
        for tb in range(NB):
            u = tb % 2
            XT, ST, QR, QN = f"xt{u}", f"st{u}", f"qr{u}", f"qn{u}"
            b.gdma(lambda e, tb=tb, srcd=srcd, u=u: e.dma_start(
                out=xt[u][:], in_=srcd[tb * 128:(tb + 1) * 128, :]),
                reads=srck, writes=(XT,), tag=f"{pfx}ld{tb}")
            b.s(lambda e, u=u: e.activation(out=qr[u][:], in_=xt[u][:], func=AF.Square,
                                            accum_out=st[u][:, 0:1]),
                reads=(XT,), writes=(ST, QR), tag=f"{pfx}sq{tb}")
            b.v(lambda e, u=u: e.tensor_scalar(st[u][:, 1:2], st[u][:, 0:1], 1.0 / D, 1e-6,
                                               ALU.mult, ALU.add),
                reads=(ST,), writes=(ST,), tag=f"{pfx}v{tb}")
            def rst2(e, u=u):
                e.activation(out=st[u][:, 2:3], in_=st[u][:, 1:2], func=AF.Ln)
                return e.activation(out=st[u][:, 2:3], in_=st[u][:, 2:3], func=AF.Exp, scale=-0.5)
            b.s(rst2, reads=(ST,), writes=(ST,), tag=f"{pfx}r{tb}")
            def app(e, nw=nw, u=u):
                e.tensor_scalar(qr[u][:], xt[u][:], st[u][:, 2:3], None, ALU.mult)
                return e.tensor_tensor(qn[u][:], qr[u][:], nw[:], ALU.mult)
            b.v(app, reads=(XT, ST, "consts", QR), writes=(QN, QR), tag=f"{pfx}a{tb}")
            b.gdma(lambda e, tb=tb, dstd=dstd, u=u: e.dma_start(
                out=dstd[tb * 128:(tb + 1) * 128, :], in_=qn[u][:]),
                reads=(QN,), writes=(dstk + str(tb),), tag=f"{pfx}w{tb}")

    def load_fm(srcd, deps, pfx):
        for d in range(8):
            b.ydma(lambda e, d=d, srcd=srcd: e.dma_start(
                out=tfm[d][:], in_=srcd[:, d * 128:(d + 1) * 128], transpose=True),
                reads=deps, writes=(f"tfm{d}",), tag=f"lf{pfx}{d}")

    def proj_rope(wname, nheads, roped, dstd, dstk, src, srck, nsrc, pfx):
        ncol = nheads * 64
        wbuf = wsl if ncol == 1024 else wk2
        wkey = "wsl" if ncol == 1024 else "wk2"
        for d in range(nsrc):
            b.gdma(lambda e, d=d, wname=wname, ncol=ncol, wbuf=wbuf: e.dma_start(
                out=wbuf[d][:, 0:ncol], in_=wdr[wname][d * 128:(d + 1) * 128, 0:ncol]),
                writes=(f"{wkey}{d}",), tag=f"lw{pfx}{d}")
        for tb in range(NB):
            u = tb % 2
            QR, QS, QN, ST, RP = f"qr{u}", f"qs{u}", f"qn{u}", f"st{u}", f"ropet{u}"
            pos = (tb % 16) * 128
            bk = [pj[2 * u], pj[2 * u + 1]]
            PK = (PS[2 * u], PS[2 * u + 1]) if ncol == 1024 else (PS[2 * u],)

            def pmm(e, tb=tb, ncol=ncol, nsrc=nsrc, src=src, wbuf=wbuf, bk=bk):
                r = None
                for d in range(nsrc):
                    lhs = src[d][:, tb * 128:(tb + 1) * 128]
                    if ncol == 1024:
                        for oc in range(2):
                            r = e.matmul(bk[oc][:], lhs, wbuf[d][:, oc * 512:(oc + 1) * 512],
                                         start=(d == 0), stop=(d == nsrc - 1), skip_group_check=True)
                    else:
                        r = e.matmul(bk[0][:, 0:256], lhs, wbuf[d][:, 0:256],
                                     start=(d == 0), stop=(d == nsrc - 1), skip_group_check=True)
                return r
            b.t(pmm, reads=srck + tuple(f"{wkey}{d}" for d in range(nsrc)), writes=PK,
                tag=f"pm{pfx}{tb}")
            b.gdma(lambda e, pos=pos, roped=roped, u=u: e.dma_start(
                out=rope_t[u][:], in_=wdr[roped][pos:pos + 128, :]),
                reads=(RP,), writes=(RP,), tag=f"rl{pfx}{tb}")

            def rope2(e, ncol=ncol, nheads=nheads, u=u, bk=bk):
                r = None
                rv = rope_t[u].rearrange("p (h i) -> p h i", i=32)
                nparts = 2 if ncol == 1024 else 1
                for oc in range(nparts):
                    hh = 8 if ncol == 1024 else nheads
                    pv = bk[oc][:, 0:hh * 64].rearrange("p (h half i) -> p h half i",
                                                        half=2, i=32)
                    qv = qr[u].rearrange("p (h half i) -> p h half i",
                                         half=2, i=32)[:, oc * 8:oc * 8 + hh, :, :]
                    cv = rv[:, oc * 8:oc * 8 + hh, :]
                    sv = rv[:, 16 + oc * 8:16 + oc * 8 + hh, :]
                    tv = qs[u].rearrange("p (h half i) -> p h half i",
                                         half=2, i=32)[:, 0:hh, :, :]
                    xe = pv[:, :, 0, :]
                    xo = pv[:, :, 1, :]
                    e.tensor_tensor(qv[:, :, 0, :], xe, cv, ALU.mult)
                    e.tensor_tensor(tv[:, :, 0, :], xo, sv, ALU.mult)
                    e.tensor_tensor(qv[:, :, 0, :], qv[:, :, 0, :], tv[:, :, 0, :], ALU.subtract)
                    e.tensor_tensor(qv[:, :, 1, :], xe, sv, ALU.mult)
                    e.tensor_tensor(tv[:, :, 1, :], xo, cv, ALU.mult)
                    r = e.tensor_tensor(qv[:, :, 1, :], qv[:, :, 1, :], tv[:, :, 1, :], ALU.add)
                return r
            b.v(rope2, reads=PK + (RP,), writes=(QR, QS) + PK, tag=f"ro{pfx}{tb}")
            b.s(lambda e, ncol=ncol, u=u: e.activation(out=qs[u][:, 0:ncol], in_=qr[u][:, 0:ncol],
                                                       func=AF.Square),
                reads=(QR,), writes=(QS,), tag=f"qs{pfx}{tb}")

            def qkn2(e, nheads=nheads, u=u):
                e.reduce_sum(st[u][:, 4:4 + nheads],
                             qs[u].rearrange("p (h f) -> p h f", f=64)[:, 0:nheads, :], axis=AX)
                return e.tensor_scalar(st[u][:, 4:4 + nheads], st[u][:, 4:4 + nheads], 1.0 / 64, 1e-6,
                                       ALU.mult, ALU.add)
            b.v(qkn2, reads=(QS,), writes=(ST,), tag=f"qn2{pfx}{tb}")

            def qkn3(e, nheads=nheads, u=u):
                e.activation(out=st[u][:, 4:4 + nheads], in_=st[u][:, 4:4 + nheads], func=AF.Ln)
                return e.activation(out=st[u][:, 4:4 + nheads], in_=st[u][:, 4:4 + nheads],
                                    func=AF.Exp, scale=-0.5)
            b.s(qkn3, reads=(ST,), writes=(ST,), tag=f"qn3{pfx}{tb}")

            def qkn4(e, nheads=nheads, u=u):
                r = None
                for hh in range(nheads):
                    r = e.tensor_scalar(qn[u][:, hh * 64:(hh + 1) * 64], qr[u][:, hh * 64:(hh + 1) * 64],
                                        st[u][:, 4 + hh:5 + hh], None, ALU.mult)
                return r
            b.v(qkn4, reads=(QR, ST), writes=(QN,), tag=f"qn4{pfx}{tb}")
            b.gdma(lambda e, tb=tb, dstd=dstd, ncol=ncol, u=u: e.dma_start(
                out=dstd[tb * 128:(tb + 1) * 128, 0:ncol], in_=qn[u][:, 0:ncol]),
                reads=(QN,), writes=(dstk + str(tb),), tag=f"qw{pfx}{tb}")

    def proj_v(wname, src, srck, nsrc, pfx):
        b.v(lambda e: e.memset(vsb[:], 1.0), reads=tuple(f"vsb{t}" for t in range(NB)),
            writes=("vsbones",), tag=f"vones{pfx}")
        for d in range(nsrc):
            b.gdma(lambda e, d=d, wname=wname: e.dma_start(
                out=wk2[d][:], in_=wdr[wname][d * 128:(d + 1) * 128, :]),
                writes=(f"wk2{d}",), tag=f"lwv{pfx}{d}")
        for tb in range(NB):
            def vmm(e, tb=tb, nsrc=nsrc, src=src):
                r = None
                for d in range(nsrc):
                    r = e.matmul(pj[2][:, 0:256], src[d][:, tb * 128:(tb + 1) * 128], wk2[d][:, 0:256],
                                 start=(d == 0), stop=(d == nsrc - 1), skip_group_check=True)
                return r
            b.t(vmm, reads=srck + tuple(f"wk2{d}" for d in range(nsrc)), writes=("ps2",),
                tag=f"vm{pfx}{tb}")
            b.v(lambda e, tb=tb: e.tensor_copy(
                vsb[:, tb * 260:tb * 260 + 260].rearrange("p (kv f) -> p kv f", f=65)[:, :, 0:64],
                pj[2][:, 0:256].rearrange("p (kv f) -> p kv f", f=64)),
                reads=("ps2", "vsbones"), writes=(f"vsb{tb}", "ps2"), tag=f"ve{pfx}{tb}")

    def attention(local, pfx, psA):
        pscore, pvbs, bcbs = psA
        qdeps = tuple(f"qtd{t}" for t in range(NB))
        kdeps = tuple(f"ktd{t}" for t in range(NB))
        for bi in range(2):
            for kvp in range(2):
                b.ydma(lambda e, bi=bi, kvp=kvp: e.dma_start(
                    out=kh[:], in_=ktd[bi * L:(bi + 1) * L, kvp * 128:(kvp + 1) * 128], transpose=True),
                    reads=kdeps, writes=("kh",), tag=f"lkh{pfx}{bi}{kvp}")
                for hqp in range(4 * kvp, 4 * kvp + 4):
                    b.ydma(lambda e, bi=bi, hqp=hqp: e.dma_start(
                        out=qh[:], in_=qtd[bi * L:(bi + 1) * L, hqp * 128:(hqp + 1) * 128],
                        transpose=True),
                        reads=qdeps, writes=("qh",), tag=f"lqh{pfx}{bi}{hqp}")
                    for hq in (2 * hqp, 2 * hqp + 1):
                      kv = hq // 4
                      kr = 64 * (kv % 2)
                      qr0 = 64 * (hq % 2)
                      b.v(lambda e, kr=kr, qr0=qr0: (
                          e.tensor_copy(kcur[:], kh[kr:kr + 64, :]),
                          e.tensor_copy(qcur[:], qh[qr0:qr0 + 64, :]))[-1],
                          reads=("kh", "qh"), writes=("kcur", "qcur"), tag=f"cp{pfx}{bi}{hq}")
                      for qt in range(4):
                        w = qt % 2
                        pvb = pvbs[w]
                        bcb = bcbs[w]
                        PVK, BCK = f"pvk{w}", f"bck{w}"
                        if local:
                            if qt == 0:
                                groups = [([0, 1], 2048 + 1024), ([2, 3], 2048 + 2048)]
                            else:
                                groups = [([4 * qt - 2, 4 * qt - 1], 2048),
                                          ([4 * qt, 4 * qt + 1], 2048 + 1024),
                                          ([4 * qt + 2, 4 * qt + 3], 2048 + 2048)]
                        else:
                            groups = [([4 * g, 4 * g + 1], None) for g in range(qt)]
                            groups += [([4 * g + 2, 4 * g + 3], None) for g in range(qt)]
                            groups.sort()
                            groups += [([4 * qt, 4 * qt + 1], 0),
                                       ([4 * qt + 2, 4 * qt + 3], 1024)]
                        ng = len(groups)

                        def emit_qk(gi, qt=qt):
                            blocks, moff = groups[gi]
                            u = gi % 2
                            SK = f"psc{u}"
                            n = len(blocks)
                            for j, kb in enumerate(blocks):
                                b.t(lambda e, kb=kb, j=j, u=u, qt=qt: e.matmul(
                                    pscore[:, (2 * u + j) * 512:(2 * u + j + 1) * 512],
                                    kcur[:, kb * 128:(kb + 1) * 128],
                                    qcur[:, qt * 512:(qt + 1) * 512],
                                    start=True, stop=True, skip_group_check=True),
                                    reads=("kcur", "qcur"), writes=(SK,),
                                    tag=f"qk{pfx}{bi}{hq}{qt}{kb}")
                            b.s(lambda e, u=u, n=n: e.activation(
                                out=ptb[u][:, 0:n * 512],
                                in_=pscore[:, 2 * u * 512:(2 * u + n) * 512],
                                func=AF.Exp, scale=0.125),
                                reads=(SK,), writes=(f"ptb{u}", SK),
                                tag=f"ex{pfx}{bi}{hq}{qt}{gi}")
                            if moff is not None:
                                b.v(lambda e, moff=moff, u=u, n=n: e.tensor_tensor(
                                    ptb[u][:, 0:n * 512], ptb[u][:, 0:n * 512],
                                    msk[:, moff:moff + n * 512], ALU.mult),
                                    reads=(f"ptb{u}", "consts"), writes=(f"ptb{u}",),
                                    tag=f"mk{pfx}{bi}{hq}{qt}{gi}")

                        def emit_pv(gi, first, last, pvb=pvb, PVK=PVK):
                            blocks, _ = groups[gi]
                            u = gi % 2
                            for j, kb in enumerate(blocks):
                                b.t(lambda e, kb=kb, j=j, u=u, first=first, last=last,
                                    bi=bi, kv=kv:
                                    e.matmul(pvb[0:65, :],
                                             vsb[:, (bi * 16 + kb) * 260 + kv * 65:
                                                 (bi * 16 + kb) * 260 + (kv + 1) * 65],
                                             ptb[u][:, j * 512:(j + 1) * 512],
                                             start=(first and j == 0),
                                             stop=(last and j == len(blocks) - 1),
                                             skip_group_check=True),
                                    reads=(f"ptb{u}", f"vsb{bi * 16 + kb}"), writes=(PVK,),
                                    tag=f"pv{pfx}{bi}{hq}{qt}{gi}{j}")

                        emit_qk(0)
                        for gi in range(ng):
                            if gi + 1 < ng:
                                emit_qk(gi + 1)
                            emit_pv(gi, gi == 0, gi == ng - 1)

                        def nrm1(e, w=w, pvb=pvb):
                            e.activation(out=rr[w][:], in_=pvb[64:65, 0:512], func=AF.Ln)
                            e.activation(out=rr[w][:], in_=rr[w][:], func=AF.Exp, scale=-1.0)
                            return e.copy(rrb[w][:], rr[w][:])
                        b.s(nrm1, reads=(PVK,), writes=(f"rrb{w}", f"rr{w}"), tag=f"n1{pfx}{bi}{hq}{qt}")
                        b.t(lambda e, w=w, bcb=bcb: e.matmul(bcb[0:64, :], onesr[0:1, 0:64], rrb[w][:],
                                                             start=True, stop=True,
                                                             skip_group_check=True),
                            reads=(f"rrb{w}", "consts"), writes=(BCK,), tag=f"n2{pfx}{bi}{hq}{qt}")

                        b.s(lambda e, w=w, pvb=pvb: e.copy(of32[w][:], pvb[0:64, 0:512]),
                            reads=(PVK,), writes=(f"of32{w}", PVK), tag=f"n3a{pfx}{bi}{hq}{qt}")
                        b.v(lambda e, w=w, bcb=bcb: e.tensor_tensor(
                            ostg[w][:], of32[w][:], bcb[0:64, 0:512], ALU.mult),
                            reads=(f"of32{w}", BCK), writes=(f"ostg{w}", BCK),
                            tag=f"n3{pfx}{bi}{hq}{qt}")
                        b.gdma(lambda e, bi=bi, hq=hq, qt=qt, w=w: e.dma_start(
                            out=ofmd[hq * 64:(hq + 1) * 64, bi * L + qt * 512:bi * L + (qt + 1) * 512],
                            in_=ostg[w][:]),
                            reads=(f"ostg{w}",), writes=("ofmd",), tag=f"n4{pfx}{bi}{hq}{qt}")

    def oproj(wname, resd, resk, dstd, dstk, pfx):
        for d in range(8):
            b.gdma(lambda e, d=d, wname=wname: e.dma_start(out=wsl[d][:],
                                                           in_=wdr[wname][d * 128:(d + 1) * 128, :]),
                   writes=(f"wsl{d}",), tag=f"lwo{pfx}{d}")
            b.ydma(lambda e, d=d: e.dma_start(out=tfm[d][:], in_=ofmd[d * 128:(d + 1) * 128, :]),
                   reads=("ofmd",), writes=(f"tfm{d}",), tag=f"lof{pfx}{d}")
        for tb in range(NB):
            u = tb % 2
            XT = f"xt{u}"
            bk = [pj[2 * u], pj[2 * u + 1]]
            PK = (PS[2 * u], PS[2 * u + 1])

            def wmm(e, tb=tb, bk=bk):
                r = None
                for d in range(8):
                    for oc in range(2):
                        r = e.matmul(bk[oc][:], tfm[d][:, tb * 128:(tb + 1) * 128],
                                     wsl[d][:, oc * 512:(oc + 1) * 512],
                                     start=(d == 0), stop=(d == 7), skip_group_check=True)
                return r
            b.t(wmm, reads=TFM + WSL, writes=PK, tag=f"om{pfx}{tb}")
            b.gdma(lambda e, tb=tb, resd=resd, u=u: e.dma_start(
                out=xt[u][:], in_=resd[tb * 128:(tb + 1) * 128, :]),
                reads=resk, writes=(XT,), tag=f"orl{pfx}{tb}")

            def radd(e, u=u, bk=bk):
                e.tensor_tensor(xm2[u][:, 0:512], xt[u][:, 0:512], bk[0][:], ALU.add)
                return e.tensor_tensor(xm2[u][:, 512:1024], xt[u][:, 512:1024], bk[1][:], ALU.add)
            b.v(radd, reads=(XT,) + PK, writes=(f"xm2{u}",) + PK, tag=f"ra{pfx}{tb}")
            b.gdma(lambda e, tb=tb, dstd=dstd, u=u: e.dma_start(
                out=dstd[tb * 128:(tb + 1) * 128, :], in_=xm2[u][:]),
                reads=(f"xm2{u}",), writes=(dstk + str(tb),), tag=f"os{pfx}{tb}")

    rms_pass(xn1_tok, (), nw1, t1d, "t1d", "t1")
    load_fm(t1d, tuple(f"t1d{t}" for t in range(NB)), "t1f")
    for d in range(8):
        b.gdma(lambda e, d=d: e.dma_start(out=wk2[d][:], in_=wdr["g_kvd"][d * 128:(d + 1) * 128, :]),
               writes=(f"wk2{d}",), tag=f"lwlat{d}")
    for lc in range(2):
        def lmm(e, lc=lc):
            r = None
            for d in range(8):
                for tt in range(8):
                    r = e.matmul(pj[tt][:], wk2[d][:, lc * 128:(lc + 1) * 128],
                                 tfm[d][:, tt * 512:(tt + 1) * 512],
                                 start=(d == 0), stop=(d == 7), skip_group_check=True)
            return r
        b.t(lmm, reads=TFM + tuple(f"wk2{d}" for d in range(8)), writes=PS, tag=f"lm{lc}")
        def lev(e, lc=lc):
            r = None
            for tt in range(8):
                r = e.tensor_copy(lat[lc][:, tt * 512:(tt + 1) * 512], pj[tt][:])
            return r
        b.v(lev, reads=PS, writes=(f"lat{lc}",) + PS, tag=f"le{lc}")
    LATK = ("lat0", "lat1")
    proj_rope("g_wq", 16, "ropeg", qtd, "qtd", tfm, TFM, 8, "qg")
    proj_rope("g_kup", 4, "ropeg", ktd, "ktd", lat, LATK, 2, "kg")
    proj_v("g_vup", lat, LATK, 2, "vg")
    esp.close()
    b.barrier("ga0")
    esp = ExitStack()
    psA = (psumt(esp, nc, "pscga", [128, 2048], F32),
           [psumt(esp, nc, f"pvga{w}", [128, 512], F32) for w in range(2)],
           [psumt(esp, nc, f"bcga{w}", [128, 512], F32) for w in range(2)])
    attention(False, "ga", psA)
    esp.close()
    b.barrier("ga1")
    esp = ExitStack()
    pj = psum8(esp, nc, sfx="b")
    oproj("g_wo", xn1_tok, (), xmidd, "xmidd", "g")
    rms_pass(xmidd, tuple(f"xmidd{t}" for t in range(NB)), nw2, t2d, "t2d", "t2")
    load_fm(t2d, tuple(f"t2d{t}" for t in range(NB)), "t2f")
    proj_rope("l_wq", 16, "ropel", qtd, "qtd", tfm, TFM, 8, "ql")
    proj_rope("l_wk", 4, "ropel", ktd, "ktd", tfm, TFM, 8, "kl")
    proj_v("l_wv", tfm, TFM, 8, "vl")
    esp.close()
    b.barrier("la0")
    esp = ExitStack()
    psB = (psumt(esp, nc, "pscla", [128, 2048], F32),
           [psumt(esp, nc, f"pvla{w}", [128, 512], F32) for w in range(2)],
           [psumt(esp, nc, f"bcla{w}", [128, 512], F32) for w in range(2)])
    attention(True, "la", psB)
    esp.close()
    b.barrier("la1")
    esp = ExitStack()
    pj = psum8(esp, nc, sfx="c")
    oproj("l_wo", xmidd, tuple(f"xmidd{t}" for t in range(NB)), s1d, "s1d", "l")

    load_fm(xn1_tok, (), "xnf")
    for tb in range(NB):
        def rmm(e, tb=tb):
            r = None
            for d in range(8):
                r = e.matmul(pj[3][:, 0:8], tfm[d][:, tb * 128:(tb + 1) * 128],
                             wrr[:, d * 8:(d + 1) * 8],
                             start=(d == 0), stop=(d == 7), skip_group_check=True)
            return r
        b.t(rmm, reads=TFM + ("wrr",), writes=("ps3",), tag=f"rt{tb}")
        def rsel(e):
            e.tensor_tensor(st[0][:, 4:12], pj[3][:, 0:8], brb[:], ALU.add)
            e.reduce_max(st[0][:, 0:1], st[0][:, 4:12], axis=AX)
            e.tensor_scalar(st[0][:, 12:20], st[0][:, 4:12], st[0][:, 0:1], None, ALU.is_ge)
            e.tensor_scalar(st[0][:, 12:20], st[0][:, 12:20], -1e30, None, ALU.mult)
            e.tensor_tensor(st[0][:, 12:20], st[0][:, 12:20], st[0][:, 4:12], ALU.add)
            e.reduce_max(st[0][:, 1:2], st[0][:, 12:20], axis=AX)
            e.tensor_scalar(selF[:], st[0][:, 4:12], st[0][:, 1:2], None, ALU.is_ge)
            e.tensor_scalar(st[0][:, 12:20], st[0][:, 4:12], st[0][:, 0:1], None, ALU.subtract)
            return e.tensor_tensor(st[0][:, 2:3], st[0][:, 1:2], st[0][:, 0:1], ALU.subtract)
        b.v(rsel, reads=("ps3", "consts"), writes=("st", "selF", "ps3"), tag=f"rs{tb}")
        def rexp(e):
            e.activation(out=st[0][:, 12:20], in_=st[0][:, 12:20], func=AF.Exp)
            return e.activation(out=st[0][:, 3:4], in_=st[0][:, 2:3], func=AF.Exp)
        b.s(rexp, reads=("st",), writes=("st",), tag=f"re{tb}")
        def rwv(e):
            e.tensor_scalar(st[0][:, 3:4], st[0][:, 3:4], 1.0, 1.0, ALU.mult, ALU.add)
            e.reciprocal(st[0][:, 3:4], st[0][:, 3:4])
            e.tensor_tensor(rwt[:], st[0][:, 12:20], selF[:], ALU.mult)
            return e.tensor_scalar(rwt[:], rwt[:], st[0][:, 3:4], None, ALU.mult)
        b.v(rwv, reads=("st", "selF"), writes=("rwt",), tag=f"rw{tb}")
        b.gdma(lambda e, tb=tb: e.dma_start(out=rwd[tb * 128:(tb + 1) * 128, :], in_=rwt[:]),
               reads=("rwt",), writes=("rwst",), tag=f"rwst{tb}")
        def rkmm(e):
            e.matmul(pj[6][:, 0:8], subF32[:], selF[:], start=True, stop=True, skip_group_check=True)
            return e.matmul(pj[7][0:1, 0:8], onesF[:], selF[:], start=True, stop=True,
                            skip_group_check=True)
        b.t(rkmm, reads=("selF", "consts"), writes=("ps6", "ps7"), tag=f"rk{tb}")
        def rke(e, tb=tb):
            e.tensor_copy(rk2[:], pj[6][:, 0:8])
            return e.tensor_copy(cnts[:, tb * 8:(tb + 1) * 8], pj[7][0:1, 0:8])
        b.v(rke, reads=("ps6", "ps7"), writes=("rk2", "cnts", "ps6", "ps7"), tag=f"rke{tb}")
        b.gdma(lambda e, tb=tb: e.dma_start(out=rankd[tb * 128:(tb + 1) * 128, :], in_=rk2[:]),
               reads=("rk2",), writes=(f"rkst{tb}",), tag=f"rkst{tb}")
    b.gdma(lambda e: e.dma_start(out=cntscr.rearrange("(o tb) f -> o (tb f)", o=1)[:, :], in_=cnts[:]),
           reads=("cnts",), writes=("cntd",), tag="cst")
    b.ydma(lambda e: e.dma_start(out=c32[:], in_=cntscr[:, :]), reads=("cntd",), writes=("c32",), tag="cld")
    def offmm(e):
        e.matmul(pj[6][0:32, 0:8], subF32[0:32, 0:32], c32[:], start=True, stop=True,
                 skip_group_check=True)
        return e.matmul(pj[7][0:1, 0:8], onesF[0:32, 0:1], c32[:], start=True, stop=True,
                        skip_group_check=True)
    b.t(offmm, reads=("c32", "consts"), writes=("ps6", "ps7"), tag="offmm")
    def offe(e):
        e.tensor_copy(st[0][0:32, 4:12], pj[6][0:32, 0:8])
        return e.tensor_copy(st[0][0:1, 12:20], pj[7][0:1, 0:8])
    b.v(offe, reads=("ps6", "ps7"), writes=("st", "ps6", "ps7"), tag="offe")
    b.gdma(lambda e: e.dma_start(out=offsscr[:, :], in_=st[0][0:32, 4:12]),
           reads=("st",), writes=("offd",), tag="ofst")
    b.gdma(lambda e: e.dma_start(out=cntd[0:1, :], in_=st[0][0:1, 12:20]),
           reads=("st",), writes=("cntout",), tag="cntst")
    b.ydma(lambda e: e.dma_start(out=offs1[:], in_=offsscr.rearrange("(o a) b -> o (a b)", o=1)[:, :]),
           reads=("offd",), writes=("offs1",), tag="ofld")
    for tb in range(NB):
        b.ydma(lambda e, tb=tb: e.dma_start(out=selF[:], in_=rankd[tb * 128:(tb + 1) * 128, :]),
               reads=(f"rkst{tb}", "selF"), writes=("selF",), tag=f"rrl{tb}")
        b.t(lambda e, tb=tb: e.matmul(pj[6][:, 0:8], onesrF[:], offs1[:, tb * 8:(tb + 1) * 8],
                                      start=True, stop=True, skip_group_check=True),
            reads=("offs1", "consts"), writes=("ps6",), tag=f"rb{tb}")
        b.v(lambda e: e.tensor_tensor(rk2[:], selF[:], pj[6][:, 0:8], ALU.add),
            reads=("selF", "ps6"), writes=("rk2", "ps6"), tag=f"ra2{tb}")
        b.gdma(lambda e, tb=tb: e.dma_start(out=rankd[tb * 128:(tb + 1) * 128, :], in_=rk2[:]),
               reads=("rk2",), writes=(f"rk2st{tb}",), tag=f"rk2w{tb}")

    esp.close()
    es.close()


def prep_l4(inputs):
    def rope_tab(base):
        inv = 1.0 / base ** (np.arange(0, 64, 2, dtype=np.float64) / 64)
        fr = np.arange(L, dtype=np.float64)[:, None] * inv[None, :]
        cos = np.cos(fr).astype(np.float32)
        sin = np.sin(fr).astype(np.float32)
        return np.concatenate([np.tile(cos, (1, 16)), np.tile(sin, (1, 16))], axis=1)
    i128 = np.arange(128)
    i512 = np.arange(512)
    mC = np.zeros((128, 4 * 512), np.float32)
    for ix, delta in enumerate([0, 128, 256, 384]):
        mC[:, ix * 512:(ix + 1) * 512] = ((i128[:, None] + delta) <= i512[None, :])
    mL = np.zeros((128, 6 * 512), np.float32)
    for ix, delta in enumerate([-256, -128, 0, 128, 256, 384]):
        d = i512[None, :] - (i128[:, None] + delta)
        mL[:, ix * 512:(ix + 1) * 512] = (d >= 0) & (d < 256)
    su = (i128[:, None] < i128[None, :]).astype(np.float32)
    perm64 = np.concatenate([np.arange(0, 64, 2), np.arange(1, 64, 2)])

    def phead(W):
        H = W.shape[1] // 64
        idx = np.concatenate([h * 64 + perm64 for h in range(H)])
        return np.ascontiguousarray(W[:, idx])
    return {
        "g_wq": phead(inputs["L1_attn_g_wq"]), "g_kvd": inputs["L1_attn_g_kv_down"],
        "g_kup": phead(inputs["L1_attn_g_k_up"]), "g_vup": inputs["L1_attn_g_v_up"],
        "g_wo": inputs["L1_attn_g_wo"],
        "l_wq": phead(inputs["L1_attn_l_wq"]), "l_wk": phead(inputs["L1_attn_l_wk"]),
        "l_wv": inputs["L1_attn_l_wv"], "l_wo": inputs["L1_attn_l_wo"],
        "Wr": inputs["L1_ffn_Wr"],
        "br_bc": np.ascontiguousarray(np.broadcast_to(inputs["L1_ffn_br"][None, :], (128, 8))).astype(np.float32),
        "n1w_bc": np.ascontiguousarray(np.broadcast_to(inputs["L1_attn_norm1_w"][None, :], (128, D))).astype(np.float32),
        "n2w_bc": np.ascontiguousarray(np.broadcast_to(inputs["L1_attn_norm2_w"][None, :], (128, D))).astype(np.float32),
        "ropeg": rope_tab(1e6), "ropel": rope_tab(1e4),
        "masksC": mC.astype(BF), "masksL": mL.astype(BF),
        "su_c": su.astype(BF),
        "su_f": su,
    }




def emit_l5(nc, b, dr):
    _PFX[0] = "e_"
    xn1_tok = dr("xn1_tok", [T, D], BF16, "ExternalInput")
    x1d = dr("x1", [T, D], F32, "ExternalInput")
    s1d = dr("s1", [T, D], F32, "ExternalInput")
    rwd = dr("rw", [T, 8], F32, "ExternalInput")
    rankd = dr("rankloc", [T, 8], F32, "ExternalInput")
    cnt_all = dr("cnt_all", [8, 8], F32, "ExternalInput")
    pref = dr("pref", [8, 1], F32, "ExternalInput")
    W1 = dr("W1", [8, D, 4096], F32, "ExternalInput")
    b1l = dr("b1l5", [128, 256], F32, "ExternalInput")
    W2 = dr("W2", [8, 4096, D], F32, "ExternalInput")
    b2in = dr("b2in", [8, D], F32, "ExternalInput")
    ident_b = dr("ident_b", [128, 128], BF16, "ExternalInput")
    x2d = dr("x2", [T, D], F32, "ExternalOutput")

    es = ExitStack()
    TS = 256
    NTS = T // TS

    w1s = [sbuf(es, nc, f"w1s{d}", [128, 4096], BF16) for d in range(8)]
    w2s = [sbuf(es, nc, f"w2s{j}", [128, 4096], BF16) for j in range(8)]
    acc = sbuf(es, nc, "acc", [128, NB * 1024], BF16)
    ghc = [sbuf(es, nc, f"ghc{i}", [128, TS], BF16) for i in range(2)]
    xnts = [[sbuf(es, nc, f"xnts{p}{d}", [128, TS], BF16) for d in range(8)] for p in range(2)]
    b1sb = sbuf(es, nc, "b1sb", [128, 256], F32)
    b2sb = sbuf(es, nc, "b2sb", [8, D], BF16)
    kwsb = sbuf(es, nc, "kwsb", [128, NB * 8], F32)
    kwb = sbuf(es, nc, "kwb", [128, 8], BF16)
    kwT = sbuf(es, nc, "kwT", [8, 256], BF16)
    ib = sbuf(es, nc, "ib", [128, 128], BF16)
    ctile = sbuf(es, nc, "ctile", [8, 8], F32)
    prefsb = sbuf(es, nc, "prefsb", [8, 1], F32)
    offr = sbuf(es, nc, "offr", [1, 8], F32)
    onesrF = sbuf(es, nc, "onesrF", [1, 128], F32)
    cobc = sbuf(es, nc, "cobc", [128, 8], F32)
    rkt = sbuf(es, nc, "rkt", [128, 8], F32)
    rwt = sbuf(es, nc, "rwt", [128, 8], F32)
    pj = [psumt(es, nc, f"ps{i}", [128, 512], F32) for i in range(7)]
    pjb = psumt(es, nc, "pjb", [128, 128], BF16)

    b.ydma(lambda e: e.dma_start(out=ctile[:], in_=cnt_all[:, :]), reads=("cntall",),
           writes=("ctile",), tag="lca")
    b.ydma(lambda e: e.dma_start(out=prefsb[:], in_=pref[:, :]), writes=("pref",), tag="lpr")
    b.ydma(lambda e: e.dma_start(out=ib[:], in_=ident_b[:, :]), writes=("ib",), tag="lib")
    b.gdma(lambda e: e.dma_start(out=b1sb[:], in_=b1l[:, :]), writes=("b1sb",), tag="lb1")
    b.gdma(lambda e: e.dma_start(out=b2sb[:], in_=b2in[:, :]), writes=("b2sb",), tag="lb2")
    b.g(lambda e: e.memset(onesrF[:], 1.0), writes=("onesrF",), tag="ones")
    b.t(lambda e: e.matmul(pj[0][0:1, 0:8], prefsb[:], ctile[:], start=True, stop=True,
                           skip_group_check=True),
        reads=("ctile", "pref"), writes=("ps0",), tag="coremm")
    b.v(lambda e: e.tensor_copy(offr[:], pj[0][0:1, 0:8]), reads=("ps0",), writes=("offr", "ps0"),
        tag="coree")
    b.t(lambda e: e.matmul(pj[1][:, 0:8], onesrF[:], offr[:], start=True, stop=True,
                           skip_group_check=True),
        reads=("offr", "onesrF"), writes=("ps1",), tag="corebc")
    b.v(lambda e: e.tensor_copy(cobc[:], pj[1][:, 0:8]), reads=("ps1",), writes=("cobc", "ps1"),
        tag="corebe")

    b.v(lambda e: e.memset(acc[:], 0.0), writes=tuple(f"acc{t}" for t in range(NB)), tag="zacc")
    for tb in range(NB):
        b.ydma(lambda e, tb=tb: e.dma_start(out=rkt[:], in_=rankd[tb * 128:(tb + 1) * 128, :]),
               reads=("rkt",), writes=("rkt",), tag=f"lrk{tb}")
        b.ydma(lambda e, tb=tb: e.dma_start(out=rwt[:], in_=rwd[tb * 128:(tb + 1) * 128, :]),
               reads=("rwt",), writes=("rwt",), tag=f"lrw{tb}")
        def kw(e, tb=tb):
            e.tensor_tensor(rkt[:], rkt[:], cobc[:], ALU.add)
            e.tensor_scalar(rkt[:], rkt[:], 8192.0, None, ALU.is_lt)
            return e.tensor_tensor(kwsb[:, tb * 8:(tb + 1) * 8], rkt[:], rwt[:], ALU.mult)
        b.v(kw, reads=("rkt", "rwt", "cobc"), writes=(f"kw{tb}", "rkt"), tag=f"kw{tb}")

    for e_i in range(8):
        for d in range(8):
            b.gdma(lambda e, e_i=e_i, d=d: e.dma_start(out=w1s[d][:],
                                                       in_=W1[e_i, d * 128:(d + 1) * 128, :]),
                   writes=(f"w1s{d}",), tag=f"lw1{e_i}{d}")
        for j in range(8):
            def lw2(e, e_i=e_i, j=j):
                return e.dma_start(
                    out=w2s[j].rearrange("p (h f) -> p h f", f=1024)[:, :, :],
                    in_=W2[e_i].rearrange("(h p) f -> p h f", p=128)[:, 4 * j:4 * j + 4, :])
            b.gdma(lw2, writes=(f"w2s{j}",), tag=f"lw2{e_i}{j}")
        W1K = tuple(f"w1s{d}" for d in range(8))
        W2K = tuple(f"w2s{j}" for j in range(8))
        for ts in range(NTS):
            tb0 = ts * 2
            p = ts % 2
            for d in range(8):
                b.ydma(lambda e, ts=ts, d=d, p=p: e.dma_start(
                    out=xnts[p][d][:], in_=xn1_tok[ts * TS:(ts + 1) * TS, d * 128:(d + 1) * 128],
                    transpose=True),
                    writes=(f"xnts{p}{d}",), tag=f"cxn{e_i}{ts}{d}")
            XN = tuple(f"xnts{p}{d}" for d in range(8))

            def mk_hmm(hc, p=p):
                u = hc % 2
                def hmm(e, hc=hc, p=p):
                    r = None
                    for d in range(8):
                        r = e.matmul(pj[5 + (hc % 2)][:, 0:TS], w1s[d][:, hc * 128:(hc + 1) * 128],
                                     xnts[p][d][:], start=(d == 0), stop=(d == 7),
                                     skip_group_check=True)
                    return r
                b.t(hmm, reads=W1K + XN, writes=(PS[5 + u],), tag=f"hm{e_i}{ts}{hc}")
                b.s(lambda e, e_i=e_i, hc=hc, u=u: e.activation(
                    out=ghc[u][:], in_=pj[5 + u][:, 0:TS], func=AF.Gelu,
                    bias=b1sb[:, e_i * 32 + hc:e_i * 32 + hc + 1]),
                    reads=(PS[5 + u], "b1sb"), writes=(f"ghc{u}", PS[5 + u]), tag=f"ge{e_i}{ts}{hc}")

            def mk_omm(hc, e_i=e_i):
                u = hc % 2
                def omm(e, hc=hc, u=u, e_i=e_i):
                    r = None
                    for tq in range(2):
                        for oc in range(2):
                            r = e.matmul(pj[tq * 2 + oc][:],
                                         ghc[u][:, tq * 128:(tq + 1) * 128],
                                         w2s[hc // 4][:, (hc % 4) * 1024 + oc * 512:
                                                      (hc % 4) * 1024 + (oc + 1) * 512],
                                         start=(hc == 0), stop=(hc == 31 and e_i != 7),
                                         skip_group_check=True)
                    return r
                b.t(omm, reads=(f"ghc{u}", f"w2s{hc // 4}"), writes=tuple(PS[i] for i in range(4)),
                    tag=f"om{e_i}{ts}{hc}")

            mk_hmm(0)
            for hc in range(32):
                if hc + 1 < 32:
                    mk_hmm(hc + 1)
                mk_omm(hc)
            def bmm(e, ts=ts, e_i=e_i):
                r = None
                for tq in range(2):
                    for oc in range(2):
                        r = e.matmul(pj[tq * 2 + oc][:],
                                     kwT[:, tq * 128:(tq + 1) * 128],
                                     b2sb[:, oc * 512:(oc + 1) * 512],
                                     start=False, stop=True, skip_group_check=True)
                return r
            if e_i == 7:
                for tq in range(2):
                    b.v(lambda e, tb=tb0 + tq: e.tensor_copy(kwb[:], kwsb[:, tb * 8:(tb + 1) * 8]),
                        reads=(f"kw{tb0 + tq}", "kwb"), writes=("kwb",), tag=f"kwc{ts}{tq}")
                    b.t(lambda e: e.transpose(pjb[0:8, :], kwb[:, 0:8], ib[:]),
                        reads=("kwb", "ib"), writes=("pjb",), tag=f"kwt{ts}{tq}")
                    b.v(lambda e, tq=tq: e.tensor_copy(kwT[:, tq * 128:(tq + 1) * 128], pjb[0:8, :]),
                        reads=("pjb",), writes=("kwT", "pjb"), tag=f"kwte{ts}{tq}")
                b.t(bmm, reads=("b2sb", "kwT"),
                    writes=tuple(PS[i] for i in range(4)), tag=f"bm{ts}")
            for tq in range(2):
                tb = tb0 + tq
                for oc in range(2):
                    def oev(e, e_i=e_i, tb=tb, tq=tq, oc=oc):
                        sl = slice(tb * 1024 + oc * 512, tb * 1024 + (oc + 1) * 512)
                        return e.scalar_tensor_tensor(acc[:, sl], pj[tq * 2 + oc][:],
                                                      kwsb[:, tb * 8 + e_i:tb * 8 + e_i + 1],
                                                      acc[:, sl], ALU.mult, ALU.add)
                    b.v(oev, reads=(PS[tq * 2 + oc], f"kw{tb}", f"acc{tb}"),
                        writes=(f"acc{tb}", PS[tq * 2 + oc]), tag=f"oe{e_i}{ts}{tq}{oc}")

    _f0 = w1s[0].bitcast(F32)
    _f1 = w1s[1].bitcast(F32)
    g1t = _f0[:, 0:1024]
    x1t = _f0[:, 1024:2048]
    s1t = _f1[:, 0:1024]
    x2t = _f1[:, 1024:2048]
    b.t(lambda e: e.nop(),
        reads=tuple(f"acc{t}" for t in range(NB)) + tuple(f"w1s{d}" for d in range(2)),
        writes=("phC",), tag="phC")
    for tb in range(NB):
        b.s(lambda e, tb=tb: e.activation(out=g1t[:, :], in_=acc[:, tb * 1024:(tb + 1) * 1024],
                                          func=AF.Silu),
            reads=(f"acc{tb}", "phC"), writes=("g1t",), tag=f"sl{tb}")
        b.ydma(lambda e, tb=tb: e.dma_start(out=x1t[:, :], in_=x1d[tb * 128:(tb + 1) * 128, :]),
               reads=("phC", "x1t"), writes=("x1t",), tag=f"lx1{tb}")
        b.ydma(lambda e, tb=tb: e.dma_start(out=s1t[:, :], in_=s1d[tb * 128:(tb + 1) * 128, :]),
               reads=("phC", "s1t"), writes=("s1t",), tag=f"ls1{tb}")
        def fin(e):
            e.tensor_tensor(x2t[:, :], s1t[:, :], g1t[:, :], ALU.mult)
            return e.tensor_tensor(x2t[:, :], x2t[:, :], x1t[:, :], ALU.add)
        b.v(fin, reads=("g1t", "x1t", "s1t"), writes=("x2t", "g1t", "s1t", "x1t"), tag=f"fn{tb}")
        b.gdma(lambda e, tb=tb: e.dma_start(out=x2d[tb * 128:(tb + 1) * 128, :], in_=x2t[:, :]),
               reads=("x2t",), writes=("x2st",), tag=f"sx2{tb}")

    es.close()


def prep_l5(inputs, core):
    return {
        "W1": inputs["L1_ffn_W1"],
        "b1l5": np.ascontiguousarray(inputs["L1_ffn_b1"].reshape(8, 32, 128).transpose(2, 0, 1).reshape(128, 256)),
        "W2": inputs["L1_ffn_W2"],
        "b2in": inputs["L1_ffn_b2"],
        "ident_b": np.eye(128, dtype=np.float32).astype(BF),
        "pref": (np.arange(8) < core).astype(np.float32).reshape(8, 1),
    }



def _mk_dr(nc, internal_names=()):
    cache = {}

    def dr(name, shape, dtype, kind):
        if name not in cache:
            k = "Internal" if name in internal_names else kind
            cache[name] = nc.dram_tensor(name, shape, dtype, kind=k).ap()
        return cache[name]
    return dr


def build_l1():
    nc = bass.Bass("TRN2", target_bir_lowering=False, debug=False)
    b = Builder()
    emit_l1(nc, b, _mk_dr(nc))
    b.build(nc)
    return nc


def build_l2(bhs=None):
    nc = bass.Bass("TRN2", target_bir_lowering=False, debug=False)
    b = Builder()
    emit_l2(nc, b, _mk_dr(nc), bhs)
    b.build(nc)
    return nc


def build_l3():
    nc = bass.Bass("TRN2", target_bir_lowering=False, debug=False)
    b = Builder()
    emit_l3(nc, b, _mk_dr(nc))
    b.build(nc)
    return nc


def build_l4():
    nc = bass.Bass("TRN2", target_bir_lowering=False, debug=False)
    b = Builder()
    emit_l4(nc, b, _mk_dr(nc))
    b.build(nc)
    return nc


def build_l5():
    nc = bass.Bass("TRN2", target_bir_lowering=False, debug=False)
    b = Builder()
    emit_l5(nc, b, _mk_dr(nc))
    b.build(nc)
    return nc


def emit_cc(nc, b, dr):
    cc_in = dr("cnt", [1, 8], F32, "Internal")
    cc_out = dr("cnt_all", [8, 8], F32, "Internal")

    def coll(e):
        return e.collective_compute(
            "AllGather", ALU.bypass, [[0, 1, 2, 3, 4, 5, 6, 7]],
            ins=[cc_in.opt()], outs=[cc_out.opt()])
    b.ccg(coll, reads=("cntout",), writes=("cntall",), tag="allgather")


_FUSE_INTERNAL_A = frozenset(["xn0_tok", "qf", "kf", "vf", "gates", "o_n"])


def build_fusedA():
    nc = bass.Bass("TRN2", target_bir_lowering=False, debug=False)
    b = Builder()
    dr = _mk_dr(nc, _FUSE_INTERNAL_A)
    emit_l1(nc, b, dr)
    b.barrier("s1")
    emit_l2(nc, b, dr)
    b.barrier("s2")
    emit_l3(nc, b, dr)
    b.barrier("s3")
    emit_l4(nc, b, dr)
    b.build(nc)
    return nc


_FUSE_INTERNAL_ALL = _FUSE_INTERNAL_A | frozenset(
    ["x1", "xn1_tok", "s1", "rw", "rankloc", "cnt", "cnt_all"])


def build_fused_all():
    nc = bass.Bass("TRN2", target_bir_lowering=False, debug=False)
    b = Builder()
    dr = _mk_dr(nc, _FUSE_INTERNAL_ALL)
    emit_l1(nc, b, dr)
    b.barrier("s1")
    emit_l2(nc, b, dr)
    b.barrier("s2")
    emit_l3(nc, b, dr)
    b.barrier("s3")
    emit_l4(nc, b, dr)
    b.barrier("s4")
    emit_cc(nc, b, dr)
    emit_l5(nc, b, dr)
    b.build(nc)
    return nc


def prep_l1(inputs, core):
    x = inputs["L0_attn_hidden_states"]
    x0 = np.ascontiguousarray(x[2 * core:2 * core + 2].reshape(T, D))
    cvw = np.concatenate([
        inputs[f"L0_attn_conv_{c}"].reshape(8, 128, 4).transpose(1, 0, 2).reshape(128, 32)
        for c in "qkv"], axis=1)
    wbgh = np.concatenate([inputs["L0_attn_Wb"], inputs["L0_attn_Wg"], inputs["L0_attn_Wh"]], axis=1)
    bgh = np.ascontiguousarray(np.broadcast_to(
        np.concatenate([inputs["L0_attn_bg"], inputs["L0_attn_bh"]])[None, :], (128, 8)))
    return {
        "x0": x0,
        "Wq": inputs["L0_attn_Wq"], "Wk": inputs["L0_attn_Wk"], "Wv": inputs["L0_attn_Wv"],
        "convw": np.ascontiguousarray(cvw),
        "Wbgh": np.ascontiguousarray(wbgh),
        "bgh_bc": bgh,
    }


_NC_CACHE = {}


def _get_nc(name, fn):
    if name not in _NC_CACHE:
        _NC_CACHE[name] = fn()
    return _NC_CACHE[name]


def kernel(**inputs):
    cores = list(range(NCORES))
    ncF = _get_nc("fALL", build_fused_all)
    shared = {}
    shared.update(prep_l1(inputs, 0))
    shared.update(prep_l2_consts())
    shared["nw_bc"] = np.ascontiguousarray(np.broadcast_to(
        inputs["L0_attn_norm_w"][None, :], (128, 256))).astype(np.float32)
    shared.update(prep_l3(inputs, 0))
    shared.update(prep_l4(inputs))
    shared.update(prep_l5(inputs, 0))
    x = inputs["L0_attn_hidden_states"]
    im = []
    for c in cores:
        m = dict(shared)
        m["x0"] = np.ascontiguousarray(x[2 * c:2 * c + 2].reshape(T, D))
        m["pref"] = (np.arange(8) < c).astype(np.float32).reshape(8, 1)
        im.append(m)
    r = run_bass_kernel_spmd(ncF, im, core_ids=cores).results

    out = np.empty((16, 2048, 1024), np.float32)
    for c in cores:
        out[2 * c:2 * c + 2] = np.asarray(r[c]["x2"]).reshape(2, L, D)
    return out


# version 12 (final)
# speedup vs baseline: 1.1140x; 1.1140x over previous
import numpy as np
import ml_dtypes
import concourse.bass as bass
import concourse.mybir as mybir
from concourse.bass_utils import run_bass_kernel_spmd
from contextlib import ExitStack

F32 = mybir.dt.float32
BF16 = mybir.dt.bfloat16
AF = mybir.ActivationFunctionType
ALU = mybir.AluOpType
AX = mybir.AxisListType.X

NCORES = 8
BLOC = 2
L = 2048
T = BLOC * L
D = 1024
NB = T // 128
BF = ml_dtypes.bfloat16

ENGS = ("t", "v", "s", "g", "y")


class Op:
    __slots__ = ("eng", "fn", "reads", "writes", "dma", "idx", "sem_val",
                 "milestone", "deps", "tag", "sem_id", "cc")

    def __init__(self, eng, fn, reads, writes, dma, idx, tag, cc=False):
        self.eng = eng
        self.fn = fn
        self.reads = tuple(reads)
        self.writes = tuple(writes)
        self.dma = dma
        self.idx = idx
        self.milestone = False
        self.deps = []
        self.sem_val = 0
        self.tag = tag
        self.cc = cc


class Builder:

    def __init__(self):
        self.ops = []
        self.last_writer = {}
        self.readers = {}
        self.stage_keys = ()

    def barrier(self, tag):
        lastc = {}
        dmas = {}
        for o in self.ops:
            if o.dma:
                dmas.setdefault(o.eng, []).append(o.idx)
            else:
                lastc[o.eng] = o.idx
        deps = set()
        for e2, idx in lastc.items():
            if e2 != "t":
                deps.add(idx)
        for q, lst in dmas.items():
            deps.update(lst[-self.NDMACLS:])
        bar = self.op("t", lambda e: e.nop(), reads=(), writes=(f"__bar_{tag}",),
                      tag=f"bar_{tag}")
        for d in sorted(deps):
            if d not in bar.deps:
                bar.deps.append(d)
        self.stage_keys = (f"__bar_{tag}",)
        return bar

    def op(self, eng, fn, reads=(), writes=(), dma=False, tag="", cc=False):
        reads = tuple(reads) + self.stage_keys
        o = Op(eng, fn, reads, writes, dma, len(self.ops), tag, cc)
        deps = set()
        for bb in o.reads:
            w = self.last_writer.get(bb)
            if w is not None:
                deps.add(w)
        for bb in o.writes:
            w = self.last_writer.get(bb)
            if w is not None:
                deps.add(w)
            for r in self.readers.get(bb, ()):
                deps.add(r)
        for d in sorted(deps):
            dop = self.ops[d]
            if dop.eng == o.eng and not dop.dma:
                continue
            o.deps.append(d)
        for bb in o.writes:
            self.last_writer[bb] = o.idx
            self.readers[bb] = []
        for bb in o.reads:
            self.readers.setdefault(bb, []).append(o.idx)
        self.ops.append(o)
        return o

    def t(self, fn, reads=(), writes=(), tag=""):
        return self.op("t", fn, reads, writes, False, tag)

    def v(self, fn, reads=(), writes=(), tag=""):
        return self.op("v", fn, reads, writes, False, tag)

    def s(self, fn, reads=(), writes=(), tag=""):
        return self.op("s", fn, reads, writes, False, tag)

    def g(self, fn, reads=(), writes=(), tag=""):
        return self.op("g", fn, reads, writes, False, tag)

    def gdma(self, fn, reads=(), writes=(), tag=""):
        return self.op("g", fn, reads, writes, True, tag)

    def ccg(self, fn, reads=(), writes=(), tag=""):
        return self.op("g", fn, reads, writes, False, tag, cc=True)

    def ydma(self, fn, reads=(), writes=(), tag=""):
        return self.op("y", fn, reads, writes, True, tag)

    NDMACLS = 8

    def build(self, nc, final_eng="g"):
        for o in self.ops:
            if o.dma:
                o.milestone = True
            for d in o.deps:
                self.ops[d].milestone = True
        last_by_sem = {}
        for o in self.ops:
            last_by_sem[(o.eng, o.dma)] = o.idx
        for idx in last_by_sem.values():
            self.ops[idx].milestone = True
        cum = {}
        dma_ord = {}
        for o in self.ops:
            if o.dma:
                k = dma_ord.get(o.eng, 0)
                dma_ord[o.eng] = k + 1
                sid = (o.eng, "d", k % self.NDMACLS)
                o.sem_id = sid
                cum[sid] = cum.get(sid, 0) + 16
                o.sem_val = cum[sid]
            elif o.cc:
                sid = (o.eng, "cc")
                o.sem_id = sid
                o.milestone = True
                cum[sid] = cum.get(sid, 0) + 1
                o.sem_val = cum[sid]
            else:
                sid = (o.eng, "c")
                o.sem_id = sid
                if o.milestone:
                    cum[sid] = cum.get(sid, 0) + 1
                o.sem_val = cum.get(sid, 0)

        es = ExitStack()
        sems = {sid: es.enter_context(
            nc.semaphore("sem_" + "_".join(map(str, sid)))) for sid in cum}
        eng_ops = {e: [o for o in self.ops if o.eng == e] for e in ENGS}

        class _AutoDrain:
            AUTODRAIN = True
            _PASS = ("wait_ge", "drain", "dma_start", "sem_inc")

            def __init__(self, eng):
                object.__setattr__(self, "_e", eng)
                object.__setattr__(self, "_n", 0)

            def __getattr__(self, name):
                fn = getattr(self._e, name)
                if name.startswith("_") or name in self._PASS or not callable(fn):
                    return fn

                def wrapped(*a, **k):
                    if self.AUTODRAIN and self._n > 0:
                        self._e.drain()
                    object.__setattr__(self, "_n", self._n + 1)
                    return fn(*a, **k)
                return wrapped

        with nc.Block() as blk:
            def emitter(eng_name):
                def inner(eng):
                    if eng_name in ("v", "s", "g"):
                        eng = _AutoDrain(eng)
                    hw = {}

                    def wait(sid, val):
                        if hw.get(sid, -1) < val:
                            eng.wait_ge(sems[sid], val)
                            hw[sid] = val

                    for o in eng_ops[eng_name]:
                        for d in o.deps:
                            dop = self.ops[d]
                            wait(dop.sem_id, dop.sem_val)
                        if o.dma and o.sem_val > 16:
                            wait(o.sem_id, o.sem_val - 16)
                        ins = o.fn(eng)
                        if o.milestone:
                            ins.then_inc(sems[o.sem_id], 16 if o.dma else 1)
                    if eng_name == final_eng:
                        for sid, v in cum.items():
                            if v > 0 and sid != (final_eng, "c"):
                                wait(sid, v)
                return inner

            if eng_ops["t"]:
                blk.tensor(emitter("t"))
            if eng_ops["v"]:
                blk.vector(emitter("v"))
            if eng_ops["s"]:
                blk.scalar(emitter("s"))
            if eng_ops["g"] or final_eng == "g":
                blk.gpsimd(emitter("g"))
            if eng_ops["y"]:
                blk.sync(emitter("y"))
        es.close()
        return nc


_PFX = [""]


def sbuf(es, nc, name, shape, dtype):
    return es.enter_context(nc.sbuf_tensor(_PFX[0] + name, shape, dtype))


def psumt(es, nc, name, shape, dtype=F32):
    return es.enter_context(nc.psum_tensor(_PFX[0] + name, shape, dtype))


def psum8(es, nc, dtype=F32, sfx=""):
    return [psumt(es, nc, f"ps{i}{sfx}", [128, 512], dtype) for i in range(8)]


PS = tuple(f"ps{i}" for i in range(8))



def emit_l1(nc, b, dr):
    _PFX[0] = "a_"
    x0 = dr("x0", [T, D], F32, "ExternalInput")
    Wqkv = {c: dr("W" + c, [D, D], F32, "ExternalInput") for c in "qkv"}
    convw = dr("convw", [128, 96], F32, "ExternalInput")
    Wbgh = dr("Wbgh", [D, 16], F32, "ExternalInput")
    bgh_bc = dr("bgh_bc", [128, 8], F32, "ExternalInput")
    xn0_tok = dr("xn0_tok", [T, D], BF16, "ExternalOutput")
    out_fm = {c: dr(c + "f", [D, T], BF16, "ExternalOutput") for c in "qkv"}
    gates = dr("gates", [T, 20], F32, "ExternalOutput")
    raw = {c: dr(c + "raw", [D, T], BF16, "Internal") for c in "qk"}

    es = ExitStack()

    xt = [sbuf(es, nc, f"xt{i}", [128, D], F32) for i in range(2)]
    stats = [sbuf(es, nc, f"stats{i}", [128, 8], F32) for i in range(2)]
    xnt = [sbuf(es, nc, f"xnt{i}", [128, D], BF16) for i in range(2)]
    sqscr = sbuf(es, nc, "sqscr", [128, D], F32)
    xn_fm = [sbuf(es, nc, f"xnfm{c}", [128, T], BF16) for c in range(8)]
    wsb = [sbuf(es, nc, f"wsb{c}", [128, D], BF16) for c in range(8)]
    cw = sbuf(es, nc, "cw", [128, 96], F32)
    stage = sbuf(es, nc, "stage", [128, 2 * (L + 3)], F32)
    conv_o = sbuf(es, nc, "convo", [128, T], F32)
    silu_o = sbuf(es, nc, "siluo", [128, T], BF16)
    wbgh_sb = sbuf(es, nc, "wbghsb", [128, 128], BF16)
    bgh_sb = sbuf(es, nc, "bghsb", [128, 8], F32)
    glog = [sbuf(es, nc, f"glog{i}", [128, 16], F32) for i in range(2)]
    gtmp = [sbuf(es, nc, f"gtmp{i}", [128, 28], F32) for i in range(2)]
    gout = [sbuf(es, nc, f"gout{i}", [128, 20], F32) for i in range(2)]
    qa = sbuf(es, nc, "qa", [128, T], BF16)
    qb = sbuf(es, nc, "qb", [128, T], BF16)
    sqa = sbuf(es, nc, "sqa", [128, T], BF16)
    sqb = sbuf(es, nc, "sqb", [128, T], BF16)
    onesc = sbuf(es, nc, "onesc", [128, 1], BF16)
    ones_row = sbuf(es, nc, "onesrow", [1, 128], BF16)
    rr = conv_o[0:1, :]
    rstd_bf = silu_o[0:1, :]
    outn = sbuf(es, nc, "outn", [128, T], BF16)
    outnB = sbuf(es, nc, "outnB", [128, T], BF16)

    pj = psum8(es, nc)

    for i in range(NB):
        u = i % 2
        XT, ST, XN = f"xt{u}", f"st{u}", f"xn{u}"
        b.ydma(lambda e, i=i, u=u: e.dma_start(out=xt[u][:], in_=x0[i * 128:(i + 1) * 128, :]),
               reads=(XT + "w",), writes=(XT,), tag=f"ln_ld{i}")
        b.s(lambda e, u=u: e.activation(out=sqscr[:], in_=xt[u][:], func=AF.Square,
                                        accum_out=stats[u][:, 0:1]),
            reads=(XT,), writes=(ST, "sqscr"), tag=f"ln_sq{i}")

        def lnv(e, u=u):
            e.reduce_sum(stats[u][:, 1:2], xt[u][:], axis=AX)
            e.tensor_scalar(stats[u][:, 2:3], stats[u][:, 1:2], -1.0 / D, None, ALU.mult)
            e.tensor_tensor(stats[u][:, 3:4], stats[u][:, 2:3], stats[u][:, 2:3], ALU.mult)
            e.tensor_scalar(stats[u][:, 4:5], stats[u][:, 0:1], 1.0 / D, None, ALU.mult)
            e.tensor_tensor(stats[u][:, 5:6], stats[u][:, 4:5], stats[u][:, 3:4], ALU.subtract)
            return e.tensor_scalar(stats[u][:, 6:7], stats[u][:, 5:6], 1.0, 1e-5, ALU.mult, ALU.add)
        b.v(lnv, reads=(XT, ST), writes=(ST,), tag=f"ln_v{i}")

        def lns(e, u=u):
            e.activation(out=stats[u][:, 7:8], in_=stats[u][:, 6:7], func=AF.Ln)
            return e.activation(out=stats[u][:, 7:8], in_=stats[u][:, 7:8], func=AF.Exp, scale=-0.5)
        b.s(lns, reads=(ST,), writes=(ST,), tag=f"ln_rs{i}")
        b.v(lambda e, u=u: e.tensor_scalar(xnt[u][:], xt[u][:], stats[u][:, 2:3],
                                           stats[u][:, 7:8], ALU.add, ALU.mult),
            reads=(XT, ST), writes=(XN, XT + "w"), tag=f"ln_ap{i}")
        b.gdma(lambda e, i=i, u=u: e.dma_start(out=xn0_tok[i * 128:(i + 1) * 128, :], in_=xnt[u][:]),
               reads=(XN,), writes=(f"xn0_tok{i}",), tag=f"ln_st{i}")

    def setup(e):
        e.memset(onesc[:], 1.0)
        e.memset(ones_row[:], 1.0)
        e.memset(stage[:, 0:3], 0.0)
        return e.memset(stage[:, L + 3:L + 6], 0.0)
    b.g(setup, reads=(), writes=("consts", "stage_pad"), tag="consts")
    b.gdma(lambda e: e.dma_start(out=cw[:], in_=convw[:, :]), writes=("cw",), tag="cwl")
    b.gdma(lambda e: e.dma_start(out=wbgh_sb.rearrange("p (c f) -> p c f", f=16)[:, :, :],
                                 in_=Wbgh.rearrange("(c p) f -> p c f", p=128)[:, :, :]),
           writes=("wbgh",), tag="wbghl")
    b.gdma(lambda e: e.dma_start(out=bgh_sb[:], in_=bgh_bc[:, :]), writes=("bgh",), tag="bghl")

    all_xn = tuple(f"xn0_tok{i}" for i in range(NB))
    for c in range(8):
        b.ydma(lambda e, c=c: e.dma_start(out=xn_fm[c][:], in_=xn0_tok[:, c * 128:(c + 1) * 128],
                                          transpose=True),
               reads=all_xn, writes=(f"xnfm{c}",), tag=f"xf{c}")
    XNFM = tuple(f"xnfm{c}" for c in range(8))

    for i in range(NB):
        u = i % 2
        GP = PS[6 + u]

        def gmm(e, i=i, u=u):
            r = None
            for dc in range(8):
                r = e.matmul(pj[6 + u][:, 0:16], xn_fm[dc][:, i * 128:(i + 1) * 128],
                             wbgh_sb[:, dc * 16:(dc + 1) * 16], start=(dc == 0), stop=(dc == 7))
            return r
        b.t(gmm, reads=XNFM + ("wbgh",), writes=(GP,), tag=f"gmm{i}")

        def gv1(e, u=u):
            e.tensor_copy(glog[u][:], pj[6 + u][:, 0:16])
            e.tensor_tensor(gtmp[u][:, 0:8], glog[u][:, 8:16], bgh_sb[:, 0:8], ALU.add)
            e.tensor_tensor(gtmp[u][:, 8:12], gtmp[u][:, 0:4], gtmp[u][:, 4:8], ALU.max)
            e.tensor_scalar(gtmp[u][:, 8:12], gtmp[u][:, 8:12], 0.0, None, ALU.max)
            e.tensor_tensor(gtmp[u][:, 12:16], gtmp[u][:, 0:4], gtmp[u][:, 8:12], ALU.subtract)
            e.tensor_tensor(gtmp[u][:, 16:20], gtmp[u][:, 4:8], gtmp[u][:, 8:12], ALU.subtract)
            return e.tensor_scalar(gtmp[u][:, 20:24], gtmp[u][:, 8:12], -1.0, None, ALU.mult)
        b.v(gv1, reads=(GP, "bgh"), writes=(f"gt{u}", GP), tag=f"gv1_{i}")

        def gs1(e, u=u):
            e.activation(out=gout[u][:, 0:8], in_=glog[u][:, 0:8], func=AF.Sigmoid)
            return e.activation(out=gtmp[u][:, 12:24], in_=gtmp[u][:, 12:24], func=AF.Exp)
        b.s(gs1, reads=(f"gt{u}",), writes=(f"gt{u}", f"go{u}"), tag=f"gs1_{i}")

        def gv2(e, u=u):
            e.tensor_tensor(gtmp[u][:, 24:28], gtmp[u][:, 12:16], gtmp[u][:, 16:20], ALU.add)
            e.tensor_tensor(gtmp[u][:, 24:28], gtmp[u][:, 24:28], gtmp[u][:, 20:24], ALU.add)
            e.reciprocal(gtmp[u][:, 24:28], gtmp[u][:, 24:28])
            e.tensor_tensor(gout[u][:, 8:12], gtmp[u][:, 12:16], gtmp[u][:, 24:28], ALU.mult)
            e.tensor_tensor(gout[u][:, 12:16], gtmp[u][:, 16:20], gtmp[u][:, 24:28], ALU.mult)
            return e.tensor_tensor(gout[u][:, 16:20], gtmp[u][:, 20:24], gtmp[u][:, 24:28], ALU.mult)
        b.v(gv2, reads=(f"gt{u}",), writes=(f"go{u}", f"gt{u}"), tag=f"gv2_{i}")
        b.gdma(lambda e, i=i, u=u: e.dma_start(out=gates[i * 128:(i + 1) * 128, :], in_=gout[u][:]),
               reads=(f"go{u}",), writes=(), tag=f"gst{i}")

    for ci, c in enumerate("qkv"):
        W = Wqkv[c]
        dst_raw = raw[c] if c in "qk" else out_fm["v"]
        WK = tuple(f"w{dc}" for dc in range(8))
        for dc in range(8):
            b.gdma(lambda e, dc=dc, W=W: e.dma_start(out=wsb[dc][:], in_=W[dc * 128:(dc + 1) * 128, :]),
                   writes=(f"w{dc}",), tag=f"wl{c}{dc}")
        for fc in range(8):
            def pmm(e, fc=fc):
                r = None
                for dc in range(8):
                    for tt in range(8):
                        r = e.matmul(pj[tt][:], wsb[dc][:, fc * 128:(fc + 1) * 128],
                                     xn_fm[dc][:, tt * 512:(tt + 1) * 512],
                                     start=(dc == 0), stop=(dc == 7))
                return r
            b.t(pmm, reads=WK + XNFM, writes=PS, tag=f"pmm{c}{fc}")

            def pev(e):
                r = None
                for tt in range(8):
                    off = (3 + tt * 512) if tt < 4 else (L + 6 + (tt - 4) * 512)
                    r = e.tensor_copy(stage[:, off:off + 512], pj[tt][:])
                return r
            b.v(pev, reads=PS, writes=("stage",) + PS, tag=f"pev{c}{fc}")

            def mkconv(sq):
                def conv(e, ci=ci, fc=fc, sq=sq):
                    base = ci * 32 + fc * 4
                    so = sq * (L + 3)
                    oo = sq * L
                    e.tensor_scalar(conv_o[:, oo:oo + L], stage[:, so:so + L],
                                    cw[:, base:base + 1], None, ALU.mult)
                    r = None
                    for j in (1, 2, 3):
                        r = e.scalar_tensor_tensor(conv_o[:, oo:oo + L], stage[:, so + j:so + j + L],
                                                   cw[:, base + j:base + j + 1], conv_o[:, oo:oo + L],
                                                   ALU.mult, ALU.add)
                    return r
                return conv
            b.v(mkconv(0), reads=("stage", "cw", "stage_pad"), writes=("convo",), tag=f"conv{c}{fc}a")
            b.v(mkconv(1), reads=("stage", "cw", "stage_pad"), writes=("convo2",), tag=f"conv{c}{fc}b")
            b.s(lambda e: e.activation(out=silu_o[:], in_=conv_o[:], func=AF.Silu),
                reads=("convo", "convo2"), writes=("siluo",), tag=f"silu{c}{fc}")
            b.gdma(lambda e, fc=fc, dst_raw=dst_raw, c=c: e.dma_start(
                out=dst_raw[fc * 128:(fc + 1) * 128, :], in_=silu_o[:]),
                reads=("siluo",), writes=(f"raw{c}{fc}",), tag=f"rst{c}{fc}")

    for c in "qk":
        for h in range(4):
            b.ydma(lambda e, c=c, h=h: e.dma_start(out=qa[:], in_=raw[c][h * 256:h * 256 + 128, :]),
                   reads=(f"raw{c}{2 * h}", "qaw"), writes=("qa",), tag=f"l2ld{c}{h}a")
            b.ydma(lambda e, c=c, h=h: e.dma_start(out=qb[:], in_=raw[c][h * 256 + 128:h * 256 + 256, :]),
                   reads=(f"raw{c}{2 * h + 1}", "qbw"), writes=("qb",), tag=f"l2ld{c}{h}b")
            b.s(lambda e: e.activation(out=sqa[:], in_=qa[:], func=AF.Square),
                reads=("qa",), writes=("sqa",), tag=f"l2sq{c}{h}a")
            b.s(lambda e: e.activation(out=sqb[:], in_=qb[:], func=AF.Square),
                reads=("qb",), writes=("sqb",), tag=f"l2sq{c}{h}b")

            def ssmm(e):
                r = None
                for tt in range(8):
                    e.matmul(pj[tt][0:1, :], onesc[:], sqa[:, tt * 512:(tt + 1) * 512],
                             start=True, stop=False)
                    r = e.matmul(pj[tt][0:1, :], onesc[:], sqb[:, tt * 512:(tt + 1) * 512],
                                 start=False, stop=True)
                return r
            b.t(ssmm, reads=("sqa", "sqb", "consts"), writes=PS, tag=f"l2ss{c}{h}")

            def epsc(e):
                r = None
                for tt in range(8):
                    r = e.tensor_scalar(rr[0:1, tt * 512:(tt + 1) * 512], pj[tt][0:1, :],
                                        1.0, 1e-12, ALU.mult, ALU.add)
                return r
            b.v(epsc, reads=PS + ("convo",), writes=("rr", "convo") + PS, tag=f"l2eps{c}{h}")

            def rstd2(e):
                e.activation(out=rr[0:1, :], in_=rr[0:1, :], func=AF.Ln)
                e.activation(out=rr[0:1, :], in_=rr[0:1, :], func=AF.Exp, scale=-0.5)
                return e.copy(rstd_bf[0:1, :], rr[0:1, :])
            b.s(rstd2, reads=("rr", "siluo"), writes=("rstdbf", "rr", "siluo"), tag=f"l2r{c}{h}")

            def bcmm(e):
                r = None
                for tt in range(8):
                    r = e.matmul(pj[tt][:], ones_row[:], rstd_bf[0:1, tt * 512:(tt + 1) * 512],
                                 start=True, stop=True)
                return r
            b.t(bcmm, reads=("rstdbf", "consts"), writes=PS, tag=f"l2bc{c}{h}")

            def l2ap(e):
                r = None
                for tt in range(8):
                    sl = slice(tt * 512, (tt + 1) * 512)
                    e.tensor_tensor(outn[:, sl], qa[:, sl], pj[tt][:], ALU.mult)
                    r = e.tensor_tensor(outnB[:, sl], qb[:, sl], pj[tt][:], ALU.mult)
                return r
            b.v(l2ap, reads=PS + ("qa", "qb"), writes=("outn", "outnB") + PS, tag=f"l2ap{c}{h}")
            b.gdma(lambda e, c=c, h=h: e.dma_start(out=out_fm[c][h * 256:h * 256 + 128, :], in_=outn[:]),
                   reads=("outn",), writes=("qaw",), tag=f"l2st{c}{h}a")
            b.gdma(lambda e, c=c, h=h: e.dma_start(out=out_fm[c][h * 256 + 128:h * 256 + 256, :], in_=outnB[:]),
                   reads=("outnB",), writes=("qbw",), tag=f"l2st{c}{h}b")

    es.close()




def emit_l2(nc, b, dr, bhs=None):
    _PFX[0] = "b_"
    qf = dr("qf", [D, T], BF16, "ExternalInput")
    kf = dr("kf", [D, T], BF16, "ExternalInput")
    vf = dr("vf", [D, T], BF16, "ExternalInput")
    gates = dr("gates", [T, 20], F32, "ExternalInput")
    maskBD = dr("maskBD", [128, 128], F32, "ExternalInput")
    maskA = dr("maskA", [64, 256], F32, "ExternalInput")
    ident_f = dr("ident_f", [128, 128], F32, "ExternalInput")
    ident_b = dr("ident_b", [128, 128], BF16, "ExternalInput")
    nw_bc = dr("nw_bc", [128, 256], F32, "ExternalInput")
    o_n = dr("o_n", [T, D], BF16, "ExternalOutput")
    ktokd = dr("ktokd", [L, 256], BF16, "Internal")

    es = ExitStack()
    NSC = L // 128

    g128 = sbuf(es, nc, "g128", [128, NB * 20], F32)
    mBD = sbuf(es, nc, "mBD", [128, 128], F32)
    mA = sbuf(es, nc, "mA", [64, 256], F32)
    g64 = sbuf(es, nc, "g64", [64, (T // 64) * 20], F32)
    i_f = sbuf(es, nc, "i_f", [128, 128], F32)
    i_b = sbuf(es, nc, "i_b", [128, 128], BF16)
    nwbc = sbuf(es, nc, "nwbc", [128, 256], F32)


    for nm, dst, srcap in [("mBD", mBD, maskBD), ("mA", mA, maskA), ("i_f", i_f, ident_f),
                           ("i_b", i_b, ident_b), ("nwbc", nwbc, nw_bc)]:
        b.ydma(lambda e, dst=dst, srcap=srcap: e.dma_start(out=dst[:], in_=srcap[:, :]),
               writes=("consts",), tag=f"ld{nm}")
    b.ydma(lambda e: e.dma_start(out=g128.rearrange("p (tb c) -> p tb c", c=20)[:, :, :],
                                 in_=gates.rearrange("(tb p) c -> p tb c", p=128)[:, :, :]),
           writes=("consts",), tag="ldg128")
    b.ydma(lambda e: e.dma_start(out=g64.rearrange("q (tp c) -> q tp c", c=20)[:, :, :],
                                 in_=gates.rearrange("(tp q) c -> q tp c", q=64)[:, :, :]),
           writes=("consts",), tag="ldg64")

    if bhs is None:
        bhs = [(bi, h) for bi in range(BLOC) for h in range(4)]

    ktokd2 = [dr(f"ktokd{X}", [L, 256], BF16, "Internal") for X in range(2)]
    ST = []
    for X in range(2):
        s = {}
        s["q_fm"] = [sbuf(es, nc, f"q_fm{X}{d}", [128, L], BF16) for d in range(2)]
        s["k_fm"] = [sbuf(es, nc, f"k_fm{X}{d}", [128, L], BF16) for d in range(2)]
        s["k_tok"] = sbuf(es, nc, f"k_tok{X}", [128, 16 * 256], BF16)
        s["v_tok"] = sbuf(es, nc, f"v_tok{X}", [128, 16 * 256], BF16)
        s["k_ctokP"] = [sbuf(es, nc, f"kctokP{X}{p}", [64, 16 * 256], BF16) for p in range(2)]
        s["S"] = {sn: [sbuf(es, nc, f"S{X}{sn}{d}", [128, 256], F32) for d in range(2)] for sn in "lg"}
        s["Sb"] = {sn: [sbuf(es, nc, f"Sb{X}{sn}{d}", [128, 256], BF16) for d in range(2)] for sn in "lg"}
        s["kb"] = {sn: sbuf(es, nc, f"kb{X}{sn}", [128, 256], BF16) for sn in "lg"}
        s["vb"] = {sn: sbuf(es, nc, f"vb{X}{sn}", [128, 256], BF16) for sn in "lg"}
        s["Nf"] = {sn: sbuf(es, nc, f"Nf{X}{sn}", [128, 128], F32) for sn in "lg"}
        s["Nbs"] = {sn: sbuf(es, nc, f"Nbs{X}{sn}", [128, 128], BF16) for sn in "lg"}
        s["Xb"] = {sn: [sbuf(es, nc, f"Xb{X}{sn}{k}", [128, 128], BF16) for k in range(5)] for sn in "lg"}
        s["Xtb"] = {sn: [sbuf(es, nc, f"Xtb{X}{sn}{k}", [128, 128], BF16) for k in range(4)] for sn in "lg"}
        s["Rb"] = {sn: [sbuf(es, nc, f"Rb{X}{sn}{k}", [128, 128], BF16) for k in range(5)] for sn in "lg"}
        s["wmn"] = {sn: [sbuf(es, nc, f"wmn{X}{sn}{d}", [128, 128], BF16) for d in range(2)] for sn in "lg"}
        s["KWnP"] = {sn: sbuf(es, nc, f"KWnP{X}{sn}", [64, 256], BF16) for sn in "lg"}
        s["aTP"] = sbuf(es, nc, f"aTP{X}", [64, 256], BF16)
        s["uP"] = {sn: [sbuf(es, nc, f"uP{X}{sn}{p}", [64, 256], BF16) for p in range(2)] for sn in "lg"}
        s["o_sb"] = sbuf(es, nc, f"o_sb{X}", [128, 256], F32)
        s["o_bf"] = sbuf(es, nc, f"o_bf{X}", [128, 256], BF16)
        s["rstt"] = sbuf(es, nc, f"rstt{X}", [128, 4], F32)
        s["SQ"] = psumt(es, nc, f"sqb{X}", [128, 256], F32)
        s["RS"] = psumt(es, nc, f"rsb{X}", [128, 256], F32)
        s["po"] = psumt(es, nc, f"po{X}", [128, 256], F32)
        s["pT"] = psumt(es, nc, f"pTb{X}", [128, 128], BF16)
        ST.append(s)

    def bh_gen(bi, h, X):
        B = ST[X]
        q_fm, k_fm = B["q_fm"], B["k_fm"]
        k_tok, v_tok, k_ctokP = B["k_tok"], B["v_tok"], B["k_ctokP"]
        S, Sb, kb, vb = B["S"], B["Sb"], B["kb"], B["vb"]
        Nf, Nbs, Xb, Xtb, Rb = B["Nf"], B["Nbs"], B["Xb"], B["Xtb"], B["Rb"]
        wmn, KWnP, aTP, uP = B["wmn"], B["KWnP"], B["aTP"], B["uP"]
        o_sb, o_bf, rstt = B["o_sb"], B["o_bf"], B["rstt"]
        ktokd = ktokd2[X]
        SQb, RSb, po, pT = B["SQ"], B["RS"], B["po"], B["pT"]
        SQK, RSK, POK, PTK = f"sq{X}", f"rs{X}", f"po{X}", f"pT{X}"
        pbk = {"l": [SQb, RSb], "g": [SQb, RSb]}
        pbk_k = {"l": (SQK, RSK), "g": (SQK, RSK)}
        bh = f"{bi}{h}"
        cS = bi * L
        r0 = 256 * h
        KFM = (f"kfm{X}0", f"kfm{X}1")
        QFM = (f"qfm{X}0", f"qfm{X}1")
        UK = {sn: (f"uP{X}{sn}0", f"uP{X}{sn}1") for sn in "lg"}
        KT, KTX = f"ktok{X}", f"ktokX{X}"
        VT, VTX = f"vtok{X}", f"vtokX{X}"
        KC, KD = f"kctok{X}", f"ktokd{X}"
        ATK = f"aTP{X}"
        for d in range(2):
            b.ydma(lambda e, d=d, cS=cS, r0=r0: e.dma_start(
                out=q_fm[d][:], in_=qf[r0 + d * 128:r0 + (d + 1) * 128, cS:cS + L]),
                writes=(QFM[d],), tag=f"ldq{bh}{d}")
            yield
            b.ydma(lambda e, d=d, cS=cS, r0=r0: e.dma_start(
                out=k_fm[d][:], in_=kf[r0 + d * 128:r0 + (d + 1) * 128, cS:cS + L]),
                writes=(KFM[d],), tag=f"ldk{bh}{d}")
            yield
        for tt in range(16):
            b.ydma(lambda e, tt=tt, cS=cS, r0=r0: e.dma_start(
                out=k_tok[:, tt * 256:(tt + 1) * 256],
                in_=kf[r0:r0 + 256, cS + tt * 128:cS + (tt + 1) * 128], transpose=True),
                writes=(KT,) if tt == 0 else (KT, KTX), tag=f"ldkt{bh}{tt}")
            yield
            b.ydma(lambda e, tt=tt, cS=cS, r0=r0: e.dma_start(
                out=v_tok[:, tt * 256:(tt + 1) * 256],
                in_=vf[r0:r0 + 256, cS + tt * 128:cS + (tt + 1) * 128], transpose=True),
                writes=(VT,) if tt == 0 else (VT, VTX), tag=f"ldvt{bh}{tt}")
            yield
        b.gdma(lambda e: e.dma_start(
            out=ktokd.rearrange("(tt p) f -> p tt f", p=128)[:, :, :],
            in_=k_tok.rearrange("p (tt f) -> p tt f", f=256)[:, :, :]),
            reads=(KT, KTX), writes=(KD,), tag=f"stkt{bh}")
        yield
        for j in range(4):
            b.ydma(lambda e, j=j: e.dma_start(
                out=k_ctokP[j // 2][32 * (j % 2):32 * (j % 2) + 32, :].rearrange(
                    "p (sc f) -> p sc f", f=256)[:, :, :],
                in_=ktokd.rearrange("(sc jj p) f -> p jj sc f",
                                    jj=4, p=32)[:, j, :, :]),
                reads=(KD,), writes=(KC,), tag=f"ldkc{bh}{j}")
            yield

        for sn in "lg":
            for d in range(2):
                b.v(lambda e, sn=sn, d=d: (e.memset(S[sn][d][:], 0.0),
                                           e.memset(Sb[sn][d][:], 0.0))[-1],
                    writes=(f"S{X}{sn}{d}", f"Sb{X}{sn}{d}"), tag=f"zs{bh}{sn}{d}")
                yield

        for sc in range(NSC):
            t0 = sc * 128
            tb = bi * 16 + sc
            tc0 = bi * 64 + sc * 4

            def gmm(e, t0=t0):
                r = None
                for d in range(2):
                    r = e.matmul(pbk["l"][0][:, 0:128], k_fm[d][:, t0:t0 + 128],
                                 k_fm[d][:, t0:t0 + 128],
                                 start=(d == 0), stop=(d == 1), skip_group_check=True)
                return r
            b.t(gmm, reads=KFM, writes=(SQK,), tag=f"G{bh}{sc}")
            yield

            for sn in "lg":
                bcol = (0 if sn == "l" else 4) + h
                def nbuild(e, tb=tb, bcol=bcol, sn=sn):
                    e.tensor_tensor(Nf[sn][:], pbk["l"][0][:, 0:128], mBD[:], ALU.mult)
                    return e.tensor_scalar(Nbs[sn][:], Nf[sn][:],
                                           g128[:, tb * 20 + bcol:tb * 20 + bcol + 1],
                                           None, ALU.mult)
                b.v(nbuild, reads=(SQK, "consts"), writes=(f"Nbs{X}{sn}", f"Nf{X}{sn}", SQK),
                    tag=f"nb{bh}{sc}{sn}")
                yield

            def amm(e, t0=t0):
                r = None
                for p in range(2):
                    for d in range(2):
                        r = e.matmul(pbk["g"][1][0:64, 128 * p:128 * (p + 1)],
                                     k_fm[d][:, t0 + 64 * p:t0 + 64 * (p + 1)],
                                     q_fm[d][:, t0:t0 + 128],
                                     start=(d == 0), stop=(d == 1), skip_group_check=True)
                return r
            b.t(amm, reads=KFM + QFM, writes=(RSK,), tag=f"amm{bh}{sc}")
            yield
            b.v(lambda e: e.tensor_tensor(aTP[:], pbk["g"][1][0:64, 0:256],
                                          mA[:], ALU.mult),
                reads=(RSK, "consts"), writes=(ATK, RSK), tag=f"aev{bh}{sc}")
            yield

            for sn in "lg":
                b.t(lambda e, sn=sn: e.transpose(pT[:], Nbs[sn][:], i_b[:]),
                    reads=(f"Nbs{X}{sn}", "consts"), writes=(PTK,), tag=f"ntr{bh}{sc}{sn}")
                yield
                def mset(e, sn=sn):
                    e.tensor_copy(Xtb[sn][0][:], pT[:])
                    return e.tensor_tensor(Rb[sn][0][:], pT[:], i_f[:], ALU.add)
                b.v(mset, reads=(PTK, "consts"), writes=(f"Xtb{X}{sn}0", f"Rb{X}{sn}0", PTK),
                    tag=f"mset{bh}{sc}{sn}")
                yield
                b.v(lambda e, sn=sn: e.tensor_copy(Xb[sn][0][:], Nbs[sn][:]),
                    reads=(f"Nbs{X}{sn}",), writes=(f"Xb{X}{sn}0",), tag=f"xcp{bh}{sc}{sn}")
                yield

            for k in range(1, 5):
                for sn in "lg":
                    SQ, RS = pbk_k[sn]
                    sqb = pbk[sn][0]
                    rsb = pbk[sn][1]
                    b.t(lambda e, k=k, sn=sn, sqb=sqb: e.matmul(
                        sqb[:, 0:128], Xtb[sn][k - 1][:], Xb[sn][k - 1][:],
                        start=True, stop=True, skip_group_check=True),
                        reads=(f"Xtb{X}{sn}{k-1}", f"Xb{X}{sn}{k-1}"), writes=(SQ,),
                        tag=f"sqX{bh}{sc}{sn}{k}")
                    yield
                    b.s(lambda e, k=k, sn=sn, sqb=sqb: e.copy(Xb[sn][k][:], sqb[:, 0:128]),
                        reads=(SQ,), writes=(f"Xb{X}{sn}{k}", SQ), tag=f"sqXe{bh}{sc}{sn}{k}")
                    yield
                    if k < 4:
                        b.t(lambda e, k=k, sn=sn, sqb=sqb: e.matmul(
                            sqb[:, 0:128], Xb[sn][k - 1][:], Xtb[sn][k - 1][:],
                            start=True, stop=True, skip_group_check=True),
                            reads=(f"Xtb{X}{sn}{k-1}", f"Xb{X}{sn}{k-1}"), writes=(SQ,),
                            tag=f"sqT{bh}{sc}{sn}{k}")
                        yield
                        b.v(lambda e, k=k, sn=sn, sqb=sqb: e.tensor_copy(Xtb[sn][k][:],
                                                                         sqb[:, 0:128]),
                            reads=(SQ,), writes=(f"Xtb{X}{sn}{k}", SQ), tag=f"sqTe{bh}{sc}{sn}{k}")
                        yield
                    def rstep(e, k=k, sn=sn, rsb=rsb):
                        e.matmul(rsb[:, 0:128], Xb[sn][k][:], Rb[sn][k - 1][:],
                                 start=True, stop=False, skip_group_check=True)
                        return e.matmul(rsb[:, 0:128], i_b[:], Rb[sn][k - 1][:],
                                        start=False, stop=True, skip_group_check=True)
                    b.t(rstep, reads=(f"Xb{X}{sn}{k}", f"Rb{X}{sn}{k-1}", "consts"), writes=(RS,),
                        tag=f"rs{bh}{sc}{sn}{k}")
                    yield
                    b.v(lambda e, k=k, sn=sn, rsb=rsb: e.tensor_copy(Rb[sn][k][:], rsb[:, 0:128]),
                        reads=(RS,), writes=(f"Rb{X}{sn}{k}", RS), tag=f"rse{bh}{sc}{sn}{k}")
                    yield

            for sn in "lg":
                bcol = (0 if sn == "l" else 4) + h
                def kvb(e, sc=sc, tb=tb, bcol=bcol, sn=sn):
                    e.activation(out=kb[sn][:], in_=k_tok[:, sc * 256:(sc + 1) * 256],
                                 func=AF.Copy,
                                 scale=g128[:, tb * 20 + bcol:tb * 20 + bcol + 1])
                    return e.activation(out=vb[sn][:], in_=v_tok[:, sc * 256:(sc + 1) * 256],
                                        func=AF.Copy,
                                        scale=g128[:, tb * 20 + bcol:tb * 20 + bcol + 1])
                b.s(kvb, reads=(KT, VT, "consts"), writes=(f"kb{X}{sn}", f"vb{X}{sn}"),
                    tag=f"kvb{bh}{sc}{sn}")
                yield
            for sn in "lg":
                SQ, RS = pbk_k[sn]
                for d in range(2):
                    b.t(lambda e, d=d, sn=sn: e.matmul(pbk[sn][0][:, 0:128],
                                                       kb[sn][:, d * 128:(d + 1) * 128],
                                                       Rb[sn][4][:], start=True, stop=True,
                                                       skip_group_check=True),
                        reads=(f"kb{X}{sn}", f"Rb{X}{sn}4"), writes=(SQ,), tag=f"wm{bh}{sc}{sn}{d}")
                    yield
                    b.s(lambda e, d=d, sn=sn: e.mul(wmn[sn][d][:],
                                                    pbk[sn][0][:, 0:128], -1.0),
                        reads=(SQ,), writes=(f"wmn{X}{sn}{d}", SQ), tag=f"wme{bh}{sc}{sn}{d}")
                    yield
                def kwmm(e, t0=t0, sn=sn):
                    r = None
                    for d in range(2):
                        r = e.matmul(pbk[sn][1][0:64, 0:128],
                                     k_fm[d][:, t0:t0 + 64],
                                     wmn[sn][d][:], start=(d == 0), stop=(d == 1),
                                     skip_group_check=True)
                    for d in range(2):
                        r = e.matmul(pbk[sn][1][0:32, 128:256],
                                     k_fm[d][:, t0 + 64:t0 + 96],
                                     wmn[sn][d][:], start=(d == 0), stop=(d == 1),
                                     skip_group_check=True)
                    return r
                b.t(kwmm, reads=KFM + (f"wmn{X}{sn}0", f"wmn{X}{sn}1"), writes=(RS,),
                    tag=f"kw{bh}{sc}{sn}")
                yield
                b.s(lambda e, sn=sn: e.copy(KWnP[sn][:], pbk[sn][1][0:64, 0:256]),
                    reads=(RS,), writes=(f"KWnP{X}{sn}", RS), tag=f"kwe{bh}{sc}{sn}")
                yield

            for ci in range(4):
                for sn in "lg":
                    wcol = (8 if sn == "l" else 12) + h
                    SBK = (f"Sb{X}{sn}0", f"Sb{X}{sn}1")
                    bank = pbk[sn][ci % 2]
                    BK = pbk_k[sn][ci % 2]
                    r32 = slice(32 * (ci % 2), 32 * (ci % 2) + 32)
                    def umm(e, ci=ci, sn=sn, bank=bank, r32=r32):
                        r = e.matmul(bank[r32, 0:256], Rb[sn][4][:, 32 * ci:32 * (ci + 1)],
                                     vb[sn][:], start=True, stop=False, skip_group_check=True)
                        for d in range(2):
                            r = e.matmul(bank[r32, 0:256], wmn[sn][d][:, 32 * ci:32 * (ci + 1)],
                                         Sb[sn][d][:], start=False,
                                         stop=(ci == 0 and d == 1), skip_group_check=True)
                        return r
                    b.t(umm, reads=(f"Rb{X}{sn}4", f"vb{X}{sn}", f"wmn{X}{sn}0", f"wmn{X}{sn}1") + SBK,
                        writes=(BK,), tag=f"umm{bh}{sc}{sn}{ci}")
                    yield
                    if ci > 0:
                        def corr(e, ci=ci, sn=sn, bank=bank, r32=r32):
                            np0 = min(ci, 2)
                            r = e.matmul(bank[r32, 0:256],
                                         KWnP[sn][0:32 * np0, 32 * ci:32 * (ci + 1)],
                                         uP[sn][0][0:32 * np0, :],
                                         start=False, stop=(ci < 3),
                                         skip_group_check=True)
                            if ci == 3:
                                r = e.matmul(bank[r32, 0:256],
                                             KWnP[sn][0:32, 128 + 96:128 + 128],
                                             uP[sn][1][0:32, :],
                                             start=False, stop=True,
                                             skip_group_check=True)
                            return r
                        b.t(corr, reads=(f"KWnP{X}{sn}", f"uP{X}{sn}0", f"uP{X}{sn}1"),
                            writes=(BK,), tag=f"corr{bh}{sc}{sn}{ci}")
                        yield
                    tp = (tc0 + ci) // 2
                    b.s(lambda e, ci=ci, tp=tp, wcol=wcol, sn=sn, bank=bank, r32=r32:
                        e.activation(
                        out=uP[sn][ci // 2][r32, :], in_=bank[r32, 0:256], func=AF.Copy,
                        scale=g64[r32, tp * 20 + wcol:tp * 20 + wcol + 1]),
                        reads=(BK, "consts"), writes=(f"uP{X}{sn}{ci // 2}", BK),
                        tag=f"uev{bh}{sc}{sn}{ci}")
                    yield

            for sn in "lg":
                SBK = (f"Sb{X}{sn}0", f"Sb{X}{sn}1")
                def oqs(e, t0=t0, sn=sn):
                    r = None
                    for d in range(2):
                        r = e.matmul(po[:], q_fm[d][:, t0:t0 + 128], Sb[sn][d][:],
                                     start=(sn == "l" and d == 0), stop=False,
                                     skip_group_check=True)
                    return r
                b.t(oqs, reads=QFM + SBK, writes=(POK,), tag=f"oqs{bh}{sc}{sn}")
                yield
                def oau(e, sn=sn):
                    r = None
                    for p in range(2):
                        r = e.matmul(po[:], aTP[:, 128 * p:128 * (p + 1)],
                                     uP[sn][p][:],
                                     start=False,
                                     stop=(sn == "g" and p == 1), skip_group_check=True)
                    return r
                b.t(oau, reads=(ATK,) + UK[sn],
                    writes=(POK,), tag=f"oau{bh}{sc}{sn}")
                yield
                for d in range(2):
                    bank = pbk[sn][d]
                    BK = pbk_k[sn][d]
                    def smm(e, d=d, sc=sc, sn=sn, bank=bank):
                        r = None
                        for p in range(2):
                            r = e.matmul(bank[:, 0:256],
                                         k_ctokP[p][:, sc * 256 + d * 128:
                                                     sc * 256 + (d + 1) * 128],
                                         uP[sn][p][:],
                                         start=(p == 0), stop=(p == 1),
                                         skip_group_check=True)
                        return r
                    b.t(smm, reads=(KC,) + UK[sn], writes=(BK,), tag=f"smm{bh}{sc}{sn}{d}")
                    yield
                    def supd(e, d=d, sn=sn, bank=bank):
                        e.tensor_tensor(S[sn][d][:], S[sn][d][:], bank[:, 0:256], ALU.add)
                        return e.tensor_copy(Sb[sn][d][:], S[sn][d][:])
                    b.v(supd, reads=(BK, f"S{X}{sn}{d}"),
                        writes=(f"S{X}{sn}{d}", f"Sb{X}{sn}{d}", BK), tag=f"sup{bh}{sc}{sn}{d}")
                    yield

            def oev(e, sc=sc, tb=tb, h=h):
                return e.scalar_tensor_tensor(o_sb[:], v_tok[:, sc * 256:(sc + 1) * 256],
                                              g128[:, tb * 20 + 16 + h:tb * 20 + 16 + h + 1],
                                              po[:], ALU.mult, ALU.add)
            b.v(oev, reads=(POK, VT, "consts"), writes=(f"o_sb{X}", POK), tag=f"oev{bh}{sc}")
            yield
            b.s(lambda e: e.activation(out=o_bf[:], in_=o_sb[:], func=AF.Square,
                                       accum_out=rstt[:, 0:1]),
                reads=(f"o_sb{X}",), writes=(f"rs{X}", f"o_bf{X}"), tag=f"osq{bh}{sc}")
            yield
            b.v(lambda e: e.tensor_scalar(rstt[:, 1:2], rstt[:, 0:1], 1.0 / 256, 1e-5,
                                          ALU.mult, ALU.add),
                reads=(f"rs{X}",), writes=(f"rs{X}",), tag=f"oeps{bh}{sc}")
            yield
            def orst(e):
                e.activation(out=rstt[:, 2:3], in_=rstt[:, 1:2], func=AF.Ln)
                return e.activation(out=rstt[:, 2:3], in_=rstt[:, 2:3], func=AF.Exp, scale=-0.5)
            b.s(orst, reads=(f"rs{X}",), writes=(f"rs{X}",), tag=f"orst{bh}{sc}")
            yield
            def oap(e):
                e.tensor_scalar(o_sb[:], o_sb[:], rstt[:, 2:3], None, ALU.mult)
                return e.tensor_tensor(o_bf[:], o_sb[:], nwbc[:], ALU.mult)
            b.v(oap, reads=(f"o_sb{X}", f"rs{X}", "consts"), writes=(f"o_bf{X}", f"o_sb{X}"),
                tag=f"oap{bh}{sc}")
            yield
            b.gdma(lambda e, bi=bi, sc=sc, h=h: e.dma_start(
                out=o_n[bi * L + sc * 128:bi * L + (sc + 1) * 128, 256 * h:256 * (h + 1)],
                in_=o_bf[:]),
                reads=(f"o_bf{X}",), writes=(), tag=f"ost{bh}{sc}")
            yield

    for pp in range(0, len(bhs), 2):
        gens = [bh_gen(*bhs[pp], 0)]
        if pp + 1 < len(bhs):
            gens.append(bh_gen(*bhs[pp + 1], 1))
        alive = list(gens)
        while alive:
            nxt = []
            for g in alive:
                try:
                    next(g)
                    nxt.append(g)
                except StopIteration:
                    pass
            alive = nxt

    es.close()


def prep_l2_consts():
    i32 = np.arange(32)
    i128 = np.arange(128)
    mBD = np.zeros((128, 128), np.float32)
    for c in range(4):
        blk = (i128[:, None] // 32 == c) & (i128[None, :] // 32 == c)
        mBD = np.where(blk & (i128[:, None] > i128[None, :]), -1.0, mBD)
    i64 = np.arange(64)
    mA = np.concatenate([(64 * p + i64[:, None] <= i128[None, :]) for p in range(2)],
                        axis=1).astype(np.float32)
    return {
        "maskBD": mBD,
        "maskA": mA,
        "ident_f": np.eye(128, dtype=np.float32),
        "ident_b": np.eye(128, dtype=np.float32).astype(BF),
    }




def emit_l3(nc, b, dr):
    _PFX[0] = "c_"
    o_nd = dr("o_n", [T, D], BF16, "ExternalInput")
    xn0_tok = dr("xn0_tok", [T, D], BF16, "ExternalInput")
    x0 = dr("x0", [T, D], F32, "ExternalInput")
    Wo_at = dr("Wo_at", [D, D], F32, "ExternalInput")
    WgT = dr("WgT", [D, 8], F32, "ExternalInput")
    W1T = dr("W1T", [8, D, 4096], F32, "ExternalInput")
    b1l = dr("b1l", [128, 256], F32, "ExternalInput")
    WoTm = dr("WoTm", [2048, D], F32, "ExternalInput")
    bo_bc = dr("bo_bc", [128, D], BF16, "ExternalInput")
    x1d = dr("x1", [T, D], F32, "ExternalOutput")
    xn1_tok = dr("xn1_tok", [T, D], BF16, "ExternalOutput")
    s0d = dr("s0d", [T, D], F32, "Internal")

    es = ExitStack()

    slab = [sbuf(es, nc, f"slab{d}", [128, 4096], BF16) for d in range(8)]
    mslab = [sbuf(es, nc, f"mslab{d}", [128, 1024], BF16) for d in range(16)]
    acc = sbuf(es, nc, "acc", [128, NB * 1024], BF16)
    gsb = [sbuf(es, nc, f"gsb{i}", [128, 512], BF16) for i in range(16)]
    xnts = [[sbuf(es, nc, f"xnts{p}{d}", [128, 512], BF16) for d in range(8)] for p in range(2)]
    wgt = sbuf(es, nc, "wgt", [128, 64], BF16)
    b1sb = sbuf(es, nc, "b1sb", [128, 256], F32)
    bobc = sbuf(es, nc, "bobc", [128, D], BF16)
    comb = sbuf(es, nc, "comb", [128, NB * 8], F32)
    combs = sbuf(es, nc, "combs", [128, NB], F32)
    ctmp = sbuf(es, nc, "ctmp", [128, 24], F32)
    sgt = [sbuf(es, nc, f"sgt{i}", [128, 512], BF16) for i in range(3)]
    _f0 = slab[0].bitcast(F32)
    _f1 = slab[1].bitcast(F32)
    _f2 = slab[2].bitcast(F32)
    xt = [_f0[:, 0:1024], _f0[:, 1024:2048]]
    st0 = [_f1[:, 0:1024], _f1[:, 1024:2048]]
    evt = sbuf(es, nc, "evt", [128, 1024], F32)
    ev = [evt[:, 0:512], evt[:, 512:1024]]
    g0t = _f2[:, 0:1024]
    x1t = _f2[:, 1024:2048]
    stats = sbuf(es, nc, "stats", [128, 8], F32)
    xnt = sbuf(es, nc, "xnt", [128, D], BF16)
    sqscr = g0t

    pj = psum8(es, nc)

    for d in range(8):
        b.ydma(lambda e, d=d: e.dma_start(out=slab[d][:, 0:T], in_=o_nd[:, d * 128:(d + 1) * 128],
                                          transpose=True),
               writes=(f"slab{d}",), tag=f"ldofm{d}")
        b.gdma(lambda e, d=d: e.dma_start(out=mslab[d][:], in_=Wo_at[d * 128:(d + 1) * 128, :]),
               writes=(f"mslab{d}",), tag=f"ldwo{d}")
    for tb in range(NB):
        u = tb % 2
        PK = (PS[2 * u], PS[2 * u + 1])
        def smm(e, tb=tb, u=u):
            r = None
            for d in range(8):
                for oc in range(2):
                    r = e.matmul(pj[2 * u + oc][:], slab[d][:, tb * 128:(tb + 1) * 128],
                                 mslab[d][:, oc * 512:(oc + 1) * 512],
                                 start=(d == 0), stop=(d == 7), skip_group_check=True)
            return r
        b.t(smm, reads=tuple(f"slab{d}" for d in range(8)) + tuple(f"mslab{d}" for d in range(8)),
            writes=PK, tag=f"smm{tb}")
        def sev(e, u=u):
            e.tensor_copy(ev[0][:, :], pj[2 * u][:])
            return e.tensor_copy(ev[1][:, :], pj[2 * u + 1][:])
        b.v(sev, reads=PK, writes=("ev0", "ev1") + PK, tag=f"sev{tb}")
        b.gdma(lambda e, tb=tb: e.dma_start(out=s0d[tb * 128:(tb + 1) * 128, 0:512], in_=ev[0][:, :]),
               reads=("ev0",), writes=("ev0w",), tag=f"sst{tb}a")
        b.gdma(lambda e, tb=tb: e.dma_start(out=s0d[tb * 128:(tb + 1) * 128, 512:1024], in_=ev[1][:, :]),
               reads=("ev1",), writes=("ev1w",), tag=f"sst{tb}b")

    b.gdma(lambda e: e.dma_start(out=wgt.rearrange("p (c f) -> p c f", f=8)[:, :, :],
                                 in_=WgT.rearrange("(c p) f -> p c f", p=128)[:, :, :]),
           writes=("wgt",), tag="ldwgt")
    b.gdma(lambda e: e.dma_start(out=b1sb[:], in_=b1l[:, :]), writes=("b1sb",), tag="ldb1")
    b.gdma(lambda e: e.dma_start(out=bobc[:], in_=bo_bc[:, :]), writes=("bobc",), tag="ldbo")
    b.v(lambda e: e.memset(acc[:], 0.0), writes=tuple(f"acc{t}" for t in range(NB)), tag="zacc")

    for ts in range(8):
        p = ts % 2
        for d in range(8):
            b.ydma(lambda e, ts=ts, d=d, p=p: e.dma_start(
                out=xnts[p][d][:], in_=xn0_tok[ts * 512:(ts + 1) * 512, d * 128:(d + 1) * 128],
                transpose=True),
                writes=(f"xnts{p}{d}",), tag=f"ldxn{ts}{d}")
        for tq in range(4):
            tb = ts * 4 + tq
            def rmm(e, tq=tq, p=p):
                r = None
                for d in range(8):
                    r = e.matmul(pj[4][:, 0:8], xnts[p][d][:, tq * 128:(tq + 1) * 128],
                                 wgt[:, d * 8:(d + 1) * 8], start=(d == 0), stop=(d == 7),
                                 skip_group_check=True)
                return r
            b.t(rmm, reads=tuple(f"xnts{p}{d}" for d in range(8)) + ("wgt",), writes=("ps4",),
                tag=f"rmm{tb}")
            def rsoft(e, tb=tb):
                e.reduce_max(ctmp[:, 0:1], pj[4][:, 0:8], axis=AX)
                e.tensor_scalar(ctmp[:, 8:16], pj[4][:, 0:8], ctmp[:, 0:1], None, ALU.subtract)
                return e.engine_nop()
            b.v(rsoft, reads=("ps4",), writes=("ct", "ps4"), tag=f"rsm{tb}")
            b.s(lambda e: e.activation(out=ctmp[:, 8:16], in_=ctmp[:, 8:16], func=AF.Exp,
                                       accum_out=ctmp[:, 1:2]),
                reads=("ct",), writes=("ct",), tag=f"rex{tb}")
            def rcomb(e, tb=tb):
                e.reciprocal(ctmp[:, 2:3], ctmp[:, 1:2])
                e.tensor_scalar(ctmp[:, 8:16], ctmp[:, 8:16], ctmp[:, 2:3], None, ALU.mult)
                e.reduce_max(ctmp[:, 3:4], ctmp[:, 8:16], axis=AX)
                e.tensor_scalar(ctmp[:, 16:24], ctmp[:, 8:16], ctmp[:, 3:4], None, ALU.is_lt)
                e.tensor_tensor(ctmp[:, 16:24], ctmp[:, 16:24], ctmp[:, 8:16], ALU.mult)
                e.reduce_max(ctmp[:, 4:5], ctmp[:, 16:24], axis=AX)
                e.tensor_tensor(ctmp[:, 5:6], ctmp[:, 3:4], ctmp[:, 4:5], ALU.add)
                e.tensor_scalar(ctmp[:, 5:6], ctmp[:, 5:6], 1.0, 1e-9, ALU.mult, ALU.add)
                e.reciprocal(ctmp[:, 5:6], ctmp[:, 5:6])
                e.tensor_scalar(ctmp[:, 16:24], ctmp[:, 8:16], ctmp[:, 4:5], None, ALU.is_ge)
                e.tensor_tensor(ctmp[:, 16:24], ctmp[:, 16:24], ctmp[:, 8:16], ALU.mult)
                e.tensor_scalar(comb[:, tb * 8:(tb + 1) * 8], ctmp[:, 16:24], ctmp[:, 5:6],
                                None, ALU.mult)
                return e.reduce_sum(combs[:, tb:tb + 1], comb[:, tb * 8:(tb + 1) * 8], axis=AX)
            b.v(rcomb, reads=("ct",), writes=(f"comb{tb}", "ct"), tag=f"rcb{tb}")


    for e_i in range(8):
        for d in range(8):
            b.gdma(lambda e, e_i=e_i, d=d: e.dma_start(out=slab[d][:],
                                                       in_=W1T[e_i, d * 128:(d + 1) * 128, :]),
                   writes=(f"slab{d}",), tag=f"lw1{e_i}{d}")
        for gc in range(16):
            b.gdma(lambda e, e_i=e_i, gc=gc: e.dma_start(out=mslab[gc][:],
                                                         in_=WoTm[gc * 128:(gc + 1) * 128, :]),
                   writes=(f"mslab{gc}",), tag=f"lwo{e_i}{gc}") if e_i == 0 else None
        for ts in range(8):
            p = ts % 2
            for d in range(8):
                b.ydma(lambda e, ts=ts, d=d, p=p: e.dma_start(
                    out=xnts[p][d][:], in_=xn0_tok[ts * 512:(ts + 1) * 512, d * 128:(d + 1) * 128],
                    transpose=True),
                    writes=(f"xnts{p}{d}",), tag=f"cxn{e_i}{ts}{d}")
            XN = tuple(f"xnts{p}{d}" for d in range(8))
            for gc in range(16):
                w = gc % 3
                ba, bb = pj[2 * w], pj[2 * w + 1]
                KA, KB = PS[2 * w], PS[2 * w + 1]

                def hmm(e, gc=gc, ba=ba, bb=bb, p=p):
                    r = None
                    for d in range(8):
                        e.matmul(ba[:], slab[d][:, gc * 128:(gc + 1) * 128], xnts[p][d][:],
                                 start=(d == 0), stop=(d == 7), skip_group_check=True)
                        r = e.matmul(bb[:], slab[d][:, (gc + 16) * 128:(gc + 17) * 128], xnts[p][d][:],
                                     start=(d == 0), stop=(d == 7), skip_group_check=True)
                    return r
                b.t(hmm, reads=tuple(f"slab{d}" for d in range(8)) + XN, writes=(KA, KB),
                    tag=f"hmm{e_i}{ts}{gc}")
                b.s(lambda e, e_i=e_i, gc=gc, bb=bb, w=w: e.activation(
                    out=sgt[w][:], in_=bb[:], func=AF.Sigmoid,
                    bias=b1sb[:, e_i * 32 + 16 + gc:e_i * 32 + 17 + gc]),
                    reads=(KB, "b1sb"), writes=(f"sgt{w}", KB), tag=f"sig{e_i}{ts}{gc}")
                b.v(lambda e, e_i=e_i, gc=gc, ba=ba, w=w: e.scalar_tensor_tensor(
                    gsb[gc][:], ba[:], b1sb[:, e_i * 32 + gc:e_i * 32 + gc + 1], sgt[w][:],
                    ALU.add, ALU.mult),
                    reads=(KA, f"sgt{w}", "b1sb"), writes=(f"gsb{gc}", KA), tag=f"glu{e_i}{ts}{gc}")
            for tq in range(4):
                tb = ts * 4 + tq
                for oc in range(2):
                    def omm(e, tq=tq, oc=oc):
                        r = None
                        for gc in range(16):
                            r = e.matmul(pj[tq * 2 + oc][:], gsb[gc][:, tq * 128:(tq + 1) * 128],
                                         mslab[gc][:, oc * 512:(oc + 1) * 512],
                                         start=(gc == 0), stop=(gc == 15), skip_group_check=True)
                        return r
                    b.t(omm, reads=tuple(f"gsb{gc}" for gc in range(16))
                        + tuple(f"mslab{gc}" for gc in range(16)),
                        writes=(PS[tq * 2 + oc],), tag=f"omm{e_i}{ts}{tq}{oc}")
                    def oev2(e, e_i=e_i, tb=tb, tq=tq, oc=oc):
                        sl = slice(tb * 1024 + oc * 512, tb * 1024 + (oc + 1) * 512)
                        return e.scalar_tensor_tensor(acc[:, sl], pj[tq * 2 + oc][:],
                                                      comb[:, tb * 8 + e_i:tb * 8 + e_i + 1],
                                                      acc[:, sl], ALU.mult, ALU.add)
                    b.v(oev2, reads=(PS[tq * 2 + oc], f"comb{tb}", f"acc{tb}"),
                        writes=(f"acc{tb}", PS[tq * 2 + oc]), tag=f"oev{e_i}{ts}{tq}{oc}")

    b.t(lambda e: e.nop(),
        reads=tuple(f"acc{t}" for t in range(NB)) + tuple(f"slab{d}" for d in range(8))
        + tuple(f"gsb{gc}" for gc in range(16)),
        writes=("phCdone",), tag="phCdone")
    for tb in range(NB):
        u = tb % 2
        def bo_add(e, tb=tb):
            return e.scalar_tensor_tensor(acc[:, tb * 1024:(tb + 1) * 1024], bobc[:],
                                          combs[:, tb:tb + 1], acc[:, tb * 1024:(tb + 1) * 1024],
                                          ALU.mult, ALU.add)
        b.v(bo_add, reads=("bobc", f"comb{tb}", f"acc{tb}"), writes=(f"acc{tb}",), tag=f"boa{tb}")
        b.s(lambda e, tb=tb: e.activation(out=g0t[:, :], in_=acc[:, tb * 1024:(tb + 1) * 1024],
                                          func=AF.Silu),
            reads=(f"acc{tb}", "phCdone"), writes=("g0t",), tag=f"sil{tb}")
        b.ydma(lambda e, tb=tb, u=u: e.dma_start(out=xt[u][:, :], in_=x0[tb * 128:(tb + 1) * 128, :]),
               reads=(f"xtw{u}", "phCdone"), writes=(f"xt{u}",), tag=f"ldx0{tb}")
        b.ydma(lambda e, tb=tb, u=u: e.dma_start(out=st0[u][:, :], in_=s0d[tb * 128:(tb + 1) * 128, :]),
               reads=(f"s0w{u}", "ev0w", "ev1w", "phCdone"), writes=(f"st0{u}",), tag=f"lds0{tb}")
        def comb2(e, u=u):
            e.tensor_tensor(g0t[:, :], g0t[:, :], st0[u][:, :], ALU.mult)
            return e.tensor_tensor(x1t[:, :], g0t[:, :], xt[u][:, :], ALU.add)
        b.v(comb2, reads=("g0t", f"st0{u}", f"xt{u}"), writes=("x1t", "g0t", f"s0w{u}"),
            tag=f"cmb{tb}")
        b.gdma(lambda e, tb=tb: e.dma_start(out=x1d[tb * 128:(tb + 1) * 128, :], in_=x1t[:, :]),
               reads=("x1t",), writes=("x1st",), tag=f"x1st{tb}")
        b.s(lambda e: e.activation(out=sqscr[:, :], in_=x1t[:, :], func=AF.Square,
                                   accum_out=stats[:, 0:1]),
            reads=("x1t",), writes=("lnst", "g0t"), tag=f"l1sq{tb}")
        def lnv(e):
            e.reduce_sum(stats[:, 1:2], x1t[:, :], axis=AX)
            e.tensor_scalar(stats[:, 2:3], stats[:, 1:2], -1.0 / D, None, ALU.mult)
            e.tensor_tensor(stats[:, 3:4], stats[:, 2:3], stats[:, 2:3], ALU.mult)
            e.tensor_scalar(stats[:, 4:5], stats[:, 0:1], 1.0 / D, None, ALU.mult)
            e.tensor_tensor(stats[:, 5:6], stats[:, 4:5], stats[:, 3:4], ALU.subtract)
            return e.tensor_scalar(stats[:, 6:7], stats[:, 5:6], 1.0, 1e-5, ALU.mult, ALU.add)
        b.v(lnv, reads=("x1t", "lnst"), writes=("lnst",), tag=f"l1v{tb}")
        def lns(e):
            e.activation(out=stats[:, 7:8], in_=stats[:, 6:7], func=AF.Ln)
            return e.activation(out=stats[:, 7:8], in_=stats[:, 7:8], func=AF.Exp, scale=-0.5)
        b.s(lns, reads=("lnst",), writes=("lnst",), tag=f"l1r{tb}")
        b.v(lambda e: e.tensor_scalar(xnt[:], x1t[:, :], stats[:, 2:3], stats[:, 7:8],
                                      ALU.add, ALU.mult),
            reads=("x1t", "lnst"), writes=("xn1o", f"xtw{u}"), tag=f"l1a{tb}")
        b.gdma(lambda e, tb=tb: e.dma_start(out=xn1_tok[tb * 128:(tb + 1) * 128, :], in_=xnt[:]),
               reads=("xn1o",), writes=("xn1st",), tag=f"xnst{tb}")

    es.close()


def prep_l3(inputs, core):
    x = inputs["L0_attn_hidden_states"]
    x0 = np.ascontiguousarray(x[2 * core:2 * core + 2].reshape(T, D))
    return {
        "x0": x0,
        "Wo_at": inputs["L0_attn_Wo"],
        "WgT": np.ascontiguousarray(inputs["L0_ffn_Wg"].T),
        "W1T": np.ascontiguousarray(inputs["L0_ffn_W1"].transpose(0, 2, 1)),
        "b1l": np.ascontiguousarray(inputs["L0_ffn_b1"].reshape(8, 32, 128).transpose(2, 0, 1).reshape(128, 256)),
        "WoTm": np.ascontiguousarray(inputs["L0_ffn_Wo"].T),
        "bo_bc": np.ascontiguousarray(np.broadcast_to(inputs["L0_ffn_bo"][None, :], (128, D))).astype(BF),
    }




def emit_l4(nc, b, dr):
    _PFX[0] = "d_"
    xn1_tok = dr("xn1_tok", [T, D], BF16, "ExternalInput")
    wdr = {n: dr(n, shp, F32, "ExternalInput") for n, shp in [
        ("g_wq", [D, D]), ("g_kvd", [D, 256]), ("g_kup", [256, 256]), ("g_vup", [256, 256]),
        ("g_wo", [D, D]), ("l_wq", [D, D]), ("l_wk", [D, 256]), ("l_wv", [D, 256]),
        ("l_wo", [D, D]), ("Wr", [D, 8]), ("br_bc", [128, 8]),
        ("n1w_bc", [128, D]), ("n2w_bc", [128, D]), ("ropeg", [L, 1024]), ("ropel", [L, 1024])]}
    masksC = dr("masksC", [128, 4 * 512], BF16, "ExternalInput")
    masksL = dr("masksL", [128, 6 * 512], BF16, "ExternalInput")
    su_c = dr("su_c", [128, 128], BF16, "ExternalInput")
    su_f = dr("su_f", [128, 128], F32, "ExternalInput")
    s1d = dr("s1", [T, D], F32, "ExternalOutput")
    rwd = dr("rw", [T, 8], F32, "ExternalOutput")
    rankd = dr("rankloc", [T, 8], F32, "ExternalOutput")
    cntd = dr("cnt", [1, 8], F32, "ExternalOutput")
    t1d = dr("t1d", [T, D], BF16, "Internal")
    t2d = dr("t2d", [T, D], BF16, "Internal")
    xmidd = dr("xmidd", [T, D], F32, "Internal")
    qtd = dr("qtd", [T, D], BF16, "Internal")
    ofmd = dr("ofmd", [D, T], BF16, "Internal")
    ktd = dr("ktd", [T, 256], BF16, "Internal")
    cntscr = dr("cntscr", [32, 8], F32, "Internal")
    offsscr = dr("offsscr", [32, 8], F32, "Internal")

    es = ExitStack()

    tfm = [sbuf(es, nc, f"tfm{d}", [128, T], BF16) for d in range(8)]
    lat = [sbuf(es, nc, f"lat{lc}", [128, T], BF16) for lc in range(2)]
    vsb = sbuf(es, nc, "vsb", [128, NB * 260], BF16)
    wsl = [sbuf(es, nc, f"wsl{d}", [128, 1024], BF16) for d in range(8)]
    wk2 = [sbuf(es, nc, f"wk2{d}", [128, 256], BF16) for d in range(8)]
    wrr = sbuf(es, nc, "wrr", [128, 64], BF16)
    qh = sbuf(es, nc, "qh", [128, L], BF16)
    kh = sbuf(es, nc, "kh", [128, L], BF16)
    qcur = sbuf(es, nc, "qcur", [64, L], BF16)
    kcur = sbuf(es, nc, "kcur", [64, L], BF16)
    ptb = [sbuf(es, nc, f"ptb{i}", [128, 1024], BF16) for i in range(2)]
    msk = sbuf(es, nc, "msk", [128, 10 * 512], BF16)
    subf = sbuf(es, nc, "subf", [128, 128], BF16)
    subF32 = sbuf(es, nc, "subF32", [128, 128], F32)
    onesF = sbuf(es, nc, "onesF", [128, 1], F32)
    onesrF = sbuf(es, nc, "onesrF", [1, 128], F32)
    rope_t = [sbuf(es, nc, f"ropet{i}", [128, 1024], BF16) for i in range(2)]
    ostg = [sbuf(es, nc, f"ostg{i}", [64, 512], BF16) for i in range(2)]
    nw1 = sbuf(es, nc, "nw1", [128, D], BF16)
    nw2 = sbuf(es, nc, "nw2", [128, D], BF16)
    brb = sbuf(es, nc, "brb", [128, 8], F32)
    xt = [sbuf(es, nc, f"xt{i}", [128, D], F32) for i in range(2)]
    qr = [sbuf(es, nc, f"qr{i}", [128, D], F32) for i in range(2)]
    qs = [sbuf(es, nc, f"qs{i}", [128, D], F32) for i in range(2)]
    qn = [sbuf(es, nc, f"qn{i}", [128, D], BF16) for i in range(2)]
    st = [sbuf(es, nc, f"st{i}", [128, 20], F32) for i in range(2)]
    xm2 = [sbuf(es, nc, f"xm2{i}", [128, D], F32) for i in range(2)]
    onesc = sbuf(es, nc, "onesc", [128, 1], BF16)
    onesr = sbuf(es, nc, "onesr", [1, 128], BF16)
    rr = [sbuf(es, nc, f"rr{i}", [1, 512], F32) for i in range(2)]
    rrb = [sbuf(es, nc, f"rrb{i}", [1, 512], BF16) for i in range(2)]
    of32 = [sbuf(es, nc, f"of32{i}", [64, 512], F32) for i in range(2)]
    selF = sbuf(es, nc, "selF", [128, 8], F32)
    selb = sbuf(es, nc, "selb", [128, 8], BF16)
    rwt = sbuf(es, nc, "rwt", [128, 8], F32)
    rk2 = sbuf(es, nc, "rk2", [128, 8], F32)
    cnts = sbuf(es, nc, "cnts", [1, 256], F32)
    c32 = sbuf(es, nc, "c32", [32, 8], F32)
    offs1 = sbuf(es, nc, "offs1", [1, 256], F32)

    esp = ExitStack()
    pj = psum8(esp, nc, sfx="a")
    TFM = tuple(f"tfm{d}" for d in range(8))
    WSL = tuple(f"wsl{d}" for d in range(8))

    for nm, dst in [("n1w_bc", nw1), ("n2w_bc", nw2), ("br_bc", brb)]:
        b.gdma(lambda e, nm=nm, dst=dst: e.dma_start(out=dst[:], in_=wdr[nm][:, :]),
               writes=("consts",), tag=f"c_{nm}")
    b.gdma(lambda e: e.dma_start(out=msk[:, 0:2048], in_=masksC[:, :]), writes=("consts",), tag="cmc")
    b.gdma(lambda e: e.dma_start(out=msk[:, 2048:5120], in_=masksL[:, :]), writes=("consts",), tag="cml")
    b.gdma(lambda e: e.dma_start(out=subf[:], in_=su_c[:, :]), writes=("consts",), tag="csu")
    b.gdma(lambda e: e.dma_start(out=subF32[:], in_=su_f[:, :]), writes=("consts",), tag="csuf")
    b.gdma(lambda e: e.dma_start(
        out=wrr.rearrange("p (c f) -> p c f", f=8)[:, :, :],
        in_=wdr["Wr"].rearrange("(c p) f -> p c f", p=128)[:, :, :]), writes=("wrr",), tag="cwr")
    b.g(lambda e: (e.memset(onesc[:], 1.0), e.memset(onesr[:], 1.0),
                   e.memset(onesF[:], 1.0), e.memset(onesrF[:], 1.0))[-1],
        writes=("consts",), tag="cones")

    def rms_pass(srcd, srck, nw, dstd, dstk, pfx):
        for tb in range(NB):
            u = tb % 2
            XT, ST, QR, QN = f"xt{u}", f"st{u}", f"qr{u}", f"qn{u}"
            b.gdma(lambda e, tb=tb, srcd=srcd, u=u: e.dma_start(
                out=xt[u][:], in_=srcd[tb * 128:(tb + 1) * 128, :]),
                reads=srck, writes=(XT,), tag=f"{pfx}ld{tb}")
            b.s(lambda e, u=u: e.activation(out=qr[u][:], in_=xt[u][:], func=AF.Square,
                                            accum_out=st[u][:, 0:1]),
                reads=(XT,), writes=(ST, QR), tag=f"{pfx}sq{tb}")
            b.v(lambda e, u=u: e.tensor_scalar(st[u][:, 1:2], st[u][:, 0:1], 1.0 / D, 1e-6,
                                               ALU.mult, ALU.add),
                reads=(ST,), writes=(ST,), tag=f"{pfx}v{tb}")
            def rst2(e, u=u):
                e.activation(out=st[u][:, 2:3], in_=st[u][:, 1:2], func=AF.Ln)
                return e.activation(out=st[u][:, 2:3], in_=st[u][:, 2:3], func=AF.Exp, scale=-0.5)
            b.s(rst2, reads=(ST,), writes=(ST,), tag=f"{pfx}r{tb}")
            def app(e, nw=nw, u=u):
                e.tensor_scalar(qr[u][:], xt[u][:], st[u][:, 2:3], None, ALU.mult)
                return e.tensor_tensor(qn[u][:], qr[u][:], nw[:], ALU.mult)
            b.v(app, reads=(XT, ST, "consts", QR), writes=(QN, QR), tag=f"{pfx}a{tb}")
            b.gdma(lambda e, tb=tb, dstd=dstd, u=u: e.dma_start(
                out=dstd[tb * 128:(tb + 1) * 128, :], in_=qn[u][:]),
                reads=(QN,), writes=(dstk + str(tb),), tag=f"{pfx}w{tb}")

    def load_fm(srcd, deps, pfx):
        for d in range(8):
            b.ydma(lambda e, d=d, srcd=srcd: e.dma_start(
                out=tfm[d][:], in_=srcd[:, d * 128:(d + 1) * 128], transpose=True),
                reads=deps, writes=(f"tfm{d}",), tag=f"lf{pfx}{d}")

    def proj_rope(wname, nheads, roped, dstd, dstk, src, srck, nsrc, pfx):
        ncol = nheads * 64
        wbuf = wsl if ncol == 1024 else wk2
        wkey = "wsl" if ncol == 1024 else "wk2"
        for d in range(nsrc):
            b.gdma(lambda e, d=d, wname=wname, ncol=ncol, wbuf=wbuf: e.dma_start(
                out=wbuf[d][:, 0:ncol], in_=wdr[wname][d * 128:(d + 1) * 128, 0:ncol]),
                writes=(f"{wkey}{d}",), tag=f"lw{pfx}{d}")
        for tb in range(NB):
            u = tb % 2
            QR, QS, QN, ST, RP = f"qr{u}", f"qs{u}", f"qn{u}", f"st{u}", f"ropet{u}"
            pos = (tb % 16) * 128
            bk = [pj[2 * u], pj[2 * u + 1]]
            PK = (PS[2 * u], PS[2 * u + 1]) if ncol == 1024 else (PS[2 * u],)

            def pmm(e, tb=tb, ncol=ncol, nsrc=nsrc, src=src, wbuf=wbuf, bk=bk):
                r = None
                for d in range(nsrc):
                    lhs = src[d][:, tb * 128:(tb + 1) * 128]
                    if ncol == 1024:
                        for oc in range(2):
                            r = e.matmul(bk[oc][:], lhs, wbuf[d][:, oc * 512:(oc + 1) * 512],
                                         start=(d == 0), stop=(d == nsrc - 1), skip_group_check=True)
                    else:
                        r = e.matmul(bk[0][:, 0:256], lhs, wbuf[d][:, 0:256],
                                     start=(d == 0), stop=(d == nsrc - 1), skip_group_check=True)
                return r
            b.t(pmm, reads=srck + tuple(f"{wkey}{d}" for d in range(nsrc)), writes=PK,
                tag=f"pm{pfx}{tb}")
            b.gdma(lambda e, pos=pos, roped=roped, u=u: e.dma_start(
                out=rope_t[u][:], in_=wdr[roped][pos:pos + 128, :]),
                reads=(RP,), writes=(RP,), tag=f"rl{pfx}{tb}")

            def rope2(e, ncol=ncol, nheads=nheads, u=u, bk=bk):
                r = None
                rv = rope_t[u].rearrange("p (h i) -> p h i", i=32)
                nparts = 2 if ncol == 1024 else 1
                for oc in range(nparts):
                    hh = 8 if ncol == 1024 else nheads
                    pv = bk[oc][:, 0:hh * 64].rearrange("p (h half i) -> p h half i",
                                                        half=2, i=32)
                    qv = qr[u].rearrange("p (h half i) -> p h half i",
                                         half=2, i=32)[:, oc * 8:oc * 8 + hh, :, :]
                    cv = rv[:, oc * 8:oc * 8 + hh, :]
                    sv = rv[:, 16 + oc * 8:16 + oc * 8 + hh, :]
                    tv = qs[u].rearrange("p (h half i) -> p h half i",
                                         half=2, i=32)[:, 0:hh, :, :]
                    xe = pv[:, :, 0, :]
                    xo = pv[:, :, 1, :]
                    e.tensor_tensor(qv[:, :, 0, :], xe, cv, ALU.mult)
                    e.tensor_tensor(tv[:, :, 0, :], xo, sv, ALU.mult)
                    e.tensor_tensor(qv[:, :, 0, :], qv[:, :, 0, :], tv[:, :, 0, :], ALU.subtract)
                    e.tensor_tensor(qv[:, :, 1, :], xe, sv, ALU.mult)
                    e.tensor_tensor(tv[:, :, 1, :], xo, cv, ALU.mult)
                    r = e.tensor_tensor(qv[:, :, 1, :], qv[:, :, 1, :], tv[:, :, 1, :], ALU.add)
                return r
            b.v(rope2, reads=PK + (RP,), writes=(QR, QS) + PK, tag=f"ro{pfx}{tb}")
            b.s(lambda e, ncol=ncol, u=u: e.activation(out=qs[u][:, 0:ncol], in_=qr[u][:, 0:ncol],
                                                       func=AF.Square),
                reads=(QR,), writes=(QS,), tag=f"qs{pfx}{tb}")

            def qkn2(e, nheads=nheads, u=u):
                e.reduce_sum(st[u][:, 4:4 + nheads],
                             qs[u].rearrange("p (h f) -> p h f", f=64)[:, 0:nheads, :], axis=AX)
                return e.tensor_scalar(st[u][:, 4:4 + nheads], st[u][:, 4:4 + nheads], 1.0 / 64, 1e-6,
                                       ALU.mult, ALU.add)
            b.v(qkn2, reads=(QS,), writes=(ST,), tag=f"qn2{pfx}{tb}")

            def qkn3(e, nheads=nheads, u=u):
                e.activation(out=st[u][:, 4:4 + nheads], in_=st[u][:, 4:4 + nheads], func=AF.Ln)
                return e.activation(out=st[u][:, 4:4 + nheads], in_=st[u][:, 4:4 + nheads],
                                    func=AF.Exp, scale=-0.5)
            b.s(qkn3, reads=(ST,), writes=(ST,), tag=f"qn3{pfx}{tb}")

            def qkn4(e, nheads=nheads, u=u):
                r = None
                for hh in range(nheads):
                    r = e.tensor_scalar(qn[u][:, hh * 64:(hh + 1) * 64], qr[u][:, hh * 64:(hh + 1) * 64],
                                        st[u][:, 4 + hh:5 + hh], None, ALU.mult)
                return r
            b.v(qkn4, reads=(QR, ST), writes=(QN,), tag=f"qn4{pfx}{tb}")
            b.gdma(lambda e, tb=tb, dstd=dstd, ncol=ncol, u=u: e.dma_start(
                out=dstd[tb * 128:(tb + 1) * 128, 0:ncol], in_=qn[u][:, 0:ncol]),
                reads=(QN,), writes=(dstk + str(tb),), tag=f"qw{pfx}{tb}")

    def proj_v(wname, src, srck, nsrc, pfx):
        b.v(lambda e: e.memset(vsb[:], 1.0), reads=tuple(f"vsb{t}" for t in range(NB)),
            writes=("vsbones",), tag=f"vones{pfx}")
        for d in range(nsrc):
            b.gdma(lambda e, d=d, wname=wname: e.dma_start(
                out=wk2[d][:], in_=wdr[wname][d * 128:(d + 1) * 128, :]),
                writes=(f"wk2{d}",), tag=f"lwv{pfx}{d}")
        for tb in range(NB):
            def vmm(e, tb=tb, nsrc=nsrc, src=src):
                r = None
                for d in range(nsrc):
                    r = e.matmul(pj[2][:, 0:256], src[d][:, tb * 128:(tb + 1) * 128], wk2[d][:, 0:256],
                                 start=(d == 0), stop=(d == nsrc - 1), skip_group_check=True)
                return r
            b.t(vmm, reads=srck + tuple(f"wk2{d}" for d in range(nsrc)), writes=("ps2",),
                tag=f"vm{pfx}{tb}")
            b.v(lambda e, tb=tb: e.tensor_copy(
                vsb[:, tb * 260:tb * 260 + 260].rearrange("p (kv f) -> p kv f", f=65)[:, :, 0:64],
                pj[2][:, 0:256].rearrange("p (kv f) -> p kv f", f=64)),
                reads=("ps2", "vsbones"), writes=(f"vsb{tb}", "ps2"), tag=f"ve{pfx}{tb}")

    def attention(local, pfx, psA):
        pscore, pvbs, bcbs = psA
        qdeps = tuple(f"qtd{t}" for t in range(NB))
        kdeps = tuple(f"ktd{t}" for t in range(NB))
        for bi in range(2):
            for kvp in range(2):
                b.ydma(lambda e, bi=bi, kvp=kvp: e.dma_start(
                    out=kh[:], in_=ktd[bi * L:(bi + 1) * L, kvp * 128:(kvp + 1) * 128], transpose=True),
                    reads=kdeps, writes=("kh",), tag=f"lkh{pfx}{bi}{kvp}")
                for hqp in range(4 * kvp, 4 * kvp + 4):
                    b.ydma(lambda e, bi=bi, hqp=hqp: e.dma_start(
                        out=qh[:], in_=qtd[bi * L:(bi + 1) * L, hqp * 128:(hqp + 1) * 128],
                        transpose=True),
                        reads=qdeps, writes=("qh",), tag=f"lqh{pfx}{bi}{hqp}")
                    for hq in (2 * hqp, 2 * hqp + 1):
                      kv = hq // 4
                      kr = 64 * (kv % 2)
                      qr0 = 64 * (hq % 2)
                      b.v(lambda e, kr=kr, qr0=qr0: (
                          e.tensor_copy(kcur[:], kh[kr:kr + 64, :]),
                          e.tensor_copy(qcur[:], qh[qr0:qr0 + 64, :]))[-1],
                          reads=("kh", "qh"), writes=("kcur", "qcur"), tag=f"cp{pfx}{bi}{hq}")
                      for qt in range(4):
                        w = qt % 2
                        pvb = pvbs[w]
                        bcb = bcbs[w]
                        PVK, BCK = f"pvk{w}", f"bck{w}"
                        if local:
                            if qt == 0:
                                groups = [([0, 1], 2048 + 1024), ([2, 3], 2048 + 2048)]
                            else:
                                groups = [([4 * qt - 2, 4 * qt - 1], 2048),
                                          ([4 * qt, 4 * qt + 1], 2048 + 1024),
                                          ([4 * qt + 2, 4 * qt + 3], 2048 + 2048)]
                        else:
                            groups = [([4 * g, 4 * g + 1], None) for g in range(qt)]
                            groups += [([4 * g + 2, 4 * g + 3], None) for g in range(qt)]
                            groups.sort()
                            groups += [([4 * qt, 4 * qt + 1], 0),
                                       ([4 * qt + 2, 4 * qt + 3], 1024)]
                        ng = len(groups)

                        def emit_qk(gi, qt=qt):
                            blocks, moff = groups[gi]
                            u = gi % 2
                            SK = f"psc{u}"
                            n = len(blocks)
                            for j, kb in enumerate(blocks):
                                b.t(lambda e, kb=kb, j=j, u=u, qt=qt: e.matmul(
                                    pscore[:, (2 * u + j) * 512:(2 * u + j + 1) * 512],
                                    kcur[:, kb * 128:(kb + 1) * 128],
                                    qcur[:, qt * 512:(qt + 1) * 512],
                                    start=True, stop=True, skip_group_check=True),
                                    reads=("kcur", "qcur"), writes=(SK,),
                                    tag=f"qk{pfx}{bi}{hq}{qt}{kb}")
                            b.s(lambda e, u=u, n=n: e.activation(
                                out=ptb[u][:, 0:n * 512],
                                in_=pscore[:, 2 * u * 512:(2 * u + n) * 512],
                                func=AF.Exp, scale=0.125),
                                reads=(SK,), writes=(f"ptb{u}", SK),
                                tag=f"ex{pfx}{bi}{hq}{qt}{gi}")
                            if moff is not None:
                                b.v(lambda e, moff=moff, u=u, n=n: e.tensor_tensor(
                                    ptb[u][:, 0:n * 512], ptb[u][:, 0:n * 512],
                                    msk[:, moff:moff + n * 512], ALU.mult),
                                    reads=(f"ptb{u}", "consts"), writes=(f"ptb{u}",),
                                    tag=f"mk{pfx}{bi}{hq}{qt}{gi}")

                        def emit_pv(gi, first, last, pvb=pvb, PVK=PVK):
                            blocks, _ = groups[gi]
                            u = gi % 2
                            for j, kb in enumerate(blocks):
                                b.t(lambda e, kb=kb, j=j, u=u, first=first, last=last,
                                    bi=bi, kv=kv:
                                    e.matmul(pvb[0:65, :],
                                             vsb[:, (bi * 16 + kb) * 260 + kv * 65:
                                                 (bi * 16 + kb) * 260 + (kv + 1) * 65],
                                             ptb[u][:, j * 512:(j + 1) * 512],
                                             start=(first and j == 0),
                                             stop=(last and j == len(blocks) - 1),
                                             skip_group_check=True),
                                    reads=(f"ptb{u}", f"vsb{bi * 16 + kb}"), writes=(PVK,),
                                    tag=f"pv{pfx}{bi}{hq}{qt}{gi}{j}")

                        emit_qk(0)
                        for gi in range(ng):
                            if gi + 1 < ng:
                                emit_qk(gi + 1)
                            emit_pv(gi, gi == 0, gi == ng - 1)

                        def nrm1(e, w=w, pvb=pvb):
                            e.activation(out=rr[w][:], in_=pvb[64:65, 0:512], func=AF.Ln)
                            e.activation(out=rr[w][:], in_=rr[w][:], func=AF.Exp, scale=-1.0)
                            return e.copy(rrb[w][:], rr[w][:])
                        b.s(nrm1, reads=(PVK,), writes=(f"rrb{w}", f"rr{w}"), tag=f"n1{pfx}{bi}{hq}{qt}")
                        b.t(lambda e, w=w, bcb=bcb: e.matmul(bcb[0:64, :], onesr[0:1, 0:64], rrb[w][:],
                                                             start=True, stop=True,
                                                             skip_group_check=True),
                            reads=(f"rrb{w}", "consts"), writes=(BCK,), tag=f"n2{pfx}{bi}{hq}{qt}")

                        b.s(lambda e, w=w, pvb=pvb: e.copy(of32[w][:], pvb[0:64, 0:512]),
                            reads=(PVK,), writes=(f"of32{w}", PVK), tag=f"n3a{pfx}{bi}{hq}{qt}")
                        b.v(lambda e, w=w, bcb=bcb: e.tensor_tensor(
                            ostg[w][:], of32[w][:], bcb[0:64, 0:512], ALU.mult),
                            reads=(f"of32{w}", BCK), writes=(f"ostg{w}", BCK),
                            tag=f"n3{pfx}{bi}{hq}{qt}")
                        b.gdma(lambda e, bi=bi, hq=hq, qt=qt, w=w: e.dma_start(
                            out=ofmd[hq * 64:(hq + 1) * 64, bi * L + qt * 512:bi * L + (qt + 1) * 512],
                            in_=ostg[w][:]),
                            reads=(f"ostg{w}",), writes=("ofmd",), tag=f"n4{pfx}{bi}{hq}{qt}")

    def oproj(wname, resd, resk, dstd, dstk, pfx):
        for d in range(8):
            b.gdma(lambda e, d=d, wname=wname: e.dma_start(out=wsl[d][:],
                                                           in_=wdr[wname][d * 128:(d + 1) * 128, :]),
                   writes=(f"wsl{d}",), tag=f"lwo{pfx}{d}")
            b.ydma(lambda e, d=d: e.dma_start(out=tfm[d][:], in_=ofmd[d * 128:(d + 1) * 128, :]),
                   reads=("ofmd",), writes=(f"tfm{d}",), tag=f"lof{pfx}{d}")
        for tb in range(NB):
            u = tb % 2
            XT = f"xt{u}"
            bk = [pj[2 * u], pj[2 * u + 1]]
            PK = (PS[2 * u], PS[2 * u + 1])

            def wmm(e, tb=tb, bk=bk):
                r = None
                for d in range(8):
                    for oc in range(2):
                        r = e.matmul(bk[oc][:], tfm[d][:, tb * 128:(tb + 1) * 128],
                                     wsl[d][:, oc * 512:(oc + 1) * 512],
                                     start=(d == 0), stop=(d == 7), skip_group_check=True)
                return r
            b.t(wmm, reads=TFM + WSL, writes=PK, tag=f"om{pfx}{tb}")
            b.gdma(lambda e, tb=tb, resd=resd, u=u: e.dma_start(
                out=xt[u][:], in_=resd[tb * 128:(tb + 1) * 128, :]),
                reads=resk, writes=(XT,), tag=f"orl{pfx}{tb}")

            def radd(e, u=u, bk=bk):
                e.tensor_tensor(xm2[u][:, 0:512], xt[u][:, 0:512], bk[0][:], ALU.add)
                return e.tensor_tensor(xm2[u][:, 512:1024], xt[u][:, 512:1024], bk[1][:], ALU.add)
            b.v(radd, reads=(XT,) + PK, writes=(f"xm2{u}",) + PK, tag=f"ra{pfx}{tb}")
            b.gdma(lambda e, tb=tb, dstd=dstd, u=u: e.dma_start(
                out=dstd[tb * 128:(tb + 1) * 128, :], in_=xm2[u][:]),
                reads=(f"xm2{u}",), writes=(dstk + str(tb),), tag=f"os{pfx}{tb}")

    rms_pass(xn1_tok, (), nw1, t1d, "t1d", "t1")
    load_fm(t1d, tuple(f"t1d{t}" for t in range(NB)), "t1f")
    for d in range(8):
        b.gdma(lambda e, d=d: e.dma_start(out=wk2[d][:], in_=wdr["g_kvd"][d * 128:(d + 1) * 128, :]),
               writes=(f"wk2{d}",), tag=f"lwlat{d}")
    for lc in range(2):
        def lmm(e, lc=lc):
            r = None
            for d in range(8):
                for tt in range(8):
                    r = e.matmul(pj[tt][:], wk2[d][:, lc * 128:(lc + 1) * 128],
                                 tfm[d][:, tt * 512:(tt + 1) * 512],
                                 start=(d == 0), stop=(d == 7), skip_group_check=True)
            return r
        b.t(lmm, reads=TFM + tuple(f"wk2{d}" for d in range(8)), writes=PS, tag=f"lm{lc}")
        def lev(e, lc=lc):
            r = None
            for tt in range(8):
                r = e.tensor_copy(lat[lc][:, tt * 512:(tt + 1) * 512], pj[tt][:])
            return r
        b.v(lev, reads=PS, writes=(f"lat{lc}",) + PS, tag=f"le{lc}")
    LATK = ("lat0", "lat1")
    proj_rope("g_wq", 16, "ropeg", qtd, "qtd", tfm, TFM, 8, "qg")
    proj_rope("g_kup", 4, "ropeg", ktd, "ktd", lat, LATK, 2, "kg")
    proj_v("g_vup", lat, LATK, 2, "vg")
    esp.close()
    b.barrier("ga0")
    esp = ExitStack()
    psA = (psumt(esp, nc, "pscga", [128, 2048], F32),
           [psumt(esp, nc, f"pvga{w}", [128, 512], F32) for w in range(2)],
           [psumt(esp, nc, f"bcga{w}", [128, 512], F32) for w in range(2)])
    attention(False, "ga", psA)
    esp.close()
    b.barrier("ga1")
    esp = ExitStack()
    pj = psum8(esp, nc, sfx="b")
    oproj("g_wo", xn1_tok, (), xmidd, "xmidd", "g")
    rms_pass(xmidd, tuple(f"xmidd{t}" for t in range(NB)), nw2, t2d, "t2d", "t2")
    load_fm(t2d, tuple(f"t2d{t}" for t in range(NB)), "t2f")
    proj_rope("l_wq", 16, "ropel", qtd, "qtd", tfm, TFM, 8, "ql")
    proj_rope("l_wk", 4, "ropel", ktd, "ktd", tfm, TFM, 8, "kl")
    proj_v("l_wv", tfm, TFM, 8, "vl")
    esp.close()
    b.barrier("la0")
    esp = ExitStack()
    psB = (psumt(esp, nc, "pscla", [128, 2048], F32),
           [psumt(esp, nc, f"pvla{w}", [128, 512], F32) for w in range(2)],
           [psumt(esp, nc, f"bcla{w}", [128, 512], F32) for w in range(2)])
    attention(True, "la", psB)
    esp.close()
    b.barrier("la1")
    esp = ExitStack()
    pj = psum8(esp, nc, sfx="c")
    oproj("l_wo", xmidd, tuple(f"xmidd{t}" for t in range(NB)), s1d, "s1d", "l")

    load_fm(xn1_tok, (), "xnf")
    for tb in range(NB):
        def rmm(e, tb=tb):
            r = None
            for d in range(8):
                r = e.matmul(pj[3][:, 0:8], tfm[d][:, tb * 128:(tb + 1) * 128],
                             wrr[:, d * 8:(d + 1) * 8],
                             start=(d == 0), stop=(d == 7), skip_group_check=True)
            return r
        b.t(rmm, reads=TFM + ("wrr",), writes=("ps3",), tag=f"rt{tb}")
        def rsel(e):
            e.tensor_tensor(st[0][:, 4:12], pj[3][:, 0:8], brb[:], ALU.add)
            e.reduce_max(st[0][:, 0:1], st[0][:, 4:12], axis=AX)
            e.tensor_scalar(st[0][:, 12:20], st[0][:, 4:12], st[0][:, 0:1], None, ALU.is_ge)
            e.tensor_scalar(st[0][:, 12:20], st[0][:, 12:20], -1e30, None, ALU.mult)
            e.tensor_tensor(st[0][:, 12:20], st[0][:, 12:20], st[0][:, 4:12], ALU.add)
            e.reduce_max(st[0][:, 1:2], st[0][:, 12:20], axis=AX)
            e.tensor_scalar(selF[:], st[0][:, 4:12], st[0][:, 1:2], None, ALU.is_ge)
            e.tensor_scalar(st[0][:, 12:20], st[0][:, 4:12], st[0][:, 0:1], None, ALU.subtract)
            return e.tensor_tensor(st[0][:, 2:3], st[0][:, 1:2], st[0][:, 0:1], ALU.subtract)
        b.v(rsel, reads=("ps3", "consts"), writes=("st", "selF", "ps3"), tag=f"rs{tb}")
        def rexp(e):
            e.activation(out=st[0][:, 12:20], in_=st[0][:, 12:20], func=AF.Exp)
            return e.activation(out=st[0][:, 3:4], in_=st[0][:, 2:3], func=AF.Exp)
        b.s(rexp, reads=("st",), writes=("st",), tag=f"re{tb}")
        def rwv(e):
            e.tensor_scalar(st[0][:, 3:4], st[0][:, 3:4], 1.0, 1.0, ALU.mult, ALU.add)
            e.reciprocal(st[0][:, 3:4], st[0][:, 3:4])
            e.tensor_tensor(rwt[:], st[0][:, 12:20], selF[:], ALU.mult)
            return e.tensor_scalar(rwt[:], rwt[:], st[0][:, 3:4], None, ALU.mult)
        b.v(rwv, reads=("st", "selF"), writes=("rwt",), tag=f"rw{tb}")
        b.gdma(lambda e, tb=tb: e.dma_start(out=rwd[tb * 128:(tb + 1) * 128, :], in_=rwt[:]),
               reads=("rwt",), writes=("rwst",), tag=f"rwst{tb}")
        def rkmm(e):
            e.matmul(pj[6][:, 0:8], subF32[:], selF[:], start=True, stop=True, skip_group_check=True)
            return e.matmul(pj[7][0:1, 0:8], onesF[:], selF[:], start=True, stop=True,
                            skip_group_check=True)
        b.t(rkmm, reads=("selF", "consts"), writes=("ps6", "ps7"), tag=f"rk{tb}")
        def rke(e, tb=tb):
            e.tensor_copy(rk2[:], pj[6][:, 0:8])
            return e.tensor_copy(cnts[:, tb * 8:(tb + 1) * 8], pj[7][0:1, 0:8])
        b.v(rke, reads=("ps6", "ps7"), writes=("rk2", "cnts", "ps6", "ps7"), tag=f"rke{tb}")
        b.gdma(lambda e, tb=tb: e.dma_start(out=rankd[tb * 128:(tb + 1) * 128, :], in_=rk2[:]),
               reads=("rk2",), writes=(f"rkst{tb}",), tag=f"rkst{tb}")
    b.gdma(lambda e: e.dma_start(out=cntscr.rearrange("(o tb) f -> o (tb f)", o=1)[:, :], in_=cnts[:]),
           reads=("cnts",), writes=("cntd",), tag="cst")
    b.ydma(lambda e: e.dma_start(out=c32[:], in_=cntscr[:, :]), reads=("cntd",), writes=("c32",), tag="cld")
    def offmm(e):
        e.matmul(pj[6][0:32, 0:8], subF32[0:32, 0:32], c32[:], start=True, stop=True,
                 skip_group_check=True)
        return e.matmul(pj[7][0:1, 0:8], onesF[0:32, 0:1], c32[:], start=True, stop=True,
                        skip_group_check=True)
    b.t(offmm, reads=("c32", "consts"), writes=("ps6", "ps7"), tag="offmm")
    def offe(e):
        e.tensor_copy(st[0][0:32, 4:12], pj[6][0:32, 0:8])
        return e.tensor_copy(st[0][0:1, 12:20], pj[7][0:1, 0:8])
    b.v(offe, reads=("ps6", "ps7"), writes=("st", "ps6", "ps7"), tag="offe")
    b.gdma(lambda e: e.dma_start(out=offsscr[:, :], in_=st[0][0:32, 4:12]),
           reads=("st",), writes=("offd",), tag="ofst")
    b.gdma(lambda e: e.dma_start(out=cntd[0:1, :], in_=st[0][0:1, 12:20]),
           reads=("st",), writes=("cntout",), tag="cntst")
    b.ydma(lambda e: e.dma_start(out=offs1[:], in_=offsscr.rearrange("(o a) b -> o (a b)", o=1)[:, :]),
           reads=("offd",), writes=("offs1",), tag="ofld")
    for tb in range(NB):
        b.ydma(lambda e, tb=tb: e.dma_start(out=selF[:], in_=rankd[tb * 128:(tb + 1) * 128, :]),
               reads=(f"rkst{tb}", "selF"), writes=("selF",), tag=f"rrl{tb}")
        b.t(lambda e, tb=tb: e.matmul(pj[6][:, 0:8], onesrF[:], offs1[:, tb * 8:(tb + 1) * 8],
                                      start=True, stop=True, skip_group_check=True),
            reads=("offs1", "consts"), writes=("ps6",), tag=f"rb{tb}")
        b.v(lambda e: e.tensor_tensor(rk2[:], selF[:], pj[6][:, 0:8], ALU.add),
            reads=("selF", "ps6"), writes=("rk2", "ps6"), tag=f"ra2{tb}")
        b.gdma(lambda e, tb=tb: e.dma_start(out=rankd[tb * 128:(tb + 1) * 128, :], in_=rk2[:]),
               reads=("rk2",), writes=(f"rk2st{tb}",), tag=f"rk2w{tb}")

    esp.close()
    es.close()


def prep_l4(inputs):
    def rope_tab(base):
        inv = 1.0 / base ** (np.arange(0, 64, 2, dtype=np.float64) / 64)
        fr = np.arange(L, dtype=np.float64)[:, None] * inv[None, :]
        cos = np.cos(fr).astype(np.float32)
        sin = np.sin(fr).astype(np.float32)
        return np.concatenate([np.tile(cos, (1, 16)), np.tile(sin, (1, 16))], axis=1)
    i128 = np.arange(128)
    i512 = np.arange(512)
    mC = np.zeros((128, 4 * 512), np.float32)
    for ix, delta in enumerate([0, 128, 256, 384]):
        mC[:, ix * 512:(ix + 1) * 512] = ((i128[:, None] + delta) <= i512[None, :])
    mL = np.zeros((128, 6 * 512), np.float32)
    for ix, delta in enumerate([-256, -128, 0, 128, 256, 384]):
        d = i512[None, :] - (i128[:, None] + delta)
        mL[:, ix * 512:(ix + 1) * 512] = (d >= 0) & (d < 256)
    su = (i128[:, None] < i128[None, :]).astype(np.float32)
    perm64 = np.concatenate([np.arange(0, 64, 2), np.arange(1, 64, 2)])

    def phead(W):
        H = W.shape[1] // 64
        idx = np.concatenate([h * 64 + perm64 for h in range(H)])
        return np.ascontiguousarray(W[:, idx])
    return {
        "g_wq": phead(inputs["L1_attn_g_wq"]), "g_kvd": inputs["L1_attn_g_kv_down"],
        "g_kup": phead(inputs["L1_attn_g_k_up"]), "g_vup": inputs["L1_attn_g_v_up"],
        "g_wo": inputs["L1_attn_g_wo"],
        "l_wq": phead(inputs["L1_attn_l_wq"]), "l_wk": phead(inputs["L1_attn_l_wk"]),
        "l_wv": inputs["L1_attn_l_wv"], "l_wo": inputs["L1_attn_l_wo"],
        "Wr": inputs["L1_ffn_Wr"],
        "br_bc": np.ascontiguousarray(np.broadcast_to(inputs["L1_ffn_br"][None, :], (128, 8))).astype(np.float32),
        "n1w_bc": np.ascontiguousarray(np.broadcast_to(inputs["L1_attn_norm1_w"][None, :], (128, D))).astype(np.float32),
        "n2w_bc": np.ascontiguousarray(np.broadcast_to(inputs["L1_attn_norm2_w"][None, :], (128, D))).astype(np.float32),
        "ropeg": rope_tab(1e6), "ropel": rope_tab(1e4),
        "masksC": mC.astype(BF), "masksL": mL.astype(BF),
        "su_c": su.astype(BF),
        "su_f": su,
    }




def emit_l5(nc, b, dr):
    _PFX[0] = "e_"
    xn1_tok = dr("xn1_tok", [T, D], BF16, "ExternalInput")
    x1d = dr("x1", [T, D], F32, "ExternalInput")
    s1d = dr("s1", [T, D], F32, "ExternalInput")
    rwd = dr("rw", [T, 8], F32, "ExternalInput")
    rankd = dr("rankloc", [T, 8], F32, "ExternalInput")
    cnt_all = dr("cnt_all", [8, 8], F32, "ExternalInput")
    pref = dr("pref", [8, 1], F32, "ExternalInput")
    W1 = dr("W1", [8, D, 4096], F32, "ExternalInput")
    b1l = dr("b1l5", [128, 256], F32, "ExternalInput")
    W2 = dr("W2", [8, 4096, D], F32, "ExternalInput")
    b2in = dr("b2in", [8, D], F32, "ExternalInput")
    ident_b = dr("ident_b", [128, 128], BF16, "ExternalInput")
    x2d = dr("x2", [T, D], F32, "ExternalOutput")

    es = ExitStack()
    TS = 256
    NTS = T // TS

    w1s = [sbuf(es, nc, f"w1s{d}", [128, 4096], BF16) for d in range(8)]
    w2s = [sbuf(es, nc, f"w2s{j}", [128, 4096], BF16) for j in range(8)]
    acc = sbuf(es, nc, "acc", [128, NB * 1024], BF16)
    ghc = [sbuf(es, nc, f"ghc{i}", [128, TS], BF16) for i in range(2)]
    xnts = [[sbuf(es, nc, f"xnts{p}{d}", [128, TS], BF16) for d in range(8)] for p in range(2)]
    b1sb = sbuf(es, nc, "b1sb", [128, 256], F32)
    b2sb = sbuf(es, nc, "b2sb", [8, D], BF16)
    kwsb = sbuf(es, nc, "kwsb", [128, NB * 8], F32)
    kwb = sbuf(es, nc, "kwb", [128, 8], BF16)
    kwT = sbuf(es, nc, "kwT", [8, 256], BF16)
    ib = sbuf(es, nc, "ib", [128, 128], BF16)
    ctile = sbuf(es, nc, "ctile", [8, 8], F32)
    prefsb = sbuf(es, nc, "prefsb", [8, 1], F32)
    offr = sbuf(es, nc, "offr", [1, 8], F32)
    onesrF = sbuf(es, nc, "onesrF", [1, 128], F32)
    cobc = sbuf(es, nc, "cobc", [128, 8], F32)
    rkt = sbuf(es, nc, "rkt", [128, 8], F32)
    rwt = sbuf(es, nc, "rwt", [128, 8], F32)
    pj = [psumt(es, nc, f"ps{i}", [128, 512], F32) for i in range(7)]
    pjb = psumt(es, nc, "pjb", [128, 128], BF16)

    b.ydma(lambda e: e.dma_start(out=ctile[:], in_=cnt_all[:, :]), reads=("cntall",),
           writes=("ctile",), tag="lca")
    b.ydma(lambda e: e.dma_start(out=prefsb[:], in_=pref[:, :]), writes=("pref",), tag="lpr")
    b.ydma(lambda e: e.dma_start(out=ib[:], in_=ident_b[:, :]), writes=("ib",), tag="lib")
    b.gdma(lambda e: e.dma_start(out=b1sb[:], in_=b1l[:, :]), writes=("b1sb",), tag="lb1")
    b.gdma(lambda e: e.dma_start(out=b2sb[:], in_=b2in[:, :]), writes=("b2sb",), tag="lb2")
    b.g(lambda e: e.memset(onesrF[:], 1.0), writes=("onesrF",), tag="ones")
    b.t(lambda e: e.matmul(pj[0][0:1, 0:8], prefsb[:], ctile[:], start=True, stop=True,
                           skip_group_check=True),
        reads=("ctile", "pref"), writes=("ps0",), tag="coremm")
    b.v(lambda e: e.tensor_copy(offr[:], pj[0][0:1, 0:8]), reads=("ps0",), writes=("offr", "ps0"),
        tag="coree")
    b.t(lambda e: e.matmul(pj[1][:, 0:8], onesrF[:], offr[:], start=True, stop=True,
                           skip_group_check=True),
        reads=("offr", "onesrF"), writes=("ps1",), tag="corebc")
    b.v(lambda e: e.tensor_copy(cobc[:], pj[1][:, 0:8]), reads=("ps1",), writes=("cobc", "ps1"),
        tag="corebe")

    b.v(lambda e: e.memset(acc[:], 0.0), writes=tuple(f"acc{t}" for t in range(NB)), tag="zacc")
    for tb in range(NB):
        b.ydma(lambda e, tb=tb: e.dma_start(out=rkt[:], in_=rankd[tb * 128:(tb + 1) * 128, :]),
               reads=("rkt",), writes=("rkt",), tag=f"lrk{tb}")
        b.ydma(lambda e, tb=tb: e.dma_start(out=rwt[:], in_=rwd[tb * 128:(tb + 1) * 128, :]),
               reads=("rwt",), writes=("rwt",), tag=f"lrw{tb}")
        def kw(e, tb=tb):
            e.tensor_tensor(rkt[:], rkt[:], cobc[:], ALU.add)
            e.tensor_scalar(rkt[:], rkt[:], 8192.0, None, ALU.is_lt)
            return e.tensor_tensor(kwsb[:, tb * 8:(tb + 1) * 8], rkt[:], rwt[:], ALU.mult)
        b.v(kw, reads=("rkt", "rwt", "cobc"), writes=(f"kw{tb}", "rkt"), tag=f"kw{tb}")

    for e_i in range(8):
        for d in range(8):
            b.gdma(lambda e, e_i=e_i, d=d: e.dma_start(out=w1s[d][:],
                                                       in_=W1[e_i, d * 128:(d + 1) * 128, :]),
                   writes=(f"w1s{d}",), tag=f"lw1{e_i}{d}")
        for j in range(8):
            def lw2(e, e_i=e_i, j=j):
                return e.dma_start(
                    out=w2s[j].rearrange("p (h f) -> p h f", f=1024)[:, :, :],
                    in_=W2[e_i].rearrange("(h p) f -> p h f", p=128)[:, 4 * j:4 * j + 4, :])
            b.gdma(lw2, writes=(f"w2s{j}",), tag=f"lw2{e_i}{j}")
        W1K = tuple(f"w1s{d}" for d in range(8))
        W2K = tuple(f"w2s{j}" for j in range(8))
        for ts in range(NTS):
            tb0 = ts * 2
            p = ts % 2
            for d in range(8):
                b.ydma(lambda e, ts=ts, d=d, p=p: e.dma_start(
                    out=xnts[p][d][:], in_=xn1_tok[ts * TS:(ts + 1) * TS, d * 128:(d + 1) * 128],
                    transpose=True),
                    writes=(f"xnts{p}{d}",), tag=f"cxn{e_i}{ts}{d}")
            XN = tuple(f"xnts{p}{d}" for d in range(8))

            def mk_hmm(hc, p=p):
                u = hc % 2
                def hmm(e, hc=hc, p=p):
                    r = None
                    for d in range(8):
                        r = e.matmul(pj[5 + (hc % 2)][:, 0:TS], w1s[d][:, hc * 128:(hc + 1) * 128],
                                     xnts[p][d][:], start=(d == 0), stop=(d == 7),
                                     skip_group_check=True)
                    return r
                b.t(hmm, reads=W1K + XN, writes=(PS[5 + u],), tag=f"hm{e_i}{ts}{hc}")
                b.s(lambda e, e_i=e_i, hc=hc, u=u: e.activation(
                    out=ghc[u][:], in_=pj[5 + u][:, 0:TS], func=AF.Gelu,
                    bias=b1sb[:, e_i * 32 + hc:e_i * 32 + hc + 1]),
                    reads=(PS[5 + u], "b1sb"), writes=(f"ghc{u}", PS[5 + u]), tag=f"ge{e_i}{ts}{hc}")

            def mk_omm(hc, e_i=e_i):
                u = hc % 2
                def omm(e, hc=hc, u=u, e_i=e_i):
                    r = None
                    for tq in range(2):
                        for oc in range(2):
                            r = e.matmul(pj[tq * 2 + oc][:],
                                         ghc[u][:, tq * 128:(tq + 1) * 128],
                                         w2s[hc // 4][:, (hc % 4) * 1024 + oc * 512:
                                                      (hc % 4) * 1024 + (oc + 1) * 512],
                                         start=(hc == 0), stop=(hc == 31 and e_i != 7),
                                         skip_group_check=True)
                    return r
                b.t(omm, reads=(f"ghc{u}", f"w2s{hc // 4}"), writes=tuple(PS[i] for i in range(4)),
                    tag=f"om{e_i}{ts}{hc}")

            mk_hmm(0)
            for hc in range(32):
                if hc + 1 < 32:
                    mk_hmm(hc + 1)
                mk_omm(hc)
            def bmm(e, ts=ts, e_i=e_i):
                r = None
                for tq in range(2):
                    for oc in range(2):
                        r = e.matmul(pj[tq * 2 + oc][:],
                                     kwT[:, tq * 128:(tq + 1) * 128],
                                     b2sb[:, oc * 512:(oc + 1) * 512],
                                     start=False, stop=True, skip_group_check=True)
                return r
            if e_i == 7:
                for tq in range(2):
                    b.v(lambda e, tb=tb0 + tq: e.tensor_copy(kwb[:], kwsb[:, tb * 8:(tb + 1) * 8]),
                        reads=(f"kw{tb0 + tq}", "kwb"), writes=("kwb",), tag=f"kwc{ts}{tq}")
                    b.t(lambda e: e.transpose(pjb[0:8, :], kwb[:, 0:8], ib[:]),
                        reads=("kwb", "ib"), writes=("pjb",), tag=f"kwt{ts}{tq}")
                    b.v(lambda e, tq=tq: e.tensor_copy(kwT[:, tq * 128:(tq + 1) * 128], pjb[0:8, :]),
                        reads=("pjb",), writes=("kwT", "pjb"), tag=f"kwte{ts}{tq}")
                b.t(bmm, reads=("b2sb", "kwT"),
                    writes=tuple(PS[i] for i in range(4)), tag=f"bm{ts}")
            for tq in range(2):
                tb = tb0 + tq
                for oc in range(2):
                    def oev(e, e_i=e_i, tb=tb, tq=tq, oc=oc):
                        sl = slice(tb * 1024 + oc * 512, tb * 1024 + (oc + 1) * 512)
                        return e.scalar_tensor_tensor(acc[:, sl], pj[tq * 2 + oc][:],
                                                      kwsb[:, tb * 8 + e_i:tb * 8 + e_i + 1],
                                                      acc[:, sl], ALU.mult, ALU.add)
                    b.v(oev, reads=(PS[tq * 2 + oc], f"kw{tb}", f"acc{tb}"),
                        writes=(f"acc{tb}", PS[tq * 2 + oc]), tag=f"oe{e_i}{ts}{tq}{oc}")

    _f0 = w1s[0].bitcast(F32)
    _f1 = w1s[1].bitcast(F32)
    g1t = _f0[:, 0:1024]
    x1t = _f0[:, 1024:2048]
    s1t = _f1[:, 0:1024]
    x2t = _f1[:, 1024:2048]
    b.t(lambda e: e.nop(),
        reads=tuple(f"acc{t}" for t in range(NB)) + tuple(f"w1s{d}" for d in range(2)),
        writes=("phC",), tag="phC")
    for tb in range(NB):
        b.s(lambda e, tb=tb: e.activation(out=g1t[:, :], in_=acc[:, tb * 1024:(tb + 1) * 1024],
                                          func=AF.Silu),
            reads=(f"acc{tb}", "phC"), writes=("g1t",), tag=f"sl{tb}")
        b.ydma(lambda e, tb=tb: e.dma_start(out=x1t[:, :], in_=x1d[tb * 128:(tb + 1) * 128, :]),
               reads=("phC", "x1t"), writes=("x1t",), tag=f"lx1{tb}")
        b.ydma(lambda e, tb=tb: e.dma_start(out=s1t[:, :], in_=s1d[tb * 128:(tb + 1) * 128, :]),
               reads=("phC", "s1t"), writes=("s1t",), tag=f"ls1{tb}")
        def fin(e):
            e.tensor_tensor(x2t[:, :], s1t[:, :], g1t[:, :], ALU.mult)
            return e.tensor_tensor(x2t[:, :], x2t[:, :], x1t[:, :], ALU.add)
        b.v(fin, reads=("g1t", "x1t", "s1t"), writes=("x2t", "g1t", "s1t", "x1t"), tag=f"fn{tb}")
        b.gdma(lambda e, tb=tb: e.dma_start(out=x2d[tb * 128:(tb + 1) * 128, :], in_=x2t[:, :]),
               reads=("x2t",), writes=("x2st",), tag=f"sx2{tb}")

    es.close()


def prep_l5(inputs, core):
    return {
        "W1": inputs["L1_ffn_W1"],
        "b1l5": np.ascontiguousarray(inputs["L1_ffn_b1"].reshape(8, 32, 128).transpose(2, 0, 1).reshape(128, 256)),
        "W2": inputs["L1_ffn_W2"],
        "b2in": inputs["L1_ffn_b2"],
        "ident_b": np.eye(128, dtype=np.float32).astype(BF),
        "pref": (np.arange(8) < core).astype(np.float32).reshape(8, 1),
    }



def _mk_dr(nc, internal_names=()):
    cache = {}

    def dr(name, shape, dtype, kind):
        if name not in cache:
            k = "Internal" if name in internal_names else kind
            cache[name] = nc.dram_tensor(name, shape, dtype, kind=k).ap()
        return cache[name]
    return dr


def build_l1():
    nc = bass.Bass("TRN2", target_bir_lowering=False, debug=False)
    b = Builder()
    emit_l1(nc, b, _mk_dr(nc))
    b.build(nc)
    return nc


def build_l2(bhs=None):
    nc = bass.Bass("TRN2", target_bir_lowering=False, debug=False)
    b = Builder()
    emit_l2(nc, b, _mk_dr(nc), bhs)
    b.build(nc)
    return nc


def build_l3():
    nc = bass.Bass("TRN2", target_bir_lowering=False, debug=False)
    b = Builder()
    emit_l3(nc, b, _mk_dr(nc))
    b.build(nc)
    return nc


def build_l4():
    nc = bass.Bass("TRN2", target_bir_lowering=False, debug=False)
    b = Builder()
    emit_l4(nc, b, _mk_dr(nc))
    b.build(nc)
    return nc


def build_l5():
    nc = bass.Bass("TRN2", target_bir_lowering=False, debug=False)
    b = Builder()
    emit_l5(nc, b, _mk_dr(nc))
    b.build(nc)
    return nc


def emit_cc(nc, b, dr):
    cc_in = dr("cnt", [1, 8], F32, "Internal")
    cc_out = dr("cnt_all", [8, 8], F32, "Internal")

    def coll(e):
        return e.collective_compute(
            "AllGather", ALU.bypass, [[0, 1, 2, 3, 4, 5, 6, 7]],
            ins=[cc_in.opt()], outs=[cc_out.opt()])
    b.ccg(coll, reads=("cntout",), writes=("cntall",), tag="allgather")


_FUSE_INTERNAL_A = frozenset(["xn0_tok", "qf", "kf", "vf", "gates", "o_n"])


def build_fusedA():
    nc = bass.Bass("TRN2", target_bir_lowering=False, debug=False)
    b = Builder()
    dr = _mk_dr(nc, _FUSE_INTERNAL_A)
    emit_l1(nc, b, dr)
    b.barrier("s1")
    emit_l2(nc, b, dr)
    b.barrier("s2")
    emit_l3(nc, b, dr)
    b.barrier("s3")
    emit_l4(nc, b, dr)
    b.build(nc)
    return nc


_FUSE_INTERNAL_ALL = _FUSE_INTERNAL_A | frozenset(
    ["x1", "xn1_tok", "s1", "rw", "rankloc", "cnt", "cnt_all"])


def build_fused_all():
    nc = bass.Bass("TRN2", target_bir_lowering=False, debug=False)
    b = Builder()
    dr = _mk_dr(nc, _FUSE_INTERNAL_ALL)
    emit_l1(nc, b, dr)
    b.barrier("s1")
    emit_l2(nc, b, dr)
    b.barrier("s2")
    emit_l3(nc, b, dr)
    b.barrier("s3")
    emit_l4(nc, b, dr)
    b.barrier("s4")
    emit_cc(nc, b, dr)
    emit_l5(nc, b, dr)
    b.build(nc)
    return nc


def prep_l1(inputs, core):
    x = inputs["L0_attn_hidden_states"]
    x0 = np.ascontiguousarray(x[2 * core:2 * core + 2].reshape(T, D))
    cvw = np.concatenate([
        inputs[f"L0_attn_conv_{c}"].reshape(8, 128, 4).transpose(1, 0, 2).reshape(128, 32)
        for c in "qkv"], axis=1)
    wbgh = np.concatenate([inputs["L0_attn_Wb"], inputs["L0_attn_Wg"], inputs["L0_attn_Wh"]], axis=1)
    bgh = np.ascontiguousarray(np.broadcast_to(
        np.concatenate([inputs["L0_attn_bg"], inputs["L0_attn_bh"]])[None, :], (128, 8)))
    return {
        "x0": x0,
        "Wq": inputs["L0_attn_Wq"], "Wk": inputs["L0_attn_Wk"], "Wv": inputs["L0_attn_Wv"],
        "convw": np.ascontiguousarray(cvw),
        "Wbgh": np.ascontiguousarray(wbgh),
        "bgh_bc": bgh,
    }


_NC_CACHE = {}


def _get_nc(name, fn):
    if name not in _NC_CACHE:
        _NC_CACHE[name] = fn()
    return _NC_CACHE[name]


def kernel(**inputs):
    cores = list(range(NCORES))
    ncF = _get_nc("fALL", build_fused_all)
    shared = {}
    shared.update(prep_l1(inputs, 0))
    shared.update(prep_l2_consts())
    shared["nw_bc"] = np.ascontiguousarray(np.broadcast_to(
        inputs["L0_attn_norm_w"][None, :], (128, 256))).astype(np.float32)
    shared.update(prep_l3(inputs, 0))
    shared.update(prep_l4(inputs))
    shared.update(prep_l5(inputs, 0))
    x = inputs["L0_attn_hidden_states"]
    im = []
    for c in cores:
        m = dict(shared)
        m["x0"] = np.ascontiguousarray(x[2 * c:2 * c + 2].reshape(T, D))
        m["pref"] = (np.arange(8) < c).astype(np.float32).reshape(8, 1)
        im.append(m)
    r = run_bass_kernel_spmd(ncF, im, core_ids=cores).results

    out = np.empty((16, 2048, 1024), np.float32)
    for c in cores:
        out[2 * c:2 * c + 2] = np.asarray(r[c]["x2"]).reshape(2, L, D)
    return out
